# version 11
# baseline (speedup 1.0000x reference)
import numpy as np
import ml_dtypes

import concourse.bass as bass
import concourse.mybir as mybir
from concourse.bass_utils import run_bass_kernel_spmd

F32 = mybir.dt.float32
BF16 = mybir.dt.bfloat16
AF = mybir.ActivationFunctionType
ALU = mybir.AluOpType
AX = mybir.AxisListType

NCORES = 8
DIM = 1024
SEQ = 2048
NHC = 8
NTB = 2
TB = 1024
E = 16
EG = 4
NG = 4
I_EXP = 512
LNEPS = 1e-5

OFF_ONES = 0
OFF_ID = 128
OFF_UNS = 256
OFF_SELB = 384
OFF_STK0 = OFF_SELB + 2048
OFF_A = OFF_STK0 + 1024
OFF_AT = OFF_A + 16
CBW = OFF_AT + 16

CF_BCB = 0
CF_BIN = 16
CF_BGATE = 24
CF_BOUT = 32
CF_EPS = 40
CF_LNC = 41
CFW = 48

STAGES = "full"
DBG_ROUT = False


class Prog:
    ENGS = ("t", "v", "s", "g", "y")

    def __init__(self, nc):
        self.nc = nc
        self.streams = {e: [] for e in self.ENGS}
        self.cnt = {"T": 0, "V": 0, "S": 0, "DW": 0, "DS": 0}

    def _op(self, eng, fn, waits, inc, n):
        for s, v in waits:
            assert v <= self.cnt[s], f"wait {s}>={v} emitted={self.cnt[s]}"
        if inc is not None:
            self.cnt[inc] += n
        self.streams[eng].append((tuple(waits), fn, inc, n))
        return self.cnt[inc] if inc is not None else None

    def t(self, fn, waits=()):
        return self._op("t", fn, waits, "T", 1)

    def v(self, fn, waits=()):
        return self._op("v", fn, waits, "V", 1)

    def vw(self, fn, waits=()):
        w = list(waits) + [("V", self.cnt["V"])]
        return self._op("v", fn, w, "V", 1)

    def s(self, fn, waits=()):
        return self._op("s", fn, waits, "S", 1)

    def g(self, fn, waits=()):
        return self._op("g", fn, waits, "DW", 16)

    def y(self, fn, waits=()):
        return self._op("y", fn, waits, "DS", 16)

    def ywait(self, waits):
        self.streams["y"].append((tuple(waits), None, None, 0))

    def emit(self, block, sems):
        def mk(eng_key, deco):
            stream = self.streams[eng_key]

            def body(eng):
                seen = {k: 0 for k in self.cnt}
                for waits, fn, inc, n in stream:
                    for s, v in waits:
                        if v > seen[s]:
                            eng.wait_ge(sems[s], v)
                            seen[s] = v
                    if fn is None:
                        continue
                    inst = fn(eng)
                    if inc is not None:
                        inst.then_inc(sems[inc], n)
            deco(body)

        mk("t", block.tensor)
        mk("v", block.vector)
        mk("s", block.scalar)
        mk("g", block.gpsimd)
        mk("y", block.sync)


def build_program():
    nc = bass.Bass("TRN2", target_bir_lowering=False)

    d_x = nc.dram_tensor("xt", [DIM, SEQ], F32, kind="ExternalInput").ap()
    d_w1 = nc.dram_tensor("w1t", [E, 128, 8 * I_EXP], F32, kind="ExternalInput").ap()
    d_w2g = nc.dram_tensor("w2g", [NG, 128, 16 * DIM], F32, kind="ExternalInput").ap()
    d_ws1 = nc.dram_tensor("ws1t", [128, 8 * DIM], F32, kind="ExternalInput").ap()
    d_ws2 = nc.dram_tensor("ws2t", [128, 8 * DIM], F32, kind="ExternalInput").ap()
    d_win = nc.dram_tensor("wint", [128, 8 * DIM], F32, kind="ExternalInput").ap()
    d_wgate = nc.dram_tensor("wgatet", [128, 8 * DIM], F32, kind="ExternalInput").ap()
    d_wout = nc.dram_tensor("woutt", [128, 8 * DIM], F32, kind="ExternalInput").ap()
    d_wg = nc.dram_tensor("wgt", [128, 8 * 16], F32, kind="ExternalInput").ap()
    d_bm = nc.dram_tensor("bmt", [128, 8 * 16], F32, kind="ExternalInput").ap()
    d_cm = nc.dram_tensor("cmt", [16, DIM], F32, kind="ExternalInput").ap()
    d_cb = nc.dram_tensor("cb", [128, CBW], F32, kind="ExternalInput").ap()
    d_cf = nc.dram_tensor("cf", [128, CFW], F32, kind="ExternalInput").ap()
    d_out = nc.dram_tensor("yt", [DIM, SEQ], F32, kind="ExternalOutput").ap()

    from contextlib import ExitStack
    ctx = ExitStack()
    Eb = ctx.enter_context

    X = Eb(nc.sbuf_tensor("X", [128, NHC, SEQ], F32))
    XLN = Eb(nc.sbuf_tensor("XLN", [128, NHC, TB], BF16))
    HID = Eb(nc.sbuf_tensor("HID", [128, 16, TB], BF16))
    W1B = Eb(nc.sbuf_tensor("W1B", [128, 2, 8, I_EXP], BF16))
    W2GB = Eb(nc.sbuf_tensor("W2GB", [128, 16, DIM], BF16))
    WS1B = Eb(nc.sbuf_tensor("WS1B", [128, 8, DIM], BF16))
    CB = Eb(nc.sbuf_tensor("CB", [128, CBW], BF16))
    CF = Eb(nc.sbuf_tensor("CF", [128, CFW], F32))
    WGT = Eb(nc.sbuf_tensor("WGT", [128, 8, 16], BF16))
    BMT = Eb(nc.sbuf_tensor("BMT", [128, 8, 16], BF16))
    CMT = Eb(nc.sbuf_tensor("CMT", [16, DIM], BF16))
    MB = Eb(nc.sbuf_tensor("MB", [128, TB], BF16))
    RB = Eb(nc.sbuf_tensor("RB", [128, TB], BF16))
    SQS = Eb(nc.sbuf_tensor("SQS", [128, TB], BF16))
    CWB = Eb(nc.sbuf_tensor("CWB", [128, 2, 512], BF16))
    CWT = Eb(nc.sbuf_tensor("CWT", [16, TB], BF16))
    GSC = Eb(nc.sbuf_tensor("GSC", [128, 2, 512], BF16))
    ARN = Eb(nc.sbuf_tensor("ARN", [128, 592], F32))
    CW16 = Eb(nc.sbuf_tensor("CW16", [128, 128], BF16))
    POW = Eb(nc.sbuf_tensor("POW", [16, 26 * 16], BF16))
    KER1 = Eb(nc.sbuf_tensor("KER1", [128, 128], BF16))
    KER2 = Eb(nc.sbuf_tensor("KER2", [128, 128], BF16))
    KER3 = Eb(nc.sbuf_tensor("KER3", [128, 128], BF16))
    B1 = Eb(nc.sbuf_tensor("B1", [128, 256], BF16))
    S1SB = Eb(nc.sbuf_tensor("S1SB", [128, 256], BF16))
    B2 = Eb(nc.sbuf_tensor("B2", [128, 32], BF16))
    S2SB = Eb(nc.sbuf_tensor("S2SB", [128, 32], BF16))
    B3 = Eb(nc.sbuf_tensor("B3", [128, 8], BF16))
    S3SB = Eb(nc.sbuf_tensor("S3SB", [128, 8], BF16))
    E3 = Eb(nc.sbuf_tensor("E3", [16, 8], BF16))
    GC = Eb(nc.sbuf_tensor("GC", [16, 8], BF16))
    CIN3 = Eb(nc.sbuf_tensor("CIN3", [16, 8], BF16))
    CIN2 = Eb(nc.sbuf_tensor("CIN2", [16, 32], BF16))
    CIN1 = Eb(nc.sbuf_tensor("CIN1", [16, 256], BF16))
    UHS = Eb(nc.sbuf_tensor("UHS", [16, SEQ], BF16))

    pA = Eb(nc.psum_tensor("pA", [128, 512], F32))
    pB = Eb(nc.psum_tensor("pB", [128, 512], F32))
    pC = Eb(nc.psum_tensor("pC", [128, 512], F32))
    pD = Eb(nc.psum_tensor("pD", [128, 512], F32))
    pE = Eb(nc.psum_tensor("pE", [128, 1024], F32))
    pT = Eb(nc.psum_tensor("pT", [16, 1024], BF16))

    sem_t = Eb(nc.semaphore("semT"))
    sem_v = Eb(nc.semaphore("semV"))
    sem_s = Eb(nc.semaphore("semS"))
    sem_dw = Eb(nc.semaphore("semDW"))
    sem_ds = Eb(nc.semaphore("semDS"))
    sems = {"T": sem_t, "V": sem_v, "S": sem_s, "DW": sem_dw, "DS": sem_ds}

    p = Prog(nc)
    guard = {}

    def twait(*keys):
        return [guard[k] for k in keys if k in guard]

    ONES = CB[:, OFF_ONES:OFF_ONES + 128]
    IDENT = CB[:, OFF_ID:OFF_ID + 128]
    ID16 = CB[0:16, OFF_ID:OFF_ID + 16]
    A_ = CB[0:16, OFF_A:OFF_A + 16]
    AT_ = CB[0:16, OFF_AT:OFF_AT + 16]

    def UNSj(j):
        return CB[:, OFF_UNS + 16 * j: OFF_UNS + 16 * (j + 1)]

    def SELBe(e):
        return CB[0:16, OFF_SELB + 128 * e: OFF_SELB + 128 * (e + 1)]

    def STK0l(l):
        return CB[0:16, OFF_STK0 + 128 * l: OFF_STK0 + 128 * (l + 1)]

    def pow_slot(i):
        return POW[0:16, 16 * i:16 * (i + 1)]

    xv = d_x.rearrange("(c p) t -> p c t", p=128)
    p.y(lambda e: e.dma_start(out=X[:, :, 0:TB], in_=xv[:, :, 0:TB]))
    dsX0 = p.cnt["DS"]
    p.y(lambda e: e.dma_start(out=X[:, :, TB:SEQ], in_=xv[:, :, TB:SEQ]))
    dsX1 = p.cnt["DS"]
    p.y(lambda e: e.dma_start(out=CF[:, :], in_=d_cf))
    dsConst = p.cnt["DS"]
    p.g(lambda e: e.dma_start(out=CB[:, :], in_=d_cb))
    p.g(lambda e: e.dma_start(out=WGT[:, :, :].rearrange("p a b -> p (a b)"),
                              in_=d_wg))
    p.g(lambda e: e.dma_start(out=BMT[:, :, :].rearrange("p a b -> p (a b)"),
                              in_=d_bm))
    p.g(lambda e: e.dma_start(out=CMT[:, :], in_=d_cm))
    dwConst = p.cnt["DW"]

    w1_ready = [0, 0]
    state = {"w2g_ready": 0, "ws1_ready": 0, "wgate_ready": 0,
             "wout_ready": 0, "win_ready": 0, "sPowAll": 0, "vPowT": 0}
    kmarks = []

    def load_w1(e_, slot, wait_t):
        p.g(lambda eng: eng.dma_start(
            out=W1B[:, slot, :, :].rearrange("p a b -> p (a b)"), in_=d_w1[e_]),
            waits=[("T", wait_t)] if wait_t else [])
        w1_ready[slot] = p.cnt["DW"]

    def load_w2g(gidx, wait_t):
        p.g(lambda eng: eng.dma_start(
            out=W2GB[:, 0:8, :].rearrange("p a b -> p (a b)"),
            in_=d_w2g[gidx][:, 0:8 * DIM]),
            waits=[("T", wait_t)] if wait_t else [])
        p.g(lambda eng: eng.dma_start(
            out=W2GB[:, 8:16, :].rearrange("p a b -> p (a b)"),
            in_=d_w2g[gidx][:, 8 * DIM:16 * DIM]))
        state["w2g_ready"] = p.cnt["DW"]

    def load_into_w2gb(dram, half, wait_t):
        lo = 8 * half
        p.g(lambda eng: eng.dma_start(
            out=W2GB[:, lo:lo + 8, :].rearrange("p a b -> p (a b)"), in_=dram),
            waits=[("T", wait_t)] if wait_t else [])
        state["w2g_ready"] = p.cnt["DW"]
        return p.cnt["DW"]

    def load_ws1(dram, wait_t):
        p.g(lambda eng: eng.dma_start(
            out=WS1B[:, 0:4, :].rearrange("p a b -> p (a b)"),
            in_=dram[:, 0:4 * DIM]),
            waits=[("T", wait_t)] if wait_t else [])
        state["ws1_half"] = p.cnt["DW"]
        p.g(lambda eng: eng.dma_start(
            out=WS1B[:, 4:8, :].rearrange("p a b -> p (a b)"),
            in_=dram[:, 4 * DIM:8 * DIM]))
        state["ws1_ready"] = p.cnt["DW"]

    load_ws1(d_ws1, 0)
    load_into_w2gb(d_ws2, 0, 0)
    load_w1(0, 0, 0)
    load_w1(1, 1, 0)

    p.v(lambda e: e.memset(CIN3[:, :], 0.0))
    p.v(lambda e: e.memset(CIN2[:, :], 0.0))
    p.v(lambda e: e.memset(CIN1[:, :], 0.0))
    vInitCin = p.cnt["V"]

    cwb_free = [0, 0]
    gsc_free = [0, 0]
    dbg_i = [0]

    def make_lr(tb, waitX, do_routing, xln_free):
        out = {}
        st_ = {}

        def A0():
            p.s(lambda e: e.copy(XLN[:, :, :],
                                 X[:, :, tb * TB:(tb + 1) * TB]),
                waits=list(waitX) + ([("T", xln_free)] if xln_free else []))
            st_["sCopy"] = p.cnt["S"]

        def A():
            for tt in range(2):
                for hc in range(NHC):
                    w = ([("S", st_["sCopy"]), ("DW", dwConst)]
                         + twait("pE0", "pE1") if tt == 0 and hc == 0 else [])
                    p.t(lambda e, tt=tt, hc=hc: e.matmul(
                        pE[:, tt * 512:(tt + 1) * 512], ONES,
                        XLN[:, hc, tt * 512:(tt + 1) * 512],
                        start=(hc == 0), stop=(hc == NHC - 1)), waits=w)
            tSum = p.cnt["T"]
            p.s(lambda e: e.activation(MB[:, :], pE[:, :], AF.Copy,
                                       scale=1.0 / DIM), waits=[("T", tSum)])
            st_["sMb"] = p.cnt["S"]
            guard["pE0"] = ("S", st_["sMb"])
            guard["pE1"] = ("S", st_["sMb"])
            p.v(lambda e: e.tensor_tensor(
                out=RB[:, :], in0=MB[:, :], in1=MB[:, :], op=ALU.mult),
                waits=[("S", st_["sMb"])])
            st_["vMsq"] = p.cnt["V"]

        def B_hc(hc):
            w = [("S", st_["sCopy"])]
            if st_.get("pair_mark"):
                w.append(("T", st_["pair_mark"]))
            p.v(lambda e, hc=hc: e.tensor_tensor(
                out=SQS[:, :], in0=XLN[:, hc, :], in1=XLN[:, hc, :],
                op=ALU.mult), waits=w)
            sSq = p.cnt["V"]
            for tt in range(2):
                w2 = [("V", sSq)]
                if hc == 0 and tt == 0:
                    w2 += [("V", st_["vMsq"])]
                p.t(lambda e, tt=tt, hc=hc: e.matmul(
                    pE[:, tt * 512:(tt + 1) * 512], ONES,
                    SQS[:, tt * 512:(tt + 1) * 512],
                    start=(hc == 0), stop=(hc == NHC - 1)), waits=w2)
            st_["pair_mark"] = p.cnt["T"]

        def Btail():
            tSum2 = p.cnt["T"]
            st_["tSum2"] = tSum2
            p.vw(lambda e: e.scalar_tensor_tensor(
                out=RB[:, :], in0=RB[:, :], scalar=-float(DIM), in1=pE[:, :],
                op0=ALU.mult, op1=ALU.add), waits=[("T", st_["tSum2"])])
            vVar = p.cnt["V"]
            guard["pE0"] = ("V", vVar)
            guard["pE1"] = ("V", vVar)
            p.s(lambda e: e.activation(RB[:, :], RB[:, :], AF.Ln,
                                       bias=CF[:, CF_EPS:CF_EPS + 1],
                                       scale=1.0),
                waits=[("V", vVar), ("DS", dsConst)])
            p.s(lambda e: e.activation(RB[:, :], RB[:, :], AF.Exp,
                                       bias=CF[:, CF_LNC:CF_LNC + 1],
                                       scale=-0.5))
            sR = p.cnt["S"]
            out["vXlnT"] = [0, 0]
            for tt in range(2):
                sl = slice(tt * 512, (tt + 1) * 512)
                p.vw(lambda e, sl=sl: e.tensor_tensor(
                    out=XLN[:, :, sl], in0=XLN[:, :, sl],
                    in1=MB[:, sl].rearrange("p (a t) -> p a t", a=1)
                        .to_broadcast([128, 8, 512]), op=ALU.subtract),
                    waits=[("S", st_["sMb"]), ("T", st_["tSum2"])])
                p.vw(lambda e, sl=sl: e.tensor_tensor(
                    out=XLN[:, :, sl], in0=XLN[:, :, sl],
                    in1=RB[:, sl].rearrange("p (a t) -> p a t", a=1)
                        .to_broadcast([128, 8, 512]), op=ALU.mult),
                    waits=[("S", sR)])
                out["vXlnT"][tt] = p.cnt["V"]
            out["vXln"] = p.cnt["V"]

        def C():
            if do_routing:
                routing_emit(out["vXln"], st_, out.get("vXlnT"))

        def D():
            if not do_routing:
                out["sCwt"] = 0
                return
            vCw = st_["vCw"]
            for tc in range(8):
                w = [("V", vCw)] + (twait("pT") if tc == 0 else [])
                p.t(lambda e, tc=tc: e.transpose(
                    pT[0:16, tc * 128:(tc + 1) * 128],
                    CW16[:, tc * 16:(tc + 1) * 16], IDENT), waits=w)
            tTr = p.cnt["T"]
            p.s(lambda e: e.copy(CWT[0:16, :], pT[0:16, :]),
                waits=[("T", tTr)] + twait("CWT"))
            out["sCwt"] = p.cnt["S"]
            guard["pT"] = ("S", out["sCwt"])

        stages = {"A0": A0, "A": A, "Bh": B_hc, "Bt": Btail,
                  "C": C, "D": D}
        return stages, out

    def routing_emit(vXln, st_, vXlnT=None):
        for tc in range(8):
            for hc in range(NHC):
                w = []
                if tc == 0 and hc == 0:
                    w = [("V", vXlnT[0] if vXlnT else vXln),
                         ("DW", dwConst)] + twait("pC")
                elif tc == 4 and hc == 0 and vXlnT:
                    w = [("V", vXlnT[1])]
                p.t(lambda e, tc=tc, hc=hc: e.matmul(
                    pC[:, tc * 16:(tc + 1) * 16],
                    XLN[:, hc, tc * 128:(tc + 1) * 128], WGT[:, hc, :],
                    start=(hc == 0), stop=(hc == NHC - 1)), waits=w)
        tGate = p.cnt["T"]
        SCO = ARN[:, 0:128]
        SCH = ARN[:, 128:256]
        MSK = ARN[:, 256:384]
        WW = ARN[:, 384:512]
        GRP = ARN[:, 512:544]
        G2 = ARN[:, 544:576]
        TH = ARN[:, 576:584]
        RCP = ARN[:, 584:592]
        p.s(lambda e: e.activation(SCO, pC[:, 0:128], AF.Sigmoid),
            waits=[("T", tGate)])
        sSco = p.cnt["S"]
        guard["pC"] = ("S", sSco)

        def v3(ap, a, b):
            return ap.rearrange("p (a b) -> p a b", b=b)

        def dump(ap, width=128):
            if not DBG_ROUT:
                return
            i = dbg_i[0]
            dbg_i[0] += 1
            row = 6 + i // 16
            col = (i % 16) * 128
            p.vw(lambda e, ap=ap, row=row, col=col, width=width: e.tensor_copy(
                X[:, row, col:col + width], ap))

        p.vw(lambda e: e.tensor_tensor(
            out=v3(SCH, 8, 16), in0=v3(SCO, 8, 16),
            in1=CF[:, CF_BCB:CF_BCB + 16].rearrange("p (a b) -> p a b", a=1)
                .to_broadcast([128, 8, 16]), op=ALU.add),
            waits=[("S", sSco), ("DS", dsConst)])
        dump(SCH)
        p.vw(lambda e: e.tensor_reduce(
            out=GRP, in_=v3(SCH, 32, 4), axis=AX.X, op=ALU.max))
        dump(GRP, 32)
        p.vw(lambda e: e.tensor_tensor(
            out=v3(MSK, 32, 4), in0=v3(SCH, 32, 4),
            in1=v3(GRP, 32, 1).to_broadcast([128, 32, 4]), op=ALU.is_equal))
        p.vw(lambda e: e.scalar_tensor_tensor(
            out=WW, in0=MSK, scalar=-1e30, in1=SCH, op0=ALU.mult, op1=ALU.add))
        p.vw(lambda e: e.tensor_reduce(
            out=G2, in_=v3(WW, 32, 4), axis=AX.X, op=ALU.max))
        p.vw(lambda e: e.tensor_tensor(out=GRP, in0=GRP, in1=G2, op=ALU.add))
        p.vw(lambda e: e.tensor_reduce(
            out=TH, in_=v3(GRP, 8, 4), axis=AX.X, op=ALU.max))
        p.vw(lambda e: e.tensor_tensor(
            out=v3(G2, 8, 4), in0=v3(GRP, 8, 4),
            in1=v3(TH, 8, 1).to_broadcast([128, 8, 4]), op=ALU.is_equal))
        p.vw(lambda e: e.scalar_tensor_tensor(
            out=G2, in0=G2, scalar=-1e30, in1=GRP, op0=ALU.mult, op1=ALU.add))
        p.vw(lambda e: e.tensor_reduce(
            out=TH, in_=v3(G2, 8, 4), axis=AX.X, op=ALU.max))
        p.vw(lambda e: e.tensor_tensor(
            out=v3(G2, 8, 4), in0=v3(GRP, 8, 4),
            in1=v3(TH, 8, 1).to_broadcast([128, 8, 4]), op=ALU.is_ge))
        p.vw(lambda e: e.tensor_tensor(
            out=v3(MSK, 32, 4), in0=v3(SCH, 32, 4),
            in1=v3(G2, 32, 1).to_broadcast([128, 32, 4]), op=ALU.mult))
        dump(MSK)
        p.vw(lambda e: e.tensor_copy(WW, MSK))
        for k in range(4):
            p.vw(lambda e: e.tensor_reduce(
                out=TH, in_=v3(WW, 8, 16), axis=AX.X, op=ALU.max))
            if k < 3:
                p.vw(lambda e: e.tensor_tensor(
                    out=v3(SCH, 8, 16), in0=v3(WW, 8, 16),
                    in1=v3(TH, 8, 1).to_broadcast([128, 8, 16]),
                    op=ALU.is_equal))
                p.vw(lambda e: e.scalar_tensor_tensor(
                    out=WW, in0=SCH, scalar=-1e30, in1=WW,
                    op0=ALU.mult, op1=ALU.add))
        dump(TH, 8)
        p.vw(lambda e: e.tensor_tensor(
            out=v3(WW, 8, 16), in0=v3(MSK, 8, 16),
            in1=v3(TH, 8, 1).to_broadcast([128, 8, 16]), op=ALU.is_ge))
        p.vw(lambda e: e.tensor_tensor(out=WW, in0=SCO, in1=WW, op=ALU.mult))
        dump(WW)
        p.vw(lambda e: e.tensor_reduce(
            out=TH, in_=v3(WW, 8, 16), axis=AX.X, op=ALU.add))
        p.vw(lambda e: e.tensor_scalar(
            out=TH, in0=TH, scalar1=1e-20, scalar2=None, op0=ALU.add))
        p.vw(lambda e: e.reciprocal(RCP, TH))
        p.vw(lambda e: e.scalar_tensor_tensor(
            out=v3(CW16[:, :], 8, 16), in0=v3(WW, 8, 16), scalar=2.5,
            in1=v3(RCP, 8, 1).to_broadcast([128, 8, 16]),
            op0=ALU.mult, op1=ALU.mult))
        st_["vCw"] = p.cnt["V"]

    def ffn_body(tb, lrout_src, lr_spec, pending_d=None,
             pending_d2=None):
        pool = [(pA, "pA"), (pB, "pB"), (pD, "pD")]
        pidx = [0]

        def next_tile():
            t_ = pool[pidx[0] % 3]
            pidx[0] += 1
            return t_

        w1_marks = [0] * E
        hid_mark = [[0, 0] for _ in range(16)]
        marks = {}
        lr_out = None
        for gi in range(NG + 1):
            shared = (gi == 0)
            g = gi - 1
            if shared and pending_d is not None:
                pending_d()
                pending_d = None
            experts = [None] if shared else list(range(g * EG, (g + 1) * EG))
            for e_ in experts:
                cw_marks = [0, 0]
                if not shared:
                    slot = e_ % 2
                    sCwt = lrout_src["sCwt"]
                    for tt in range(2):
                        tile, tname = next_tile()
                        p.t(lambda en, tile=tile, e_=e_, tt=tt: en.matmul(
                            tile[:, :], SELBe(e_),
                            CWT[0:16, tt * 512:(tt + 1) * 512],
                            start=True, stop=True),
                            waits=[("S", sCwt)] + twait(tname))
                        tCwb = p.cnt["T"]
                        wv = [("T", tCwb)]
                        if cwb_free[tt]:
                            wv.append(("V", cwb_free[tt]))
                        p.s(lambda en, tile=tile, tt=tt: en.copy(
                            CWB[:, tt, :], tile[:, :]), waits=wv)
                        guard[tname] = ("S", p.cnt["S"])
                        cw_marks[tt] = p.cnt["S"]
                nch = 8 if shared else 4
                for ic in range(nch):
                    for tt in range(2):
                        tile, tname = next_tile()
                        for hc in range(NHC):
                            w = []
                            if hc == 0:
                                w = twait(tname) + [
                                    ("V", lrout_src["vXlnT"][tt])]
                                w.append(("DW", state["ws1_half"] if shared
                                          else w1_ready[slot]))
                            elif hc == 4 and shared:
                                w = [("DW", state["ws1_ready"])]
                            lhs = (WS1B[:, hc, ic * 128:(ic + 1) * 128]
                                   if shared else
                                   W1B[:, slot, hc, ic * 128:(ic + 1) * 128])
                            p.t(lambda en, lhs=lhs, tile=tile, tt=tt, hc=hc:
                                en.matmul(tile[:, :], lhs,
                                          XLN[:, hc, tt * 512:(tt + 1) * 512],
                                          start=(hc == 0),
                                          stop=(hc == NHC - 1)), waits=w)
                        tFill = p.cnt["T"]
                        k = ic if shared else 4 * (e_ - g * EG) + ic
                        hs = HID[:, k, tt * 512:(tt + 1) * 512]
                        p.s(lambda en, hs=hs, tile=tile: en.activation(
                            hs, tile[:, :], AF.Relu), waits=[("T", tFill)])
                        sRelu = p.cnt["S"]
                        guard[tname] = ("S", sRelu)
                        p.v(lambda en, hs=hs: en.tensor_tensor(
                            out=hs, in0=hs, in1=hs, op=ALU.mult),
                            waits=[("S", sRelu)])
                        if not shared:
                            p.vw(lambda en, hs=hs, tt=tt: en.tensor_tensor(
                                out=hs, in0=hs, in1=CWB[:, tt, :],
                                op=ALU.mult), waits=[("S", cw_marks[tt])])
                        hid_mark[k][tt] = p.cnt["V"]
                if not shared:
                    w1_marks[e_] = p.cnt["T"]
                    cwb_free[0] = p.cnt["V"]
                    cwb_free[1] = p.cnt["V"]
                    nxt = e_ + 2
                    if nxt < E:
                        load_w1(nxt, nxt % 2, w1_marks[e_])
            last_group = (gi == NG)
            if last_group:
                marks["t_xln_free"] = p.cnt["T"]
                guard["CWT"] = ("T", p.cnt["T"])
            nk = 8 if shared else 16
            lr_stages = None
            if last_group and lr_spec is not None:
                tb2, waitX2, do_rout2 = lr_spec
                lr_stages, lr_out = make_lr(tb2, waitX2, do_rout2,
                                            marks["t_xln_free"])
            tile_i = 0
            for oc in range(8):
                for tt in range(2):
                    if lr_stages is not None:
                        if tile_i == 0:
                            lr_stages["A0"]()
                        elif tile_i == 2:
                            lr_stages["A"]()
                        elif 3 <= tile_i <= 10:
                            lr_stages["Bh"](tile_i - 3)
                        elif tile_i == 11:
                            lr_stages["Bt"]()
                        elif tile_i == 13:
                            lr_stages["C"]()
                    tile_i += 1
                    tile, tname = next_tile()
                    for k in range(nk):
                        w = [("V", hid_mark[k][tt])]
                        if k == 0:
                            w += twait(tname) + [("DW", state["w2g_ready"])]
                        p.t(lambda en, tile=tile, k=k, oc=oc, tt=tt: en.matmul(
                            tile[:, :], W2GB[:, k, oc * 128:(oc + 1) * 128],
                            HID[:, k, tt * 512:(tt + 1) * 512],
                            start=(k == 0), stop=(k == nk - 1)), waits=w)
                    tFill = p.cnt["T"]
                    p.v(lambda en, tile=tile, oc=oc, tt=tt:
                        en.scalar_tensor_tensor(
                            out=X[:, oc, tb * TB + tt * 512:
                                  tb * TB + (tt + 1) * 512],
                            in0=tile[:, :], scalar=0.5,
                            in1=X[:, oc, tb * TB + tt * 512:
                                  tb * TB + (tt + 1) * 512],
                            op0=ALU.mult, op1=ALU.add), waits=[("T", tFill)])
                    guard[tname] = ("V", p.cnt["V"])
            tW2 = p.cnt["T"]
            if gi == 0 and pending_d2 is not None:
                pending_d2()
                pending_d2 = None
            if gi < NG:
                load_w2g(gi, tW2)
        marks["vX"] = p.cnt["V"]
        marks["tEnd"] = p.cnt["T"]
        if lr_stages is not None:
            marks["lr_d"] = lr_stages["D"]
        return marks, lr_out

    def attn_body(lra0_out, waitX_t1, lr_next_spec, pending_d=None):
        marks = {}
        vXlnT_cur = lra0_out["vXlnT"]
        ub_mark_s = [0, 0]
        u_mark = [[0, 0] for _ in range(8)]
        for tb in range(NTB):
            pool = [(pA, "pA"), (pB, "pB"), (pC, "pC"), (pD, "pD")]
            pi = [0]

            def nt():
                t_ = pool[pi[0] % 4]
                pi[0] += 1
                return t_

            for oc in range(8):
                for tt in range(2):
                    if pending_d is not None and (oc, tt) == (1, 0) and tb == 0:
                        pending_d()
                        pending_d = None
                    tileX, tnX = nt()
                    tileG, tnG = nt()
                    W1F = W1B[:, :, :, :].rearrange("p a b c -> p (a b c)")
                    for hc in range(NHC):
                        w = []
                        if hc == 0:
                            w = twait(tnX) + [("V", vXlnT_cur[tt]),
                                              ("DW", state["win_ready"])]
                        p.t(lambda en, W1F=W1F, oc=oc, tt=tt, hc=hc,
                            tileX=tileX:
                            en.matmul(tileX[:, :],
                                      W1F[:, hc * 1024 + oc * 128:
                                          hc * 1024 + (oc + 1) * 128],
                                      XLN[:, hc, tt * 512:(tt + 1) * 512],
                                      start=(hc == 0), stop=(hc == NHC - 1)),
                            waits=w)
                    tXin = p.cnt["T"]
                    for hc in range(NHC):
                        w = []
                        if hc == 0:
                            w = twait(tnG) + [("DW", state["wgate_ready"])]
                        p.t(lambda en, tileG=tileG, oc=oc, tt=tt, hc=hc:
                            en.matmul(tileG[:, :],
                                      W2GB[:, hc, oc * 128:(oc + 1) * 128],
                                      XLN[:, hc, tt * 512:(tt + 1) * 512],
                                      start=(hc == 0), stop=(hc == NHC - 1)),
                            waits=w)
                    tGat = p.cnt["T"]
                    hs = HID[:, oc, tt * 512:(tt + 1) * 512]
                    p.s(lambda en, hs=hs, tileX=tileX, oc=oc: en.activation(
                        hs, tileX[:, :], AF.Identity,
                        bias=CF[:, CF_BIN + oc:CF_BIN + oc + 1]),
                        waits=[("T", tXin), ("DS", dsConst)])
                    guard[tnX] = ("S", p.cnt["S"])
                    gpar = (oc * 2 + tt) % 2
                    wv = [("T", tGat)]
                    if gsc_free[gpar]:
                        wv.append(("V", gsc_free[gpar]))
                    p.s(lambda en, tileG=tileG, oc=oc, gpar=gpar: en.activation(
                        GSC[:, gpar, :], tileG[:, :], AF.Sigmoid,
                        bias=CF[:, CF_BGATE + oc:CF_BGATE + oc + 1]), waits=wv)
                    sG = p.cnt["S"]
                    guard[tnG] = ("S", sG)
                    p.v(lambda en, hs=hs, gpar=gpar: en.tensor_tensor(
                        out=hs, in0=hs, in1=GSC[:, gpar, :], op=ALU.mult),
                        waits=[("S", sG)])
                    gsc_free[gpar] = p.cnt["V"]
                    u_mark[oc][tt] = p.cnt["V"]
            vU = p.cnt["V"]
            if tb == 0:
                marks["t_a_xln0"] = p.cnt["T"]
                lr_t1, lr_t1_out = make_lr(1, waitX_t1, False,
                                           marks["t_a_xln0"])
                marks["lr_t1_out"] = lr_t1_out
                lr_t1["A0"]()
            else:
                marks["t_xln_free"] = p.cnt["T"]
            for tt in range(2):
                for hc in range(NHC):
                    w = [("V", u_mark[hc][tt])]
                    if hc == 0:
                        w += twait("pC") + [("DW", dwConst)]
                    p.t(lambda en, tt=tt, hc=hc: en.matmul(
                        pC[0:16, :], BMT[:, hc, :],
                        HID[:, hc, tt * 512:(tt + 1) * 512],
                        start=(hc == 0), stop=(hc == NHC - 1)), waits=w)
                tUb = p.cnt["T"]
                p.s(lambda en, tb=tb, tt=tt: en.copy(
                    UHS[0:16, tb * TB + tt * 512:tb * TB + (tt + 1) * 512],
                    pC[0:16, :]), waits=[("T", tUb)])
                guard["pC"] = ("S", p.cnt["S"])
            ub_mark_s[tb] = p.cnt["S"]

            if tb == 0:
                lr_t1["A"]()
                p.v(lambda e: e.tensor_copy(pow_slot(0), AT_),
                    waits=[("DW", dwConst)])
                vAT = p.cnt["V"]
                ppool = [(pC, "pC"), (pD, "pD")]
                par = [0]

                def npp():
                    t_ = ppool[par[0] % 2]
                    par[0] += 1
                    return t_

                spow = [0]

                def chain(lhs_fn, base, first_waits):
                    for k in range(1, 8):
                        tile, tn = npp()
                        w = (list(first_waits) if k == 1 else
                             [("V", spow[0])]) + twait(tn)
                        p.t(lambda e, tile=tile, k=k, base=base: e.matmul(
                            tile[0:16, 0:16], lhs_fn(),
                            pow_slot(base + k - 1), start=True, stop=True),
                            waits=w)
                        tPk = p.cnt["T"]
                        p.v(lambda e, tile=tile, k=k, base=base: e.tensor_copy(
                            pow_slot(base + k), tile[0:16, 0:16]),
                            waits=[("T", tPk)])
                        spow[0] = p.cnt["V"]
                        guard[tn] = ("V", spow[0])

                chain(lambda: A_, 0, [("V", vAT), ("DW", dwConst)])
                lr_t1["Bh"](0)
                lr_t1["Bh"](1)
                p.t(lambda e: e.transpose(pT[0:16, 0:16], pow_slot(7), ID16),
                    waits=[("V", spow[0])] + twait("pT"))
                tP = p.cnt["T"]
                p.v(lambda e: e.tensor_copy(pow_slot(24), pT[0:16, 0:16]),
                    waits=[("T", tP)])
                spow[0] = p.cnt["V"]
                guard["pT"] = ("V", spow[0])
                p.vw(lambda e: e.tensor_copy(pow_slot(8), pow_slot(7)))
                vC = p.cnt["V"]
                lr_t1["Bh"](2)
                lr_t1["Bh"](3)
                chain(lambda: pow_slot(24), 8, [("V", vC)])
                lr_t1["Bh"](4)
                lr_t1["Bh"](5)
                p.t(lambda e: e.transpose(pT[0:16, 16:32], pow_slot(15), ID16),
                    waits=[("V", spow[0])] + twait("pT"))
                tP = p.cnt["T"]
                p.v(lambda e: e.tensor_copy(pow_slot(25), pT[0:16, 16:32]),
                    waits=[("T", tP)])
                spow[0] = p.cnt["V"]
                guard["pT"] = ("V", spow[0])
                p.vw(lambda e: e.tensor_copy(pow_slot(16), pow_slot(15)))
                vC = p.cnt["V"]
                chain(lambda: pow_slot(25), 16, [("V", vC)])
                state["sPowAll"] = spow[0]
                lr_t1["Bh"](6)
                lr_t1["Bh"](7)
                lr_t1["Bt"]()

                def build_ker(K, src0):
                    first = True
                    for j in range(8):
                        for i in range(j + 1):
                            d = j - i
                            rhs = ID16 if d == 0 else pow_slot(src0 + d - 1)
                            w = ([("V", state["sPowAll"])] + twait("pD")
                                 if first else [])
                            first = False
                            p.t(lambda e, j=j, i=i, rhs=rhs: e.matmul(
                                pD[:, 16 * j:16 * (j + 1)], STK0l(i), rhs,
                                start=(i == 0), stop=(i == j)), waits=w)
                    tKer = p.cnt["T"]
                    p.v(lambda e, K=K: e.tensor_copy(K[:, :], pD[:, 0:128]),
                        waits=[("T", tKer)])
                    sK = p.cnt["V"]
                    guard["pD"] = ("V", sK)
                    return sK

                kmarks.clear()
                kmarks.append(build_ker(KER1, 0))
                kmarks.append(build_ker(KER2, 8))
                kmarks.append(build_ker(KER3, 16))
                ARNB = ARN[:, :].bitcast(BF16)
                for idx, src0 in enumerate((0, 8, 16)):
                    for k in range(8):
                        p.vw(lambda e, idx=idx, src0=src0, k=k: e.tensor_copy(
                            ARNB[0:16, idx * 128 + 16 * k:
                                 idx * 128 + 16 * (k + 1)],
                            pow_slot(src0 + k)),
                            waits=[("V", state["sPowAll"])])
                state["vPowT"] = p.cnt["V"]
                for l in range(8):
                    w = ([("S", ub_mark_s[0]), ("DW", dwConst)]
                         + twait("pE0") if l == 0 else [])
                    p.t(lambda e, l=l: e.matmul(
                        pE[:, 0:128], STK0l(l), UHS[0:16, l::8][:, 0:128],
                        start=(l == 0), stop=(l == 7)), waits=w)
                vXlnT_cur = lr_t1_out["vXlnT"]

        ARNB = ARN[:, :].bitcast(BF16)

        def POWT(i):
            return ARNB[0:16, i * 128:(i + 1) * 128]

        for l in range(8):
            w = ([("S", ub_mark_s[1])] if l == 0 else [])
            p.t(lambda e, l=l: e.matmul(
                pE[:, 128:256], STK0l(l), UHS[0:16, l::8][:, 128:256],
                start=(l == 0), stop=(l == 7)), waits=w)
        tB1 = p.cnt["T"]
        p.s(lambda e: e.copy(B1[:, :], pE[:, 0:256]), waits=[("T", tB1)])
        sB1 = p.cnt["S"]
        guard["pE0"] = ("S", sB1)
        p.t(lambda e: e.matmul(pE[:, 512:768], KER1[:, :], B1[:, :],
                               start=True, stop=False, skip_group_check=True),
            waits=[("V", kmarks[0]), ("S", sB1)] + twait("pE1"))
        tS1 = p.cnt["T"]
        p.s(lambda e: e.copy(S1SB[:, :], pE[:, 512:768]), waits=[("T", tS1)])
        sS1 = p.cnt["S"]
        p.t(lambda e: e.matmul(pC[0:16, 0:256], UNSj(7), S1SB[:, :],
                               start=True, stop=True),
            waits=[("S", sS1)] + twait("pC"))
        tE1 = p.cnt["T"]
        p.s(lambda e: e.copy(B1[0:16, :], pC[0:16, 0:256]), waits=[("T", tE1)])
        sE1 = p.cnt["S"]
        guard["pC"] = ("S", sE1)
        for l in range(8):
            w = ([("S", sE1)] + twait("pD") if l == 0 else [])
            p.t(lambda e, l=l: e.matmul(
                pD[:, 0:32], STK0l(l), B1[0:16, l::8][:, 0:32],
                start=(l == 0), stop=(l == 7)), waits=w)
        tB2 = p.cnt["T"]
        p.s(lambda e: e.copy(B2[:, :], pD[:, 0:32]), waits=[("T", tB2)])
        sB2 = p.cnt["S"]
        guard["pD"] = ("S", sB2)
        p.t(lambda e: e.matmul(pA[:, 0:32], KER2[:, :], B2[:, :],
                               start=True, stop=False, skip_group_check=True),
            waits=[("V", kmarks[1]), ("S", sB2)] + twait("pA"))
        tS2 = p.cnt["T"]
        p.s(lambda e: e.copy(S2SB[:, :], pA[:, 0:32]), waits=[("T", tS2)])
        sS2 = p.cnt["S"]
        p.t(lambda e: e.matmul(pC[0:16, 256:288], UNSj(7), S2SB[:, :],
                               start=True, stop=True),
            waits=[("S", sS2)] + twait("pC"))
        tE2 = p.cnt["T"]
        p.s(lambda e: e.copy(B1[0:16, 0:32], pC[0:16, 256:288]),
            waits=[("T", tE2)])
        sE2 = p.cnt["S"]
        guard["pC"] = ("S", sE2)
        for l in range(8):
            w = ([("S", sE2)] + twait("pD") if l == 0 else [])
            p.t(lambda e, l=l: e.matmul(
                pD[:, 64:68], STK0l(l), B1[0:16, l::8][:, 0:4],
                start=(l == 0), stop=(l == 7)), waits=w)
        tB3 = p.cnt["T"]
        p.s(lambda e: e.copy(B3[:, 0:4], pD[:, 64:68]), waits=[("T", tB3)])
        sB3 = p.cnt["S"]
        guard["pD"] = ("S", sB3)
        p.t(lambda e: e.matmul(pB[:, 0:4], KER3[:, :], B3[:, 0:4],
                               start=True, stop=False, skip_group_check=True),
            waits=[("V", kmarks[2]), ("S", sB3)] + twait("pB"))
        tS3 = p.cnt["T"]
        p.s(lambda e: e.copy(S3SB[:, 0:4], pB[:, 0:4]), waits=[("T", tS3)])
        sS3 = p.cnt["S"]
        p.t(lambda e: e.matmul(pC[0:16, 288:292], UNSj(7), S3SB[:, 0:4],
                               start=True, stop=True),
            waits=[("S", sS3)] + twait("pC"))
        tE3 = p.cnt["T"]
        p.s(lambda e: e.copy(E3[0:16, 0:4], pC[0:16, 288:292]),
            waits=[("T", tE3)])
        sE3 = p.cnt["S"]
        guard["pC"] = ("S", sE3)
        p.v(lambda e: e.tensor_copy(GC[0:16, 0:1], E3[0:16, 0:1]),
            waits=[("S", sE3)])
        vG = p.cnt["V"]
        for c in range(1, 4):
            p.t(lambda e, c=c: e.matmul(
                pC[0:16, 296 + c:297 + c], pow_slot(23), GC[0:16, c - 1:c],
                start=True, stop=True),
                waits=[("V", vG), ("V", state["sPowAll"])])
            tGc = p.cnt["T"]
            p.v(lambda e, c=c: e.tensor_tensor(
                out=GC[0:16, c:c + 1], in0=pC[0:16, 296 + c:297 + c],
                in1=E3[0:16, c:c + 1], op=ALU.add), waits=[("T", tGc)])
            vG = p.cnt["V"]
        guard["pC"] = ("V", vG)
        p.vw(lambda e: e.tensor_copy(CIN3[0:16, 1:4], GC[0:16, 0:3]),
             waits=[("V", vG), ("V", vInitCin)])
        vCin3 = p.cnt["V"]
        p.t(lambda e: e.matmul(pB[:, 0:4], POWT(2), CIN3[0:16, 0:4],
                               start=False, stop=True, skip_group_check=True),
            waits=[("V", vCin3), ("V", state["vPowT"])])
        tC3 = p.cnt["T"]
        p.s(lambda e: e.copy(S3SB[:, 0:4], pB[:, 0:4]), waits=[("T", tC3)])
        sS3t = p.cnt["S"]
        guard["pB"] = ("S", sS3t)
        for l3 in range(8):
            w = ([("S", sS3t)] + twait("pC") if l3 == 0 else [])
            p.t(lambda e, l3=l3: e.matmul(
                pC[0:16, 304 + 4 * l3:304 + 4 * (l3 + 1)], UNSj(l3),
                S3SB[:, 0:4], start=True, stop=True), waits=w)
        tCin2 = p.cnt["T"]
        for l3 in range(8):
            if l3 < 7:
                p.s(lambda e, l3=l3: e.copy(
                    CIN2[0:16, (l3 + 1)::8][:, 0:4],
                    pC[0:16, 304 + 4 * l3:304 + 4 * (l3 + 1)]),
                    waits=[("T", tCin2)] if l3 == 0 else [])
            else:
                p.s(lambda e: e.copy(CIN2[0:16, 8::8][:, 0:3],
                                     pC[0:16, 332:335]))
        sCin2 = p.cnt["S"]
        guard["pC"] = ("S", sCin2)
        p.t(lambda e: e.matmul(pA[:, 0:32], POWT(1), CIN2[0:16, :],
                               start=False, stop=True, skip_group_check=True),
            waits=[("S", sCin2)])
        tC2 = p.cnt["T"]
        p.s(lambda e: e.copy(S2SB[:, :], pA[:, 0:32]), waits=[("T", tC2)])
        sS2t = p.cnt["S"]
        guard["pA"] = ("S", sS2t)
        for l2 in range(8):
            w = ([("S", sS2t)] + twait("pC") if l2 == 0 else [])
            p.t(lambda e, l2=l2: e.matmul(
                pC[0:16, 32 * l2:32 * (l2 + 1)], UNSj(l2), S2SB[:, :],
                start=True, stop=True), waits=w)
        tCin1 = p.cnt["T"]
        for l2 in range(8):
            if l2 < 7:
                p.s(lambda e, l2=l2: e.copy(
                    CIN1[0:16, (l2 + 1)::8][:, 0:32],
                    pC[0:16, 32 * l2:32 * (l2 + 1)]),
                    waits=[("T", tCin1)] if l2 == 0 else [])
            else:
                p.s(lambda e: e.copy(CIN1[0:16, 8::8][:, 0:31],
                                     pC[0:16, 224:255]))
        sCin1 = p.cnt["S"]
        guard["pC"] = ("S", sCin1)
        p.t(lambda e: e.matmul(pE[:, 512:768], POWT(0), CIN1[0:16, :],
                               start=False, stop=True, skip_group_check=True),
            waits=[("S", sCin1)])
        tC1 = p.cnt["T"]
        p.s(lambda e: e.copy(S1SB[:, :], pE[:, 512:768]), waits=[("T", tC1)])
        sHst = p.cnt["S"]
        guard["pE1"] = ("S", sHst)
        for j in range(8):
            tile, tn = (pC, "pC") if j % 2 == 0 else (pD, "pD")
            p.t(lambda e, j=j, tile=tile: e.matmul(
                tile[0:16, 256:512], UNSj(j), S1SB[:, :],
                start=True, stop=True), waits=[("S", sHst)] + twait(tn))
            tU = p.cnt["T"]
            p.s(lambda e, j=j, tile=tile: e.copy(
                UHS[0:16, j::8][:, 0:256], tile[0:16, 256:512]),
                waits=[("T", tU)])
            guard[tn] = ("S", p.cnt["S"])
        sHs = p.cnt["S"]

        pool2 = [(pA, "pA"), (pB, "pB"), (pD, "pD")]
        pj = [0]

        def nt2():
            t_ = pool2[pj[0] % 3]
            pj[0] += 1
            return t_

        lr_next = None
        lr_next_out = None
        for gt in range(4):
            ypar = gt % 2
            if lr_next_spec is not None and gt == 2:
                lr_next["A"]()
            for oc in range(8):
                tile, tn = nt2()
                w = ([("S", sHs), ("DW", dwConst)] + twait(tn) +
                     twait("Y%d" % ypar))
                p.t(lambda e, oc=oc, gt=gt, tile=tile: e.matmul(
                    tile[:, :], CMT[0:16, oc * 128:(oc + 1) * 128],
                    UHS[0:16, gt * 512:(gt + 1) * 512], start=True, stop=True),
                    waits=w)
                tY = p.cnt["T"]
                p.s(lambda e, oc=oc, ypar=ypar, tile=tile: e.copy(
                    HID[:, 8 + oc, ypar * 512:(ypar + 1) * 512], tile[:, :]),
                    waits=[("T", tY)])
                guard[tn] = ("S", p.cnt["S"])
            sY = p.cnt["S"]
            for oc in range(8):
                tile, tn = nt2()
                for hc in range(NHC):
                    w = []
                    if hc == 0:
                        w = twait(tn) + [("S", sY),
                                         ("DW", state["wout_ready"])]
                    p.t(lambda e, oc=oc, hc=hc, ypar=ypar, tile=tile: e.matmul(
                        tile[:, :], W2GB[:, 8 + hc, oc * 128:(oc + 1) * 128],
                        HID[:, 8 + hc, ypar * 512:(ypar + 1) * 512],
                        start=(hc == 0), stop=(hc == NHC - 1)), waits=w)
                tO = p.cnt["T"]
                p.v(lambda e, oc=oc, gt=gt, tile=tile: e.scalar_tensor_tensor(
                    out=X[:, oc, gt * 512:(gt + 1) * 512], in0=tile[:, :],
                    scalar=CF[:, CF_BOUT + oc:CF_BOUT + oc + 1],
                    in1=X[:, oc, gt * 512:(gt + 1) * 512],
                    op0=ALU.add, op1=ALU.add), waits=[("T", tO)])
                guard[tn] = ("V", p.cnt["V"])
                if lr_next is not None and gt == 2:
                    lr_next["Bh"](oc)
                    if oc == 7:
                        lr_next["Bt"]()

            guard["Y%d" % ypar] = ("T", p.cnt["T"])
            if gt == 1 and lr_next_spec is not None:
                vX01 = p.cnt["V"]
                tb2, do_rout2 = lr_next_spec
                lr_next, lr_next_out = make_lr(
                    tb2, [("V", vX01)], do_rout2, marks["t_xln_free"])
                lr_next["A0"]()
        if lr_next is not None:
            lr_next["C"]()
            marks["lr_d"] = lr_next["D"]
        marks["vX"] = p.cnt["V"]
        marks["tEnd"] = p.cnt["T"]
        return marks, lr_next_out

    lr0, lr0_out = make_lr(0, [("DS", dsX0)], True, 0)
    lr0["A0"]()
    lr0["A"]()
    for i_ in range(8):
        lr0["Bh"](i_)
    lr0["Bt"]()

    m_f1t0, lr1_out = ffn_body(0, lr0_out,
                               (1, [("DS", dsX1)], True),
                               pending_d=lr0["C"], pending_d2=lr0["D"])
    load_into_w2gb(d_ws2, 0, m_f1t0["tEnd"])
    load_w1(0, 0, m_f1t0["tEnd"])
    load_w1(1, 1, m_f1t0["tEnd"])

    m_f1t1, lra0_out = ffn_body(1, lr1_out,
                                (0, [("V", m_f1t0["vX"])], False),
                                pending_d=m_f1t0["lr_d"])
    p.g(lambda eng: eng.dma_start(
        out=W1B[:, :, :, :].rearrange("p a b c -> p (a b c)"), in_=d_win),
        waits=[("T", m_f1t1["t_xln_free"])])
    state["win_ready"] = p.cnt["DW"]
    state["wgate_ready"] = load_into_w2gb(d_wgate, 0, m_f1t1["tEnd"])
    state["wout_ready"] = load_into_w2gb(d_wout, 1, m_f1t1["tEnd"])

    m_attn, lrf2_out = attn_body(lra0_out,
                                 [("V", m_f1t1["vX"])], (0, True),
                                 pending_d=m_f1t1["lr_d"])
    load_into_w2gb(d_ws2, 0, m_attn["t_xln_free"])
    load_w1(0, 0, m_attn["t_xln_free"])
    load_w1(1, 1, m_attn["t_xln_free"])

    m_f2t0, lrf2t1_out = ffn_body(0, lrf2_out,
                                  (1, [("V", m_attn["vX"])], True),
                                  pending_d=m_attn["lr_d"])
    load_into_w2gb(d_ws2, 0, m_f2t0["tEnd"])
    load_w1(0, 0, m_f2t0["tEnd"])
    load_w1(1, 1, m_f2t0["tEnd"])

    ov = d_out.rearrange("(c p) t -> p c t", p=128)
    p.y(lambda e: e.dma_start(out=ov[:, :, 0:TB], in_=X[:, :, 0:TB]),
        waits=[("V", m_f2t0["vX"])])

    m_f2t1, _ = ffn_body(1, lrf2t1_out, None,
                         pending_d=m_f2t0["lr_d"])

    p.y(lambda e: e.dma_start(out=ov[:, :, TB:SEQ], in_=X[:, :, TB:SEQ]),
        waits=[("V", m_f2t1["vX"])])
    dsOut = p.cnt["DS"]
    p.ywait([("DS", dsOut)])

    with nc.Block() as block:
        p.emit(block, sems)
    ctx.close()
    return nc


def _host_prepare(inp):
    f32 = np.float32
    x = np.ascontiguousarray(inp["L0_attn_x"], dtype=f32)
    w1 = np.ascontiguousarray(inp["L0_ffn_W1"], dtype=f32)
    w2 = np.ascontiguousarray(inp["L0_ffn_W2"], dtype=f32)
    ws1 = np.ascontiguousarray(inp["L0_ffn_Ws1"], dtype=f32)
    ws2 = np.ascontiguousarray(inp["L0_ffn_Ws2"], dtype=f32)
    wg = np.ascontiguousarray(inp["L0_ffn_Wg"], dtype=f32)
    bias_corr = np.ascontiguousarray(inp["L0_ffn_bias_corr"], dtype=f32)
    win = np.ascontiguousarray(inp["L0_attn_W_in"], dtype=f32)
    b_in = np.ascontiguousarray(inp["L0_attn_b_in"], dtype=f32)
    a = np.ascontiguousarray(inp["L0_attn_A"], dtype=f32)
    bm = np.ascontiguousarray(inp["L0_attn_Bm"], dtype=f32)
    cm = np.ascontiguousarray(inp["L0_attn_Cm"], dtype=f32)
    wgate = np.ascontiguousarray(inp["L0_attn_W_gate"], dtype=f32)
    b_gate = np.ascontiguousarray(inp["L0_attn_b_gate"], dtype=f32)
    wout = np.ascontiguousarray(inp["L0_attn_W_out"], dtype=f32)
    b_out = np.ascontiguousarray(inp["L0_attn_b_out"], dtype=f32)

    def fm_chunks(wT):
        inf, outf = wT.shape
        return np.ascontiguousarray(
            wT.reshape(inf // 128, 128, outf).transpose(1, 0, 2).reshape(128, -1))

    w1t = np.stack([fm_chunks(w1[e].T) for e in range(E)])
    w2g = np.zeros((NG, 128, 16 * DIM), f32)
    for g in range(NG):
        for ei in range(EG):
            w2g[g][:, ei * 4 * DIM:(ei + 1) * 4 * DIM] = fm_chunks(
                w2[g * EG + ei].T)
    ws1t = fm_chunks(ws1.T)
    ws2t = fm_chunks(ws2.T)
    wint = fm_chunks(win[DIM:2 * DIM].T)
    wgatet = fm_chunks(wgate.T)
    woutt = fm_chunks(wout.T)
    wgt = fm_chunks(wg.T)
    bmt = fm_chunks(bm.T)
    cmt = np.ascontiguousarray(cm.T)

    cb = np.zeros((128, CBW), f32)
    cb[:, OFF_ONES:OFF_ONES + 128] = 1.0
    cb[:, OFF_ID:OFF_ID + 128] = np.eye(128)
    for j in range(8):
        blk = np.zeros((128, 16), f32)
        for m in range(16):
            blk[16 * j + m, m] = 1.0
        cb[:, OFF_UNS + 16 * j:OFF_UNS + 16 * (j + 1)] = blk
    for e in range(E):
        blk = np.zeros((16, 128), f32)
        blk[e, :] = 1.0
        cb[0:16, OFF_SELB + 128 * e:OFF_SELB + 128 * (e + 1)] = blk
    for l in range(8):
        blk = np.zeros((16, 128), f32)
        for n in range(16):
            blk[n, 16 * l + n] = 1.0
        cb[0:16, OFF_STK0 + 128 * l:OFF_STK0 + 128 * (l + 1)] = blk
    cb[0:16, OFF_A:OFF_A + 16] = a
    cb[0:16, OFF_AT:OFF_AT + 16] = a.T

    cf = np.zeros((128, CFW), f32)
    cf[:, CF_BCB:CF_BCB + 16] = np.broadcast_to(bias_corr, (128, 16))
    cf[:, CF_BIN:CF_BIN + 8] = b_in[DIM:2 * DIM].reshape(8, 128).T
    cf[:, CF_BGATE:CF_BGATE + 8] = b_gate.reshape(8, 128).T
    cf[:, CF_BOUT:CF_BOUT + 8] = b_out.reshape(8, 128).T
    cf[:, CF_EPS] = DIM * LNEPS
    cf[:, CF_LNC] = np.log(np.sqrt(float(DIM)))

    common = {
        "w1t": w1t, "w2g": w2g, "ws1t": ws1t, "ws2t": ws2t,
        "wint": wint, "wgatet": wgatet, "woutt": woutt,
        "wgt": wgt, "bmt": bmt, "cmt": cmt, "cb": cb, "cf": cf,
    }
    in_maps = []
    for b in range(NCORES):
        m = dict(common)
        m["xt"] = np.ascontiguousarray(x[b].T)
        in_maps.append(m)
    return in_maps


def kernel(**inputs):
    in_maps = _host_prepare(inputs)
    nc = build_program()
    res = run_bass_kernel_spmd(nc, in_maps, core_ids=list(range(NCORES)))
    out = np.empty((NCORES, SEQ, DIM), np.float32)
    for b in range(NCORES):
        out[b] = np.asarray(res.results[b]["yt"], np.float32).T
    return out


# version 12
# speedup vs baseline: 1.0097x; 1.0097x over previous
import numpy as np
import ml_dtypes

import concourse.bass as bass
import concourse.mybir as mybir
from concourse.bass_utils import run_bass_kernel_spmd

F32 = mybir.dt.float32
BF16 = mybir.dt.bfloat16
AF = mybir.ActivationFunctionType
ALU = mybir.AluOpType
AX = mybir.AxisListType

NCORES = 8
DIM = 1024
SEQ = 2048
NHC = 8
NTB = 2
TB = 1024
E = 16
EG = 4
NG = 4
I_EXP = 512
LNEPS = 1e-5

OFF_ONES = 0
OFF_ID = 128
OFF_UNS = 256
OFF_SELB = 384
OFF_STK0 = OFF_SELB + 2048
OFF_A = OFF_STK0 + 1024
OFF_AT = OFF_A + 16
CBW = OFF_AT + 16

CF_BCB = 0
CF_BIN = 16
CF_BGATE = 24
CF_BOUT = 32
CF_EPS = 40
CF_LNC = 41
CFW = 48

STAGES = "full"
DBG_ROUT = False


class Prog:
    ENGS = ("t", "v", "s", "g", "y")

    def __init__(self, nc):
        self.nc = nc
        self.streams = {e: [] for e in self.ENGS}
        self.cnt = {"T": 0, "V": 0, "S": 0, "DW": 0, "DS": 0}

    def _op(self, eng, fn, waits, inc, n):
        for s, v in waits:
            assert v <= self.cnt[s], f"wait {s}>={v} emitted={self.cnt[s]}"
        if inc is not None:
            self.cnt[inc] += n
        self.streams[eng].append((tuple(waits), fn, inc, n))
        return self.cnt[inc] if inc is not None else None

    def t(self, fn, waits=()):
        return self._op("t", fn, waits, "T", 1)

    def v(self, fn, waits=()):
        return self._op("v", fn, waits, "V", 1)

    def vw(self, fn, waits=()):
        w = list(waits) + [("V", self.cnt["V"])]
        return self._op("v", fn, w, "V", 1)

    def s(self, fn, waits=()):
        return self._op("s", fn, waits, "S", 1)

    def g(self, fn, waits=()):
        return self._op("g", fn, waits, "DW", 16)

    def y(self, fn, waits=()):
        return self._op("y", fn, waits, "DS", 16)

    def ywait(self, waits):
        self.streams["y"].append((tuple(waits), None, None, 0))

    def emit(self, block, sems):
        def mk(eng_key, deco):
            stream = self.streams[eng_key]

            def body(eng):
                seen = {k: 0 for k in self.cnt}
                for waits, fn, inc, n in stream:
                    for s, v in waits:
                        if v > seen[s]:
                            eng.wait_ge(sems[s], v)
                            seen[s] = v
                    if fn is None:
                        continue
                    inst = fn(eng)
                    if inc is not None:
                        inst.then_inc(sems[inc], n)
            deco(body)

        mk("t", block.tensor)
        mk("v", block.vector)
        mk("s", block.scalar)
        mk("g", block.gpsimd)
        mk("y", block.sync)


def build_program():
    nc = bass.Bass("TRN2", target_bir_lowering=False)

    d_x = nc.dram_tensor("xt", [DIM, SEQ], F32, kind="ExternalInput").ap()
    d_w1 = nc.dram_tensor("w1t", [E, 128, 8 * I_EXP], F32, kind="ExternalInput").ap()
    d_w2g = nc.dram_tensor("w2g", [NG, 128, 16 * DIM], F32, kind="ExternalInput").ap()
    d_ws1 = nc.dram_tensor("ws1t", [128, 8 * DIM], F32, kind="ExternalInput").ap()
    d_ws2 = nc.dram_tensor("ws2t", [128, 8 * DIM], F32, kind="ExternalInput").ap()
    d_win = nc.dram_tensor("wint", [128, 8 * DIM], F32, kind="ExternalInput").ap()
    d_wgate = nc.dram_tensor("wgatet", [128, 8 * DIM], F32, kind="ExternalInput").ap()
    d_wout = nc.dram_tensor("woutt", [128, 8 * DIM], F32, kind="ExternalInput").ap()
    d_wg = nc.dram_tensor("wgt", [128, 8 * 16], F32, kind="ExternalInput").ap()
    d_bm = nc.dram_tensor("bmt", [128, 8 * 16], F32, kind="ExternalInput").ap()
    d_cm = nc.dram_tensor("cmt", [16, DIM], F32, kind="ExternalInput").ap()
    d_cb = nc.dram_tensor("cb", [128, CBW], F32, kind="ExternalInput").ap()
    d_cf = nc.dram_tensor("cf", [128, CFW], F32, kind="ExternalInput").ap()
    d_out = nc.dram_tensor("yt", [DIM, SEQ], F32, kind="ExternalOutput").ap()

    from contextlib import ExitStack
    ctx = ExitStack()
    Eb = ctx.enter_context

    X = Eb(nc.sbuf_tensor("X", [128, NHC, SEQ], F32))
    XLN = Eb(nc.sbuf_tensor("XLN", [128, NHC, TB], BF16))
    HID = Eb(nc.sbuf_tensor("HID", [128, 16, TB], BF16))
    W1B = Eb(nc.sbuf_tensor("W1B", [128, 2, 8, I_EXP], BF16))
    W2GB = Eb(nc.sbuf_tensor("W2GB", [128, 16, DIM], BF16))
    WS1B = Eb(nc.sbuf_tensor("WS1B", [128, 8, DIM], BF16))
    CB = Eb(nc.sbuf_tensor("CB", [128, CBW], BF16))
    CF = Eb(nc.sbuf_tensor("CF", [128, CFW], F32))
    WGT = Eb(nc.sbuf_tensor("WGT", [128, 8, 16], BF16))
    BMT = Eb(nc.sbuf_tensor("BMT", [128, 8, 16], BF16))
    CMT = Eb(nc.sbuf_tensor("CMT", [16, DIM], BF16))
    MB = Eb(nc.sbuf_tensor("MB", [128, TB], BF16))
    RB = Eb(nc.sbuf_tensor("RB", [128, TB], BF16))
    SQS = Eb(nc.sbuf_tensor("SQS", [128, TB], BF16))
    CWB = Eb(nc.sbuf_tensor("CWB", [128, 2, 512], BF16))
    CWT = Eb(nc.sbuf_tensor("CWT", [16, TB], BF16))
    GSC = Eb(nc.sbuf_tensor("GSC", [128, 2, 512], BF16))
    ARN = Eb(nc.sbuf_tensor("ARN", [128, 592], F32))
    CW16 = Eb(nc.sbuf_tensor("CW16", [128, 128], BF16))
    POW = Eb(nc.sbuf_tensor("POW", [16, 26 * 16], BF16))
    KER1 = Eb(nc.sbuf_tensor("KER1", [128, 128], BF16))
    KER2 = Eb(nc.sbuf_tensor("KER2", [128, 128], BF16))
    KER3 = Eb(nc.sbuf_tensor("KER3", [128, 128], BF16))
    B1 = Eb(nc.sbuf_tensor("B1", [128, 256], BF16))
    S1SB = Eb(nc.sbuf_tensor("S1SB", [128, 256], BF16))
    B2 = Eb(nc.sbuf_tensor("B2", [128, 32], BF16))
    S2SB = Eb(nc.sbuf_tensor("S2SB", [128, 32], BF16))
    B3 = Eb(nc.sbuf_tensor("B3", [128, 8], BF16))
    S3SB = Eb(nc.sbuf_tensor("S3SB", [128, 8], BF16))
    E3 = Eb(nc.sbuf_tensor("E3", [16, 8], BF16))
    GC = Eb(nc.sbuf_tensor("GC", [16, 8], BF16))
    CIN3 = Eb(nc.sbuf_tensor("CIN3", [16, 8], BF16))
    CIN2 = Eb(nc.sbuf_tensor("CIN2", [16, 32], BF16))
    CIN1 = Eb(nc.sbuf_tensor("CIN1", [16, 256], BF16))
    UHS = Eb(nc.sbuf_tensor("UHS", [16, SEQ], BF16))

    pA = Eb(nc.psum_tensor("pA", [128, 512], F32))
    pB = Eb(nc.psum_tensor("pB", [128, 512], F32))
    pC = Eb(nc.psum_tensor("pC", [128, 512], F32))
    pD = Eb(nc.psum_tensor("pD", [128, 512], F32))
    pE = Eb(nc.psum_tensor("pE", [128, 1024], F32))
    pT = Eb(nc.psum_tensor("pT", [16, 1024], BF16))

    sem_t = Eb(nc.semaphore("semT"))
    sem_v = Eb(nc.semaphore("semV"))
    sem_s = Eb(nc.semaphore("semS"))
    sem_dw = Eb(nc.semaphore("semDW"))
    sem_ds = Eb(nc.semaphore("semDS"))
    sems = {"T": sem_t, "V": sem_v, "S": sem_s, "DW": sem_dw, "DS": sem_ds}

    p = Prog(nc)
    guard = {}

    def twait(*keys):
        return [guard[k] for k in keys if k in guard]

    ONES = CB[:, OFF_ONES:OFF_ONES + 128]
    IDENT = CB[:, OFF_ID:OFF_ID + 128]
    ID16 = CB[0:16, OFF_ID:OFF_ID + 16]
    A_ = CB[0:16, OFF_A:OFF_A + 16]
    AT_ = CB[0:16, OFF_AT:OFF_AT + 16]

    def UNSj(j):
        return CB[:, OFF_UNS + 16 * j: OFF_UNS + 16 * (j + 1)]

    def SELBe(e):
        return CB[0:16, OFF_SELB + 128 * e: OFF_SELB + 128 * (e + 1)]

    def STK0l(l):
        return CB[0:16, OFF_STK0 + 128 * l: OFF_STK0 + 128 * (l + 1)]

    def pow_slot(i):
        return POW[0:16, 16 * i:16 * (i + 1)]

    xv = d_x.rearrange("(c p) t -> p c t", p=128)
    p.y(lambda e: e.dma_start(out=X[:, :, 0:TB], in_=xv[:, :, 0:TB]))
    dsX0 = p.cnt["DS"]
    p.y(lambda e: e.dma_start(out=X[:, :, TB:SEQ], in_=xv[:, :, TB:SEQ]))
    dsX1 = p.cnt["DS"]
    p.y(lambda e: e.dma_start(out=CF[:, :], in_=d_cf))
    dsConst = p.cnt["DS"]
    p.g(lambda e: e.dma_start(out=CB[:, :], in_=d_cb))
    p.g(lambda e: e.dma_start(out=WGT[:, :, :].rearrange("p a b -> p (a b)"),
                              in_=d_wg))
    p.g(lambda e: e.dma_start(out=BMT[:, :, :].rearrange("p a b -> p (a b)"),
                              in_=d_bm))
    p.g(lambda e: e.dma_start(out=CMT[:, :], in_=d_cm))
    dwConst = p.cnt["DW"]

    w1_ready = [0, 0]
    state = {"w2g_ready": 0, "ws1_ready": 0, "wgate_ready": 0,
             "wout_ready": 0, "win_ready": 0, "sPowAll": 0, "vPowT": 0}
    kmarks = []

    def load_w1(e_, slot, wait_t):
        p.g(lambda eng: eng.dma_start(
            out=W1B[:, slot, :, :].rearrange("p a b -> p (a b)"), in_=d_w1[e_]),
            waits=[("T", wait_t)] if wait_t else [])
        w1_ready[slot] = p.cnt["DW"]

    def load_w2g(gidx, wait_t):
        p.g(lambda eng: eng.dma_start(
            out=W2GB[:, 0:8, :].rearrange("p a b -> p (a b)"),
            in_=d_w2g[gidx][:, 0:8 * DIM]),
            waits=[("T", wait_t)] if wait_t else [])
        p.g(lambda eng: eng.dma_start(
            out=W2GB[:, 8:16, :].rearrange("p a b -> p (a b)"),
            in_=d_w2g[gidx][:, 8 * DIM:16 * DIM]))
        state["w2g_ready"] = p.cnt["DW"]

    def load_into_w2gb(dram, half, wait_t):
        lo = 8 * half
        p.g(lambda eng: eng.dma_start(
            out=W2GB[:, lo:lo + 8, :].rearrange("p a b -> p (a b)"), in_=dram),
            waits=[("T", wait_t)] if wait_t else [])
        state["w2g_ready"] = p.cnt["DW"]
        return p.cnt["DW"]

    def load_ws1(dram, wait_t):
        p.g(lambda eng: eng.dma_start(
            out=WS1B[:, 0:4, :].rearrange("p a b -> p (a b)"),
            in_=dram[:, 0:4 * DIM]),
            waits=[("T", wait_t)] if wait_t else [])
        state["ws1_half"] = p.cnt["DW"]
        p.g(lambda eng: eng.dma_start(
            out=WS1B[:, 4:8, :].rearrange("p a b -> p (a b)"),
            in_=dram[:, 4 * DIM:8 * DIM]))
        state["ws1_ready"] = p.cnt["DW"]

    load_ws1(d_ws1, 0)
    load_into_w2gb(d_ws2, 0, 0)
    load_w1(0, 0, 0)
    load_w1(1, 1, 0)

    p.v(lambda e: e.memset(CIN3[:, :], 0.0))
    p.v(lambda e: e.memset(CIN2[:, :], 0.0))
    p.v(lambda e: e.memset(CIN1[:, :], 0.0))
    vInitCin = p.cnt["V"]

    cwb_free = [0, 0]
    gsc_free = [0, 0]
    dbg_i = [0]

    def make_lr(tb, waitX, do_routing, xln_free):
        out = {}
        st_ = {}

        def A0():
            p.s(lambda e: e.copy(XLN[:, :, :],
                                 X[:, :, tb * TB:(tb + 1) * TB]),
                waits=list(waitX) + ([("T", xln_free)] if xln_free else []))
            st_["sCopy"] = p.cnt["S"]

        def A():
            for tt in range(2):
                for hc in range(NHC):
                    w = ([("S", st_["sCopy"]), ("DW", dwConst)]
                         + twait("pE0", "pE1") if tt == 0 and hc == 0 else [])
                    p.t(lambda e, tt=tt, hc=hc: e.matmul(
                        pE[:, tt * 512:(tt + 1) * 512], ONES,
                        XLN[:, hc, tt * 512:(tt + 1) * 512],
                        start=(hc == 0), stop=(hc == NHC - 1)), waits=w)
            tSum = p.cnt["T"]
            p.s(lambda e: e.activation(MB[:, :], pE[:, :], AF.Copy,
                                       scale=1.0 / DIM), waits=[("T", tSum)])
            st_["sMb"] = p.cnt["S"]
            guard["pE0"] = ("S", st_["sMb"])
            guard["pE1"] = ("S", st_["sMb"])
            p.v(lambda e: e.tensor_tensor(
                out=RB[:, :], in0=MB[:, :], in1=MB[:, :], op=ALU.mult),
                waits=[("S", st_["sMb"])])
            st_["vMsq"] = p.cnt["V"]

        def B_hc(hc):
            w = [("S", st_["sCopy"])]
            if st_.get("pair_mark"):
                w.append(("T", st_["pair_mark"]))
            p.v(lambda e, hc=hc: e.tensor_tensor(
                out=SQS[:, :], in0=XLN[:, hc, :], in1=XLN[:, hc, :],
                op=ALU.mult), waits=w)
            sSq = p.cnt["V"]
            for tt in range(2):
                w2 = [("V", sSq)]
                if hc == 0 and tt == 0:
                    w2 += [("V", st_["vMsq"])]
                p.t(lambda e, tt=tt, hc=hc: e.matmul(
                    pE[:, tt * 512:(tt + 1) * 512], ONES,
                    SQS[:, tt * 512:(tt + 1) * 512],
                    start=(hc == 0), stop=(hc == NHC - 1)), waits=w2)
            st_["pair_mark"] = p.cnt["T"]

        def Btail():
            tSum2 = p.cnt["T"]
            st_["tSum2"] = tSum2
            p.vw(lambda e: e.scalar_tensor_tensor(
                out=RB[:, :], in0=RB[:, :], scalar=-float(DIM), in1=pE[:, :],
                op0=ALU.mult, op1=ALU.add), waits=[("T", st_["tSum2"])])
            vVar = p.cnt["V"]
            guard["pE0"] = ("V", vVar)
            guard["pE1"] = ("V", vVar)
            p.s(lambda e: e.activation(RB[:, :], RB[:, :], AF.Ln,
                                       bias=CF[:, CF_EPS:CF_EPS + 1],
                                       scale=1.0),
                waits=[("V", vVar), ("DS", dsConst)])
            p.s(lambda e: e.activation(RB[:, :], RB[:, :], AF.Exp,
                                       bias=CF[:, CF_LNC:CF_LNC + 1],
                                       scale=-0.5))
            sR = p.cnt["S"]
            out["vXlnT"] = [0, 0]
            for tt in range(2):
                sl = slice(tt * 512, (tt + 1) * 512)
                p.vw(lambda e, sl=sl: e.tensor_tensor(
                    out=XLN[:, :, sl], in0=XLN[:, :, sl],
                    in1=MB[:, sl].rearrange("p (a t) -> p a t", a=1)
                        .to_broadcast([128, 8, 512]), op=ALU.subtract),
                    waits=[("S", st_["sMb"]), ("T", st_["tSum2"])])
                p.vw(lambda e, sl=sl: e.tensor_tensor(
                    out=XLN[:, :, sl], in0=XLN[:, :, sl],
                    in1=RB[:, sl].rearrange("p (a t) -> p a t", a=1)
                        .to_broadcast([128, 8, 512]), op=ALU.mult),
                    waits=[("S", sR)])
                out["vXlnT"][tt] = p.cnt["V"]
            out["vXln"] = p.cnt["V"]

        def C():
            if do_routing:
                routing_emit(out["vXln"], st_, out.get("vXlnT"))

        def D():
            if not do_routing:
                out["sCwt"] = 0
                return
            vCw = st_["vCw"]
            for tc in range(8):
                w = [("V", vCw)] + (twait("pT") if tc == 0 else [])
                p.t(lambda e, tc=tc: e.transpose(
                    pT[0:16, tc * 128:(tc + 1) * 128],
                    CW16[:, tc * 16:(tc + 1) * 16], IDENT), waits=w)
            tTr = p.cnt["T"]
            p.s(lambda e: e.copy(CWT[0:16, :], pT[0:16, :]),
                waits=[("T", tTr)] + twait("CWT"))
            out["sCwt"] = p.cnt["S"]
            guard["pT"] = ("S", out["sCwt"])

        stages = {"A0": A0, "A": A, "Bh": B_hc, "Bt": Btail,
                  "C": C, "D": D}
        return stages, out

    def routing_emit(vXln, st_, vXlnT=None):
        for tc in range(8):
            for hc in range(NHC):
                w = []
                if tc == 0 and hc == 0:
                    w = [("V", vXlnT[0] if vXlnT else vXln),
                         ("DW", dwConst)] + twait("pC")
                elif tc == 4 and hc == 0 and vXlnT:
                    w = [("V", vXlnT[1])]
                p.t(lambda e, tc=tc, hc=hc: e.matmul(
                    pC[:, tc * 16:(tc + 1) * 16],
                    XLN[:, hc, tc * 128:(tc + 1) * 128], WGT[:, hc, :],
                    start=(hc == 0), stop=(hc == NHC - 1)), waits=w)
        tGate = p.cnt["T"]
        SCO = ARN[:, 0:128]
        SCH = ARN[:, 128:256]
        MSK = ARN[:, 256:384]
        WW = ARN[:, 384:512]
        GRP = ARN[:, 512:544]
        G2 = ARN[:, 544:576]
        TH = ARN[:, 576:584]
        RCP = ARN[:, 584:592]
        p.s(lambda e: e.activation(SCO, pC[:, 0:128], AF.Sigmoid),
            waits=[("T", tGate)])
        sSco = p.cnt["S"]
        guard["pC"] = ("S", sSco)

        def v3(ap, a, b):
            return ap.rearrange("p (a b) -> p a b", b=b)

        def dump(ap, width=128):
            if not DBG_ROUT:
                return
            i = dbg_i[0]
            dbg_i[0] += 1
            row = 6 + i // 16
            col = (i % 16) * 128
            p.vw(lambda e, ap=ap, row=row, col=col, width=width: e.tensor_copy(
                X[:, row, col:col + width], ap))

        p.vw(lambda e: e.tensor_tensor(
            out=v3(SCH, 8, 16), in0=v3(SCO, 8, 16),
            in1=CF[:, CF_BCB:CF_BCB + 16].rearrange("p (a b) -> p a b", a=1)
                .to_broadcast([128, 8, 16]), op=ALU.add),
            waits=[("S", sSco), ("DS", dsConst)])
        dump(SCH)
        p.vw(lambda e: e.tensor_reduce(
            out=GRP, in_=v3(SCH, 32, 4), axis=AX.X, op=ALU.max))
        dump(GRP, 32)
        p.vw(lambda e: e.tensor_tensor(
            out=v3(MSK, 32, 4), in0=v3(SCH, 32, 4),
            in1=v3(GRP, 32, 1).to_broadcast([128, 32, 4]), op=ALU.is_equal))
        p.vw(lambda e: e.scalar_tensor_tensor(
            out=WW, in0=MSK, scalar=-1e30, in1=SCH, op0=ALU.mult, op1=ALU.add))
        p.vw(lambda e: e.tensor_reduce(
            out=G2, in_=v3(WW, 32, 4), axis=AX.X, op=ALU.max))
        p.vw(lambda e: e.tensor_tensor(out=GRP, in0=GRP, in1=G2, op=ALU.add))
        p.vw(lambda e: e.tensor_reduce(
            out=TH, in_=v3(GRP, 8, 4), axis=AX.X, op=ALU.max))
        p.vw(lambda e: e.tensor_tensor(
            out=v3(G2, 8, 4), in0=v3(GRP, 8, 4),
            in1=v3(TH, 8, 1).to_broadcast([128, 8, 4]), op=ALU.is_equal))
        p.vw(lambda e: e.scalar_tensor_tensor(
            out=G2, in0=G2, scalar=-1e30, in1=GRP, op0=ALU.mult, op1=ALU.add))
        p.vw(lambda e: e.tensor_reduce(
            out=TH, in_=v3(G2, 8, 4), axis=AX.X, op=ALU.max))
        p.vw(lambda e: e.tensor_tensor(
            out=v3(G2, 8, 4), in0=v3(GRP, 8, 4),
            in1=v3(TH, 8, 1).to_broadcast([128, 8, 4]), op=ALU.is_ge))
        p.vw(lambda e: e.tensor_tensor(
            out=v3(MSK, 32, 4), in0=v3(SCH, 32, 4),
            in1=v3(G2, 32, 1).to_broadcast([128, 32, 4]), op=ALU.mult))
        dump(MSK)
        p.vw(lambda e: e.tensor_copy(WW, MSK))
        for k in range(4):
            p.vw(lambda e: e.tensor_reduce(
                out=TH, in_=v3(WW, 8, 16), axis=AX.X, op=ALU.max))
            if k < 3:
                p.vw(lambda e: e.tensor_tensor(
                    out=v3(SCH, 8, 16), in0=v3(WW, 8, 16),
                    in1=v3(TH, 8, 1).to_broadcast([128, 8, 16]),
                    op=ALU.is_equal))
                p.vw(lambda e: e.scalar_tensor_tensor(
                    out=WW, in0=SCH, scalar=-1e30, in1=WW,
                    op0=ALU.mult, op1=ALU.add))
        dump(TH, 8)
        p.vw(lambda e: e.tensor_tensor(
            out=v3(WW, 8, 16), in0=v3(MSK, 8, 16),
            in1=v3(TH, 8, 1).to_broadcast([128, 8, 16]), op=ALU.is_ge))
        p.vw(lambda e: e.tensor_tensor(out=WW, in0=SCO, in1=WW, op=ALU.mult))
        dump(WW)
        p.vw(lambda e: e.tensor_reduce(
            out=TH, in_=v3(WW, 8, 16), axis=AX.X, op=ALU.add))
        p.vw(lambda e: e.tensor_scalar(
            out=TH, in0=TH, scalar1=1e-20, scalar2=None, op0=ALU.add))
        p.vw(lambda e: e.reciprocal(RCP, TH))
        p.vw(lambda e: e.scalar_tensor_tensor(
            out=v3(CW16[:, :], 8, 16), in0=v3(WW, 8, 16), scalar=2.5,
            in1=v3(RCP, 8, 1).to_broadcast([128, 8, 16]),
            op0=ALU.mult, op1=ALU.mult))
        st_["vCw"] = p.cnt["V"]

    def ffn_body(tb, lrout_src, lr_spec, pending_d=None,
             pending_d2=None, pending_head=None):
        pool = [(pA, "pA"), (pB, "pB"), (pD, "pD")]
        pidx = [0]

        def next_tile():
            t_ = pool[pidx[0] % 3]
            pidx[0] += 1
            return t_

        w1_marks = [0] * E
        hid_mark = [[0, 0] for _ in range(16)]
        marks = {}
        lr_out = None
        for gi in range(NG + 1):
            shared = (gi == 0)
            g = gi - 1
            if shared and pending_head is not None:
                pending_head()
            experts = [None] if shared else list(range(g * EG, (g + 1) * EG))
            for e_ in experts:
                cw_marks = [0, 0]
                if not shared:
                    slot = e_ % 2
                    sCwt = lrout_src["sCwt"]
                    for tt in range(2):
                        tile, tname = next_tile()
                        p.t(lambda en, tile=tile, e_=e_, tt=tt: en.matmul(
                            tile[:, :], SELBe(e_),
                            CWT[0:16, tt * 512:(tt + 1) * 512],
                            start=True, stop=True),
                            waits=[("S", sCwt)] + twait(tname))
                        tCwb = p.cnt["T"]
                        wv = [("T", tCwb)]
                        if cwb_free[tt]:
                            wv.append(("V", cwb_free[tt]))
                        p.s(lambda en, tile=tile, tt=tt: en.copy(
                            CWB[:, tt, :], tile[:, :]), waits=wv)
                        guard[tname] = ("S", p.cnt["S"])
                        cw_marks[tt] = p.cnt["S"]
                nch = 8 if shared else 4
                for ic in range(nch):
                    for tt in range(2):
                        tile, tname = next_tile()
                        for hc in range(NHC):
                            w = []
                            if hc == 0:
                                w = twait(tname) + [
                                    ("V", lrout_src["vXlnT"][tt])]
                                w.append(("DW", state["ws1_half"] if shared
                                          else w1_ready[slot]))
                            elif hc == 4 and shared:
                                w = [("DW", state["ws1_ready"])]
                            lhs = (WS1B[:, hc, ic * 128:(ic + 1) * 128]
                                   if shared else
                                   W1B[:, slot, hc, ic * 128:(ic + 1) * 128])
                            p.t(lambda en, lhs=lhs, tile=tile, tt=tt, hc=hc:
                                en.matmul(tile[:, :], lhs,
                                          XLN[:, hc, tt * 512:(tt + 1) * 512],
                                          start=(hc == 0),
                                          stop=(hc == NHC - 1)), waits=w)
                        tFill = p.cnt["T"]
                        k = ic if shared else 4 * (e_ - g * EG) + ic
                        hs = HID[:, k, tt * 512:(tt + 1) * 512]
                        p.s(lambda en, hs=hs, tile=tile: en.activation(
                            hs, tile[:, :], AF.Relu), waits=[("T", tFill)])
                        sRelu = p.cnt["S"]
                        guard[tname] = ("S", sRelu)
                        p.v(lambda en, hs=hs: en.tensor_tensor(
                            out=hs, in0=hs, in1=hs, op=ALU.mult),
                            waits=[("S", sRelu)])
                        if not shared:
                            p.vw(lambda en, hs=hs, tt=tt: en.tensor_tensor(
                                out=hs, in0=hs, in1=CWB[:, tt, :],
                                op=ALU.mult), waits=[("S", cw_marks[tt])])
                        hid_mark[k][tt] = p.cnt["V"]
                if not shared:
                    w1_marks[e_] = p.cnt["T"]
                    cwb_free[0] = p.cnt["V"]
                    cwb_free[1] = p.cnt["V"]
                    nxt = e_ + 2
                    if nxt < E:
                        load_w1(nxt, nxt % 2, w1_marks[e_])
            if gi == 0 and pending_d is not None:
                pending_d()
                pending_d = None
            last_group = (gi == NG)
            if last_group:
                marks["t_xln_free"] = p.cnt["T"]
                guard["CWT"] = ("T", p.cnt["T"])
            nk = 8 if shared else 16
            lr_stages = None
            if last_group and lr_spec is not None:
                tb2, waitX2, do_rout2 = lr_spec
                lr_stages, lr_out = make_lr(tb2, waitX2, do_rout2,
                                            marks["t_xln_free"])
            tile_i = 0
            for oc in range(8):
                for tt in range(2):
                    if lr_stages is not None:
                        if tile_i == 0:
                            lr_stages["A0"]()
                        elif tile_i == 2:
                            lr_stages["A"]()
                        elif 3 <= tile_i <= 10:
                            lr_stages["Bh"](tile_i - 3)
                        elif tile_i == 11:
                            lr_stages["Bt"]()
                        elif tile_i == 13:
                            lr_stages["C"]()
                    tile_i += 1
                    tile, tname = next_tile()
                    for k in range(nk):
                        w = [("V", hid_mark[k][tt])]
                        if k == 0:
                            w += twait(tname) + [("DW", state["w2g_ready"])]
                        p.t(lambda en, tile=tile, k=k, oc=oc, tt=tt: en.matmul(
                            tile[:, :], W2GB[:, k, oc * 128:(oc + 1) * 128],
                            HID[:, k, tt * 512:(tt + 1) * 512],
                            start=(k == 0), stop=(k == nk - 1)), waits=w)
                    tFill = p.cnt["T"]
                    p.v(lambda en, tile=tile, oc=oc, tt=tt:
                        en.scalar_tensor_tensor(
                            out=X[:, oc, tb * TB + tt * 512:
                                  tb * TB + (tt + 1) * 512],
                            in0=tile[:, :], scalar=0.5,
                            in1=X[:, oc, tb * TB + tt * 512:
                                  tb * TB + (tt + 1) * 512],
                            op0=ALU.mult, op1=ALU.add), waits=[("T", tFill)])
                    guard[tname] = ("V", p.cnt["V"])
            tW2 = p.cnt["T"]
            if gi == 0 and pending_d2 is not None:
                pending_d2()
                pending_d2 = None
            if gi < NG:
                load_w2g(gi, tW2)
        marks["vX"] = p.cnt["V"]
        marks["tEnd"] = p.cnt["T"]
        if lr_stages is not None:
            marks["lr_d"] = lr_stages["D"]
        return marks, lr_out

    def attn_body(lra0_out, waitX_t1, lr_next_spec, pending_d=None):
        marks = {}
        vXlnT_cur = lra0_out["vXlnT"]
        ub_mark_s = [0, 0]
        u_mark = [[0, 0] for _ in range(8)]
        for tb in range(NTB):
            pool = [(pA, "pA"), (pB, "pB"), (pC, "pC"), (pD, "pD")]
            pi = [0]

            def nt():
                t_ = pool[pi[0] % 4]
                pi[0] += 1
                return t_

            for oc in range(8):
                for tt in range(2):
                    if pending_d is not None and (oc, tt) == (1, 0) and tb == 0:
                        pending_d()
                        pending_d = None
                    tileX, tnX = nt()
                    tileG, tnG = nt()
                    W1F = W1B[:, :, :, :].rearrange("p a b c -> p (a b c)")
                    for hc in range(NHC):
                        w = []
                        if hc == 0:
                            w = twait(tnX) + [("V", vXlnT_cur[tt]),
                                              ("DW", state["win_ready"])]
                        p.t(lambda en, W1F=W1F, oc=oc, tt=tt, hc=hc,
                            tileX=tileX:
                            en.matmul(tileX[:, :],
                                      W1F[:, hc * 1024 + oc * 128:
                                          hc * 1024 + (oc + 1) * 128],
                                      XLN[:, hc, tt * 512:(tt + 1) * 512],
                                      start=(hc == 0), stop=(hc == NHC - 1)),
                            waits=w)
                    tXin = p.cnt["T"]
                    for hc in range(NHC):
                        w = []
                        if hc == 0:
                            w = twait(tnG) + [("DW", state["wgate_ready"])]
                        p.t(lambda en, tileG=tileG, oc=oc, tt=tt, hc=hc:
                            en.matmul(tileG[:, :],
                                      W2GB[:, hc, oc * 128:(oc + 1) * 128],
                                      XLN[:, hc, tt * 512:(tt + 1) * 512],
                                      start=(hc == 0), stop=(hc == NHC - 1)),
                            waits=w)
                    tGat = p.cnt["T"]
                    hs = HID[:, oc, tt * 512:(tt + 1) * 512]
                    p.s(lambda en, hs=hs, tileX=tileX, oc=oc: en.activation(
                        hs, tileX[:, :], AF.Identity,
                        bias=CF[:, CF_BIN + oc:CF_BIN + oc + 1]),
                        waits=[("T", tXin), ("DS", dsConst)])
                    guard[tnX] = ("S", p.cnt["S"])
                    gpar = (oc * 2 + tt) % 2
                    wv = [("T", tGat)]
                    if gsc_free[gpar]:
                        wv.append(("V", gsc_free[gpar]))
                    p.s(lambda en, tileG=tileG, oc=oc, gpar=gpar: en.activation(
                        GSC[:, gpar, :], tileG[:, :], AF.Sigmoid,
                        bias=CF[:, CF_BGATE + oc:CF_BGATE + oc + 1]), waits=wv)
                    sG = p.cnt["S"]
                    guard[tnG] = ("S", sG)
                    p.v(lambda en, hs=hs, gpar=gpar: en.tensor_tensor(
                        out=hs, in0=hs, in1=GSC[:, gpar, :], op=ALU.mult),
                        waits=[("S", sG)])
                    gsc_free[gpar] = p.cnt["V"]
                    u_mark[oc][tt] = p.cnt["V"]
            vU = p.cnt["V"]
            if tb == 0:
                marks["t_a_xln0"] = p.cnt["T"]
                lr_t1, lr_t1_out = make_lr(1, waitX_t1, False,
                                           marks["t_a_xln0"])
                marks["lr_t1_out"] = lr_t1_out
                lr_t1["A0"]()
            else:
                marks["t_xln_free"] = p.cnt["T"]
            for tt in range(2):
                for hc in range(NHC):
                    w = [("V", u_mark[hc][tt])]
                    if hc == 0:
                        w += twait("pC") + [("DW", dwConst)]
                    p.t(lambda en, tt=tt, hc=hc: en.matmul(
                        pC[0:16, :], BMT[:, hc, :],
                        HID[:, hc, tt * 512:(tt + 1) * 512],
                        start=(hc == 0), stop=(hc == NHC - 1)), waits=w)
                tUb = p.cnt["T"]
                p.s(lambda en, tb=tb, tt=tt: en.copy(
                    UHS[0:16, tb * TB + tt * 512:tb * TB + (tt + 1) * 512],
                    pC[0:16, :]), waits=[("T", tUb)])
                guard["pC"] = ("S", p.cnt["S"])
            ub_mark_s[tb] = p.cnt["S"]

            if tb == 0:
                lr_t1["A"]()
                p.v(lambda e: e.tensor_copy(pow_slot(0), AT_),
                    waits=[("DW", dwConst)])
                vAT = p.cnt["V"]
                ppool = [(pC, "pC"), (pD, "pD")]
                par = [0]

                def npp():
                    t_ = ppool[par[0] % 2]
                    par[0] += 1
                    return t_

                spow = [0]

                def chain(lhs_fn, base, first_waits):
                    for k in range(1, 8):
                        tile, tn = npp()
                        w = (list(first_waits) if k == 1 else
                             [("V", spow[0])]) + twait(tn)
                        p.t(lambda e, tile=tile, k=k, base=base: e.matmul(
                            tile[0:16, 0:16], lhs_fn(),
                            pow_slot(base + k - 1), start=True, stop=True),
                            waits=w)
                        tPk = p.cnt["T"]
                        p.v(lambda e, tile=tile, k=k, base=base: e.tensor_copy(
                            pow_slot(base + k), tile[0:16, 0:16]),
                            waits=[("T", tPk)])
                        spow[0] = p.cnt["V"]
                        guard[tn] = ("V", spow[0])

                chain(lambda: A_, 0, [("V", vAT), ("DW", dwConst)])
                lr_t1["Bh"](0)
                lr_t1["Bh"](1)
                p.t(lambda e: e.transpose(pT[0:16, 0:16], pow_slot(7), ID16),
                    waits=[("V", spow[0])] + twait("pT"))
                tP = p.cnt["T"]
                p.v(lambda e: e.tensor_copy(pow_slot(24), pT[0:16, 0:16]),
                    waits=[("T", tP)])
                spow[0] = p.cnt["V"]
                guard["pT"] = ("V", spow[0])
                p.vw(lambda e: e.tensor_copy(pow_slot(8), pow_slot(7)))
                vC = p.cnt["V"]
                lr_t1["Bh"](2)
                lr_t1["Bh"](3)
                chain(lambda: pow_slot(24), 8, [("V", vC)])
                lr_t1["Bh"](4)
                lr_t1["Bh"](5)
                p.t(lambda e: e.transpose(pT[0:16, 16:32], pow_slot(15), ID16),
                    waits=[("V", spow[0])] + twait("pT"))
                tP = p.cnt["T"]
                p.v(lambda e: e.tensor_copy(pow_slot(25), pT[0:16, 16:32]),
                    waits=[("T", tP)])
                spow[0] = p.cnt["V"]
                guard["pT"] = ("V", spow[0])
                p.vw(lambda e: e.tensor_copy(pow_slot(16), pow_slot(15)))
                vC = p.cnt["V"]
                chain(lambda: pow_slot(25), 16, [("V", vC)])
                state["sPowAll"] = spow[0]
                lr_t1["Bh"](6)
                lr_t1["Bh"](7)
                lr_t1["Bt"]()

                def build_ker(K, src0):
                    first = True
                    for j in range(8):
                        for i in range(j + 1):
                            d = j - i
                            rhs = ID16 if d == 0 else pow_slot(src0 + d - 1)
                            w = ([("V", state["sPowAll"])] + twait("pD")
                                 if first else [])
                            first = False
                            p.t(lambda e, j=j, i=i, rhs=rhs: e.matmul(
                                pD[:, 16 * j:16 * (j + 1)], STK0l(i), rhs,
                                start=(i == 0), stop=(i == j)), waits=w)
                    tKer = p.cnt["T"]
                    p.v(lambda e, K=K: e.tensor_copy(K[:, :], pD[:, 0:128]),
                        waits=[("T", tKer)])
                    sK = p.cnt["V"]
                    guard["pD"] = ("V", sK)
                    return sK

                kmarks.clear()
                kmarks.append(build_ker(KER1, 0))
                kmarks.append(build_ker(KER2, 8))
                kmarks.append(build_ker(KER3, 16))
                ARNB = ARN[:, :].bitcast(BF16)
                for idx, src0 in enumerate((0, 8, 16)):
                    for k in range(8):
                        p.vw(lambda e, idx=idx, src0=src0, k=k: e.tensor_copy(
                            ARNB[0:16, idx * 128 + 16 * k:
                                 idx * 128 + 16 * (k + 1)],
                            pow_slot(src0 + k)),
                            waits=[("V", state["sPowAll"])])
                state["vPowT"] = p.cnt["V"]
                for l in range(8):
                    w = ([("S", ub_mark_s[0]), ("DW", dwConst)]
                         + twait("pE0") if l == 0 else [])
                    p.t(lambda e, l=l: e.matmul(
                        pE[:, 0:128], STK0l(l), UHS[0:16, l::8][:, 0:128],
                        start=(l == 0), stop=(l == 7)), waits=w)
                vXlnT_cur = lr_t1_out["vXlnT"]

        ARNB = ARN[:, :].bitcast(BF16)

        def POWT(i):
            return ARNB[0:16, i * 128:(i + 1) * 128]

        for l in range(8):
            w = ([("S", ub_mark_s[1])] if l == 0 else [])
            p.t(lambda e, l=l: e.matmul(
                pE[:, 128:256], STK0l(l), UHS[0:16, l::8][:, 128:256],
                start=(l == 0), stop=(l == 7)), waits=w)
        tB1 = p.cnt["T"]
        p.s(lambda e: e.copy(B1[:, :], pE[:, 0:256]), waits=[("T", tB1)])
        sB1 = p.cnt["S"]
        guard["pE0"] = ("S", sB1)
        p.t(lambda e: e.matmul(pE[:, 512:768], KER1[:, :], B1[:, :],
                               start=True, stop=False, skip_group_check=True),
            waits=[("V", kmarks[0]), ("S", sB1)] + twait("pE1"))
        tS1 = p.cnt["T"]
        p.s(lambda e: e.copy(S1SB[:, :], pE[:, 512:768]), waits=[("T", tS1)])
        sS1 = p.cnt["S"]
        p.t(lambda e: e.matmul(pC[0:16, 0:256], UNSj(7), S1SB[:, :],
                               start=True, stop=True),
            waits=[("S", sS1)] + twait("pC"))
        tE1 = p.cnt["T"]
        p.s(lambda e: e.copy(B1[0:16, :], pC[0:16, 0:256]), waits=[("T", tE1)])
        sE1 = p.cnt["S"]
        guard["pC"] = ("S", sE1)
        for l in range(8):
            w = ([("S", sE1)] + twait("pD") if l == 0 else [])
            p.t(lambda e, l=l: e.matmul(
                pD[:, 0:32], STK0l(l), B1[0:16, l::8][:, 0:32],
                start=(l == 0), stop=(l == 7)), waits=w)
        tB2 = p.cnt["T"]
        p.s(lambda e: e.copy(B2[:, :], pD[:, 0:32]), waits=[("T", tB2)])
        sB2 = p.cnt["S"]
        guard["pD"] = ("S", sB2)
        p.t(lambda e: e.matmul(pA[:, 0:32], KER2[:, :], B2[:, :],
                               start=True, stop=False, skip_group_check=True),
            waits=[("V", kmarks[1]), ("S", sB2)] + twait("pA"))
        tS2 = p.cnt["T"]
        p.s(lambda e: e.copy(S2SB[:, :], pA[:, 0:32]), waits=[("T", tS2)])
        sS2 = p.cnt["S"]
        p.t(lambda e: e.matmul(pC[0:16, 256:288], UNSj(7), S2SB[:, :],
                               start=True, stop=True),
            waits=[("S", sS2)] + twait("pC"))
        tE2 = p.cnt["T"]
        p.s(lambda e: e.copy(B1[0:16, 0:32], pC[0:16, 256:288]),
            waits=[("T", tE2)])
        sE2 = p.cnt["S"]
        guard["pC"] = ("S", sE2)
        for l in range(8):
            w = ([("S", sE2)] + twait("pD") if l == 0 else [])
            p.t(lambda e, l=l: e.matmul(
                pD[:, 64:68], STK0l(l), B1[0:16, l::8][:, 0:4],
                start=(l == 0), stop=(l == 7)), waits=w)
        tB3 = p.cnt["T"]
        p.s(lambda e: e.copy(B3[:, 0:4], pD[:, 64:68]), waits=[("T", tB3)])
        sB3 = p.cnt["S"]
        guard["pD"] = ("S", sB3)
        p.t(lambda e: e.matmul(pB[:, 0:4], KER3[:, :], B3[:, 0:4],
                               start=True, stop=False, skip_group_check=True),
            waits=[("V", kmarks[2]), ("S", sB3)] + twait("pB"))
        tS3 = p.cnt["T"]
        p.s(lambda e: e.copy(S3SB[:, 0:4], pB[:, 0:4]), waits=[("T", tS3)])
        sS3 = p.cnt["S"]
        p.t(lambda e: e.matmul(pC[0:16, 288:292], UNSj(7), S3SB[:, 0:4],
                               start=True, stop=True),
            waits=[("S", sS3)] + twait("pC"))
        tE3 = p.cnt["T"]
        p.s(lambda e: e.copy(E3[0:16, 0:4], pC[0:16, 288:292]),
            waits=[("T", tE3)])
        sE3 = p.cnt["S"]
        guard["pC"] = ("S", sE3)
        p.v(lambda e: e.tensor_copy(GC[0:16, 0:1], E3[0:16, 0:1]),
            waits=[("S", sE3)])
        vG = p.cnt["V"]
        for c in range(1, 4):
            p.t(lambda e, c=c: e.matmul(
                pC[0:16, 296 + c:297 + c], pow_slot(23), GC[0:16, c - 1:c],
                start=True, stop=True),
                waits=[("V", vG), ("V", state["sPowAll"])])
            tGc = p.cnt["T"]
            p.v(lambda e, c=c: e.tensor_tensor(
                out=GC[0:16, c:c + 1], in0=pC[0:16, 296 + c:297 + c],
                in1=E3[0:16, c:c + 1], op=ALU.add), waits=[("T", tGc)])
            vG = p.cnt["V"]
        guard["pC"] = ("V", vG)
        p.vw(lambda e: e.tensor_copy(CIN3[0:16, 1:4], GC[0:16, 0:3]),
             waits=[("V", vG), ("V", vInitCin)])
        vCin3 = p.cnt["V"]
        p.t(lambda e: e.matmul(pB[:, 0:4], POWT(2), CIN3[0:16, 0:4],
                               start=False, stop=True, skip_group_check=True),
            waits=[("V", vCin3), ("V", state["vPowT"])])
        tC3 = p.cnt["T"]
        p.s(lambda e: e.copy(S3SB[:, 0:4], pB[:, 0:4]), waits=[("T", tC3)])
        sS3t = p.cnt["S"]
        guard["pB"] = ("S", sS3t)
        for l3 in range(8):
            w = ([("S", sS3t)] + twait("pC") if l3 == 0 else [])
            p.t(lambda e, l3=l3: e.matmul(
                pC[0:16, 304 + 4 * l3:304 + 4 * (l3 + 1)], UNSj(l3),
                S3SB[:, 0:4], start=True, stop=True), waits=w)
        tCin2 = p.cnt["T"]
        for l3 in range(8):
            if l3 < 7:
                p.s(lambda e, l3=l3: e.copy(
                    CIN2[0:16, (l3 + 1)::8][:, 0:4],
                    pC[0:16, 304 + 4 * l3:304 + 4 * (l3 + 1)]),
                    waits=[("T", tCin2)] if l3 == 0 else [])
            else:
                p.s(lambda e: e.copy(CIN2[0:16, 8::8][:, 0:3],
                                     pC[0:16, 332:335]))
        sCin2 = p.cnt["S"]
        guard["pC"] = ("S", sCin2)
        p.t(lambda e: e.matmul(pA[:, 0:32], POWT(1), CIN2[0:16, :],
                               start=False, stop=True, skip_group_check=True),
            waits=[("S", sCin2)])
        tC2 = p.cnt["T"]
        p.s(lambda e: e.copy(S2SB[:, :], pA[:, 0:32]), waits=[("T", tC2)])
        sS2t = p.cnt["S"]
        guard["pA"] = ("S", sS2t)
        for l2 in range(8):
            w = ([("S", sS2t)] + twait("pC") if l2 == 0 else [])
            p.t(lambda e, l2=l2: e.matmul(
                pC[0:16, 32 * l2:32 * (l2 + 1)], UNSj(l2), S2SB[:, :],
                start=True, stop=True), waits=w)
        tCin1 = p.cnt["T"]
        for l2 in range(8):
            if l2 < 7:
                p.s(lambda e, l2=l2: e.copy(
                    CIN1[0:16, (l2 + 1)::8][:, 0:32],
                    pC[0:16, 32 * l2:32 * (l2 + 1)]),
                    waits=[("T", tCin1)] if l2 == 0 else [])
            else:
                p.s(lambda e: e.copy(CIN1[0:16, 8::8][:, 0:31],
                                     pC[0:16, 224:255]))
        sCin1 = p.cnt["S"]
        guard["pC"] = ("S", sCin1)
        p.t(lambda e: e.matmul(pE[:, 512:768], POWT(0), CIN1[0:16, :],
                               start=False, stop=True, skip_group_check=True),
            waits=[("S", sCin1)])
        tC1 = p.cnt["T"]
        p.s(lambda e: e.copy(S1SB[:, :], pE[:, 512:768]), waits=[("T", tC1)])
        sHst = p.cnt["S"]
        guard["pE1"] = ("S", sHst)
        for j in range(8):
            tile, tn = (pC, "pC") if j % 2 == 0 else (pD, "pD")
            p.t(lambda e, j=j, tile=tile: e.matmul(
                tile[0:16, 256:512], UNSj(j), S1SB[:, :],
                start=True, stop=True), waits=[("S", sHst)] + twait(tn))
            tU = p.cnt["T"]
            p.s(lambda e, j=j, tile=tile: e.copy(
                UHS[0:16, j::8][:, 0:256], tile[0:16, 256:512]),
                waits=[("T", tU)])
            guard[tn] = ("S", p.cnt["S"])
        sHs = p.cnt["S"]

        pool2 = [(pA, "pA"), (pB, "pB"), (pD, "pD")]
        pj = [0]

        def nt2():
            t_ = pool2[pj[0] % 3]
            pj[0] += 1
            return t_

        lr_next = None
        lr_next_out = None
        for gt in range(4):
            ypar = gt % 2
            if lr_next_spec is not None and gt == 2:
                lr_next["A"]()
            for oc in range(8):
                tile, tn = nt2()
                w = ([("S", sHs), ("DW", dwConst)] + twait(tn) +
                     twait("Y%d" % ypar))
                p.t(lambda e, oc=oc, gt=gt, tile=tile: e.matmul(
                    tile[:, :], CMT[0:16, oc * 128:(oc + 1) * 128],
                    UHS[0:16, gt * 512:(gt + 1) * 512], start=True, stop=True),
                    waits=w)
                tY = p.cnt["T"]
                p.s(lambda e, oc=oc, ypar=ypar, tile=tile: e.copy(
                    HID[:, 8 + oc, ypar * 512:(ypar + 1) * 512], tile[:, :]),
                    waits=[("T", tY)])
                guard[tn] = ("S", p.cnt["S"])
            sY = p.cnt["S"]
            for oc in range(8):
                tile, tn = nt2()
                for hc in range(NHC):
                    w = []
                    if hc == 0:
                        w = twait(tn) + [("S", sY),
                                         ("DW", state["wout_ready"])]
                    p.t(lambda e, oc=oc, hc=hc, ypar=ypar, tile=tile: e.matmul(
                        tile[:, :], W2GB[:, 8 + hc, oc * 128:(oc + 1) * 128],
                        HID[:, 8 + hc, ypar * 512:(ypar + 1) * 512],
                        start=(hc == 0), stop=(hc == NHC - 1)), waits=w)
                tO = p.cnt["T"]
                p.v(lambda e, oc=oc, gt=gt, tile=tile: e.scalar_tensor_tensor(
                    out=X[:, oc, gt * 512:(gt + 1) * 512], in0=tile[:, :],
                    scalar=CF[:, CF_BOUT + oc:CF_BOUT + oc + 1],
                    in1=X[:, oc, gt * 512:(gt + 1) * 512],
                    op0=ALU.add, op1=ALU.add), waits=[("T", tO)])
                guard[tn] = ("V", p.cnt["V"])
                if lr_next is not None and gt == 2:
                    lr_next["Bh"](oc)
                    if oc == 7:
                        lr_next["Bt"]()

            guard["Y%d" % ypar] = ("T", p.cnt["T"])
            if gt == 1 and lr_next_spec is not None:
                vX01 = p.cnt["V"]
                tb2, do_rout2 = lr_next_spec
                lr_next, lr_next_out = make_lr(
                    tb2, [("V", vX01)], do_rout2, marks["t_xln_free"])
                lr_next["A0"]()
        if lr_next is not None:
            lr_next["C"]()
            marks["lr_d"] = lr_next["D"]
        marks["vX"] = p.cnt["V"]
        marks["tEnd"] = p.cnt["T"]
        return marks, lr_next_out

    lr0, lr0_out = make_lr(0, [("DS", dsX0)], True, 0)
    lr0["A0"]()
    lr0["A"]()
    for i_ in range(8):
        lr0["Bh"](i_)
    lr0["Bt"]()

    m_f1t0, lr1_out = ffn_body(0, lr0_out,
                               (1, [("DS", dsX1)], True),
                               pending_head=lr0["C"], pending_d2=lr0["D"])
    load_into_w2gb(d_ws2, 0, m_f1t0["tEnd"])
    load_w1(0, 0, m_f1t0["tEnd"])
    load_w1(1, 1, m_f1t0["tEnd"])

    m_f1t1, lra0_out = ffn_body(1, lr1_out,
                                (0, [("V", m_f1t0["vX"])], False),
                                pending_d=m_f1t0["lr_d"])
    p.g(lambda eng: eng.dma_start(
        out=W1B[:, :, :, :].rearrange("p a b c -> p (a b c)"), in_=d_win),
        waits=[("T", m_f1t1["t_xln_free"])])
    state["win_ready"] = p.cnt["DW"]
    state["wgate_ready"] = load_into_w2gb(d_wgate, 0, m_f1t1["tEnd"])
    state["wout_ready"] = load_into_w2gb(d_wout, 1, m_f1t1["tEnd"])

    m_attn, lrf2_out = attn_body(lra0_out,
                                 [("V", m_f1t1["vX"])], (0, True),
                                 pending_d=m_f1t1["lr_d"])
    load_into_w2gb(d_ws2, 0, m_attn["t_xln_free"])
    load_w1(0, 0, m_attn["t_xln_free"])
    load_w1(1, 1, m_attn["t_xln_free"])

    m_f2t0, lrf2t1_out = ffn_body(0, lrf2_out,
                                  (1, [("V", m_attn["vX"])], True),
                                  pending_d=m_attn["lr_d"])
    load_into_w2gb(d_ws2, 0, m_f2t0["tEnd"])
    load_w1(0, 0, m_f2t0["tEnd"])
    load_w1(1, 1, m_f2t0["tEnd"])

    ov = d_out.rearrange("(c p) t -> p c t", p=128)
    p.y(lambda e: e.dma_start(out=ov[:, :, 0:TB], in_=X[:, :, 0:TB]),
        waits=[("V", m_f2t0["vX"])])

    m_f2t1, _ = ffn_body(1, lrf2t1_out, None,
                         pending_d=m_f2t0["lr_d"])

    p.y(lambda e: e.dma_start(out=ov[:, :, TB:SEQ], in_=X[:, :, TB:SEQ]),
        waits=[("V", m_f2t1["vX"])])
    dsOut = p.cnt["DS"]
    p.ywait([("DS", dsOut)])

    with nc.Block() as block:
        p.emit(block, sems)
    ctx.close()
    return nc


def _host_prepare(inp):
    f32 = np.float32
    x = np.ascontiguousarray(inp["L0_attn_x"], dtype=f32)
    w1 = np.ascontiguousarray(inp["L0_ffn_W1"], dtype=f32)
    w2 = np.ascontiguousarray(inp["L0_ffn_W2"], dtype=f32)
    ws1 = np.ascontiguousarray(inp["L0_ffn_Ws1"], dtype=f32)
    ws2 = np.ascontiguousarray(inp["L0_ffn_Ws2"], dtype=f32)
    wg = np.ascontiguousarray(inp["L0_ffn_Wg"], dtype=f32)
    bias_corr = np.ascontiguousarray(inp["L0_ffn_bias_corr"], dtype=f32)
    win = np.ascontiguousarray(inp["L0_attn_W_in"], dtype=f32)
    b_in = np.ascontiguousarray(inp["L0_attn_b_in"], dtype=f32)
    a = np.ascontiguousarray(inp["L0_attn_A"], dtype=f32)
    bm = np.ascontiguousarray(inp["L0_attn_Bm"], dtype=f32)
    cm = np.ascontiguousarray(inp["L0_attn_Cm"], dtype=f32)
    wgate = np.ascontiguousarray(inp["L0_attn_W_gate"], dtype=f32)
    b_gate = np.ascontiguousarray(inp["L0_attn_b_gate"], dtype=f32)
    wout = np.ascontiguousarray(inp["L0_attn_W_out"], dtype=f32)
    b_out = np.ascontiguousarray(inp["L0_attn_b_out"], dtype=f32)

    def fm_chunks(wT):
        inf, outf = wT.shape
        return np.ascontiguousarray(
            wT.reshape(inf // 128, 128, outf).transpose(1, 0, 2).reshape(128, -1))

    w1t = np.stack([fm_chunks(w1[e].T) for e in range(E)])
    w2g = np.zeros((NG, 128, 16 * DIM), f32)
    for g in range(NG):
        for ei in range(EG):
            w2g[g][:, ei * 4 * DIM:(ei + 1) * 4 * DIM] = fm_chunks(
                w2[g * EG + ei].T)
    ws1t = fm_chunks(ws1.T)
    ws2t = fm_chunks(ws2.T)
    wint = fm_chunks(win[DIM:2 * DIM].T)
    wgatet = fm_chunks(wgate.T)
    woutt = fm_chunks(wout.T)
    wgt = fm_chunks(wg.T)
    bmt = fm_chunks(bm.T)
    cmt = np.ascontiguousarray(cm.T)

    cb = np.zeros((128, CBW), f32)
    cb[:, OFF_ONES:OFF_ONES + 128] = 1.0
    cb[:, OFF_ID:OFF_ID + 128] = np.eye(128)
    for j in range(8):
        blk = np.zeros((128, 16), f32)
        for m in range(16):
            blk[16 * j + m, m] = 1.0
        cb[:, OFF_UNS + 16 * j:OFF_UNS + 16 * (j + 1)] = blk
    for e in range(E):
        blk = np.zeros((16, 128), f32)
        blk[e, :] = 1.0
        cb[0:16, OFF_SELB + 128 * e:OFF_SELB + 128 * (e + 1)] = blk
    for l in range(8):
        blk = np.zeros((16, 128), f32)
        for n in range(16):
            blk[n, 16 * l + n] = 1.0
        cb[0:16, OFF_STK0 + 128 * l:OFF_STK0 + 128 * (l + 1)] = blk
    cb[0:16, OFF_A:OFF_A + 16] = a
    cb[0:16, OFF_AT:OFF_AT + 16] = a.T

    cf = np.zeros((128, CFW), f32)
    cf[:, CF_BCB:CF_BCB + 16] = np.broadcast_to(bias_corr, (128, 16))
    cf[:, CF_BIN:CF_BIN + 8] = b_in[DIM:2 * DIM].reshape(8, 128).T
    cf[:, CF_BGATE:CF_BGATE + 8] = b_gate.reshape(8, 128).T
    cf[:, CF_BOUT:CF_BOUT + 8] = b_out.reshape(8, 128).T
    cf[:, CF_EPS] = DIM * LNEPS
    cf[:, CF_LNC] = np.log(np.sqrt(float(DIM)))

    common = {
        "w1t": w1t, "w2g": w2g, "ws1t": ws1t, "ws2t": ws2t,
        "wint": wint, "wgatet": wgatet, "woutt": woutt,
        "wgt": wgt, "bmt": bmt, "cmt": cmt, "cb": cb, "cf": cf,
    }
    in_maps = []
    for b in range(NCORES):
        m = dict(common)
        m["xt"] = np.ascontiguousarray(x[b].T)
        in_maps.append(m)
    return in_maps


def kernel(**inputs):
    in_maps = _host_prepare(inputs)
    nc = build_program()
    res = run_bass_kernel_spmd(nc, in_maps, core_ids=list(range(NCORES)))
    out = np.empty((NCORES, SEQ, DIM), np.float32)
    for b in range(NCORES):
        out[b] = np.asarray(res.results[b]["yt"], np.float32).T
    return out


# version 13
# speedup vs baseline: 1.0109x; 1.0012x over previous
import numpy as np
import ml_dtypes

import concourse.bass as bass
import concourse.mybir as mybir
from concourse.bass_utils import run_bass_kernel_spmd

F32 = mybir.dt.float32
BF16 = mybir.dt.bfloat16
AF = mybir.ActivationFunctionType
ALU = mybir.AluOpType
AX = mybir.AxisListType

NCORES = 8
DIM = 1024
SEQ = 2048
NHC = 8
NTB = 2
TB = 1024
E = 16
EG = 4
NG = 4
I_EXP = 512
LNEPS = 1e-5

OFF_ONES = 0
OFF_ID = 128
OFF_UNS = 256
OFF_SELB = 384
OFF_STK0 = OFF_SELB + 2048
OFF_A = OFF_STK0 + 1024
OFF_AT = OFF_A + 16
CBW = OFF_AT + 16

CF_BCB = 0
CF_BIN = 16
CF_BGATE = 24
CF_BOUT = 32
CF_EPS = 40
CF_LNC = 41
CFW = 48

STAGES = "full"
DBG_ROUT = False


class Prog:
    ENGS = ("t", "v", "s", "g", "y")

    def __init__(self, nc):
        self.nc = nc
        self.streams = {e: [] for e in self.ENGS}
        self.cnt = {"T": 0, "V": 0, "S": 0, "DW": 0, "DS": 0}

    def _op(self, eng, fn, waits, inc, n):
        for s, v in waits:
            assert v <= self.cnt[s], f"wait {s}>={v} emitted={self.cnt[s]}"
        if inc is not None:
            self.cnt[inc] += n
        self.streams[eng].append((tuple(waits), fn, inc, n))
        return self.cnt[inc] if inc is not None else None

    def t(self, fn, waits=()):
        return self._op("t", fn, waits, "T", 1)

    def v(self, fn, waits=()):
        return self._op("v", fn, waits, "V", 1)

    def vw(self, fn, waits=()):
        w = list(waits) + [("V", self.cnt["V"])]
        return self._op("v", fn, w, "V", 1)

    def s(self, fn, waits=()):
        return self._op("s", fn, waits, "S", 1)

    def g(self, fn, waits=()):
        return self._op("g", fn, waits, "DW", 16)

    def y(self, fn, waits=()):
        return self._op("y", fn, waits, "DS", 16)

    def ywait(self, waits):
        self.streams["y"].append((tuple(waits), None, None, 0))

    def emit(self, block, sems):
        def mk(eng_key, deco):
            stream = self.streams[eng_key]

            def body(eng):
                seen = {k: 0 for k in self.cnt}
                for waits, fn, inc, n in stream:
                    for s, v in waits:
                        if v > seen[s]:
                            eng.wait_ge(sems[s], v)
                            seen[s] = v
                    if fn is None:
                        continue
                    inst = fn(eng)
                    if inc is not None:
                        inst.then_inc(sems[inc], n)
            deco(body)

        mk("t", block.tensor)
        mk("v", block.vector)
        mk("s", block.scalar)
        mk("g", block.gpsimd)
        mk("y", block.sync)


def build_program():
    nc = bass.Bass("TRN2", target_bir_lowering=False)

    d_x = nc.dram_tensor("xt", [DIM, SEQ], F32, kind="ExternalInput").ap()
    d_w1 = nc.dram_tensor("w1t", [E, 128, 8 * I_EXP], F32, kind="ExternalInput").ap()
    d_w2g = nc.dram_tensor("w2g", [NG, 128, 16 * DIM], F32, kind="ExternalInput").ap()
    d_ws1 = nc.dram_tensor("ws1t", [128, 8 * DIM], F32, kind="ExternalInput").ap()
    d_ws2 = nc.dram_tensor("ws2t", [128, 8 * DIM], F32, kind="ExternalInput").ap()
    d_win = nc.dram_tensor("wint", [128, 8 * DIM], F32, kind="ExternalInput").ap()
    d_wgate = nc.dram_tensor("wgatet", [128, 8 * DIM], F32, kind="ExternalInput").ap()
    d_wout = nc.dram_tensor("woutt", [128, 8 * DIM], F32, kind="ExternalInput").ap()
    d_wg = nc.dram_tensor("wgt", [128, 8 * 16], F32, kind="ExternalInput").ap()
    d_bm = nc.dram_tensor("bmt", [128, 8 * 16], F32, kind="ExternalInput").ap()
    d_cm = nc.dram_tensor("cmt", [16, DIM], F32, kind="ExternalInput").ap()
    d_cb = nc.dram_tensor("cb", [128, CBW], F32, kind="ExternalInput").ap()
    d_cf = nc.dram_tensor("cf", [128, CFW], F32, kind="ExternalInput").ap()
    d_out = nc.dram_tensor("yt", [DIM, SEQ], F32, kind="ExternalOutput").ap()

    from contextlib import ExitStack
    ctx = ExitStack()
    Eb = ctx.enter_context

    X = Eb(nc.sbuf_tensor("X", [128, NHC, SEQ], F32))
    XLN = Eb(nc.sbuf_tensor("XLN", [128, NHC, TB], BF16))
    HID = Eb(nc.sbuf_tensor("HID", [128, 16, TB], BF16))
    W1B = Eb(nc.sbuf_tensor("W1B", [128, 2, 8, I_EXP], BF16))
    W2GB = Eb(nc.sbuf_tensor("W2GB", [128, 16, DIM], BF16))
    WS1B = Eb(nc.sbuf_tensor("WS1B", [128, 8, DIM], BF16))
    CB = Eb(nc.sbuf_tensor("CB", [128, CBW], BF16))
    CF = Eb(nc.sbuf_tensor("CF", [128, CFW], F32))
    WGT = Eb(nc.sbuf_tensor("WGT", [128, 8, 16], BF16))
    BMT = Eb(nc.sbuf_tensor("BMT", [128, 8, 16], BF16))
    CMT = Eb(nc.sbuf_tensor("CMT", [16, DIM], BF16))
    MB = Eb(nc.sbuf_tensor("MB", [128, TB], BF16))
    RB = Eb(nc.sbuf_tensor("RB", [128, TB], BF16))
    SQS = Eb(nc.sbuf_tensor("SQS", [128, TB], BF16))
    CWB = Eb(nc.sbuf_tensor("CWB", [128, 2, 512], BF16))
    CWT = Eb(nc.sbuf_tensor("CWT", [16, TB], BF16))
    GSC = Eb(nc.sbuf_tensor("GSC", [128, 2, 512], BF16))
    ARN = Eb(nc.sbuf_tensor("ARN", [128, 592], F32))
    CW16 = Eb(nc.sbuf_tensor("CW16", [128, 128], BF16))
    POW = Eb(nc.sbuf_tensor("POW", [16, 26 * 16], BF16))
    KER1 = Eb(nc.sbuf_tensor("KER1", [128, 128], BF16))
    KER2 = Eb(nc.sbuf_tensor("KER2", [128, 128], BF16))
    KER3 = Eb(nc.sbuf_tensor("KER3", [128, 128], BF16))
    B1 = Eb(nc.sbuf_tensor("B1", [128, 256], BF16))
    S1SB = Eb(nc.sbuf_tensor("S1SB", [128, 256], BF16))
    B2 = Eb(nc.sbuf_tensor("B2", [128, 32], BF16))
    S2SB = Eb(nc.sbuf_tensor("S2SB", [128, 32], BF16))
    B3 = Eb(nc.sbuf_tensor("B3", [128, 8], BF16))
    S3SB = Eb(nc.sbuf_tensor("S3SB", [128, 8], BF16))
    E3 = Eb(nc.sbuf_tensor("E3", [16, 8], BF16))
    GC = Eb(nc.sbuf_tensor("GC", [16, 8], BF16))
    CIN3 = Eb(nc.sbuf_tensor("CIN3", [16, 8], BF16))
    CIN2 = Eb(nc.sbuf_tensor("CIN2", [16, 32], BF16))
    CIN1 = Eb(nc.sbuf_tensor("CIN1", [16, 256], BF16))
    UHS = Eb(nc.sbuf_tensor("UHS", [16, SEQ], BF16))

    pA = Eb(nc.psum_tensor("pA", [128, 512], F32))
    pB = Eb(nc.psum_tensor("pB", [128, 512], F32))
    pC = Eb(nc.psum_tensor("pC", [128, 512], F32))
    pD = Eb(nc.psum_tensor("pD", [128, 512], F32))
    pE = Eb(nc.psum_tensor("pE", [128, 1024], F32))
    pT = Eb(nc.psum_tensor("pT", [16, 1024], BF16))

    sem_t = Eb(nc.semaphore("semT"))
    sem_v = Eb(nc.semaphore("semV"))
    sem_s = Eb(nc.semaphore("semS"))
    sem_dw = Eb(nc.semaphore("semDW"))
    sem_ds = Eb(nc.semaphore("semDS"))
    sems = {"T": sem_t, "V": sem_v, "S": sem_s, "DW": sem_dw, "DS": sem_ds}

    p = Prog(nc)
    guard = {}

    def twait(*keys):
        return [guard[k] for k in keys if k in guard]

    ONES = CB[:, OFF_ONES:OFF_ONES + 128]
    IDENT = CB[:, OFF_ID:OFF_ID + 128]
    ID16 = CB[0:16, OFF_ID:OFF_ID + 16]
    A_ = CB[0:16, OFF_A:OFF_A + 16]
    AT_ = CB[0:16, OFF_AT:OFF_AT + 16]

    def UNSj(j):
        return CB[:, OFF_UNS + 16 * j: OFF_UNS + 16 * (j + 1)]

    def SELBe(e):
        return CB[0:16, OFF_SELB + 128 * e: OFF_SELB + 128 * (e + 1)]

    def STK0l(l):
        return CB[0:16, OFF_STK0 + 128 * l: OFF_STK0 + 128 * (l + 1)]

    def pow_slot(i):
        return POW[0:16, 16 * i:16 * (i + 1)]

    xv = d_x.rearrange("(c p) t -> p c t", p=128)
    p.y(lambda e: e.dma_start(out=X[:, :, 0:TB], in_=xv[:, :, 0:TB]))
    dsX0 = p.cnt["DS"]
    p.y(lambda e: e.dma_start(out=X[:, :, TB:SEQ], in_=xv[:, :, TB:SEQ]))
    dsX1 = p.cnt["DS"]
    p.y(lambda e: e.dma_start(out=CF[:, :], in_=d_cf))
    dsConst = p.cnt["DS"]
    p.g(lambda e: e.dma_start(out=CB[:, :], in_=d_cb))
    p.g(lambda e: e.dma_start(out=WGT[:, :, :].rearrange("p a b -> p (a b)"),
                              in_=d_wg))
    p.g(lambda e: e.dma_start(out=BMT[:, :, :].rearrange("p a b -> p (a b)"),
                              in_=d_bm))
    p.g(lambda e: e.dma_start(out=CMT[:, :], in_=d_cm))
    dwConst = p.cnt["DW"]

    w1_ready = [0, 0]
    state = {"w2g_ready": 0, "ws1_ready": 0, "wgate_ready": 0,
             "wout_ready": 0, "win_ready": 0, "sPowAll": 0, "vPowT": 0}
    kmarks = []

    def load_w1(e_, slot, wait_t):
        p.g(lambda eng: eng.dma_start(
            out=W1B[:, slot, :, :].rearrange("p a b -> p (a b)"), in_=d_w1[e_]),
            waits=[("T", wait_t)] if wait_t else [])
        w1_ready[slot] = p.cnt["DW"]

    def load_w2g(gidx, wait_t):
        p.g(lambda eng: eng.dma_start(
            out=W2GB[:, 0:8, :].rearrange("p a b -> p (a b)"),
            in_=d_w2g[gidx][:, 0:8 * DIM]),
            waits=[("T", wait_t)] if wait_t else [])
        p.g(lambda eng: eng.dma_start(
            out=W2GB[:, 8:16, :].rearrange("p a b -> p (a b)"),
            in_=d_w2g[gidx][:, 8 * DIM:16 * DIM]))
        state["w2g_ready"] = p.cnt["DW"]

    def load_into_w2gb(dram, half, wait_t):
        lo = 8 * half
        p.g(lambda eng: eng.dma_start(
            out=W2GB[:, lo:lo + 8, :].rearrange("p a b -> p (a b)"), in_=dram),
            waits=[("T", wait_t)] if wait_t else [])
        state["w2g_ready"] = p.cnt["DW"]
        return p.cnt["DW"]

    def load_ws1(dram, wait_t):
        p.g(lambda eng: eng.dma_start(
            out=WS1B[:, 0:4, :].rearrange("p a b -> p (a b)"),
            in_=dram[:, 0:4 * DIM]),
            waits=[("T", wait_t)] if wait_t else [])
        state["ws1_half"] = p.cnt["DW"]
        p.g(lambda eng: eng.dma_start(
            out=WS1B[:, 4:8, :].rearrange("p a b -> p (a b)"),
            in_=dram[:, 4 * DIM:8 * DIM]))
        state["ws1_ready"] = p.cnt["DW"]

    load_ws1(d_ws1, 0)
    load_into_w2gb(d_ws2, 0, 0)
    load_w1(0, 0, 0)
    load_w1(1, 1, 0)

    p.v(lambda e: e.memset(CIN3[:, :], 0.0))
    p.v(lambda e: e.memset(CIN2[:, :], 0.0))
    p.v(lambda e: e.memset(CIN1[:, :], 0.0))
    vInitCin = p.cnt["V"]

    cwb_free = [0, 0]
    gsc_free = [0, 0]
    dbg_i = [0]

    def make_lr(tb, waitX, do_routing, xln_free):
        out = {}
        st_ = {}

        def A0():
            p.s(lambda e: e.copy(XLN[:, :, :],
                                 X[:, :, tb * TB:(tb + 1) * TB]),
                waits=list(waitX) + ([("T", xln_free)] if xln_free else []))
            st_["sCopy"] = p.cnt["S"]

        def A():
            for tt in range(2):
                for hc in range(NHC):
                    w = ([("S", st_["sCopy"]), ("DW", dwConst)]
                         + twait("pE0", "pE1") if tt == 0 and hc == 0 else [])
                    p.t(lambda e, tt=tt, hc=hc: e.matmul(
                        pE[:, tt * 512:(tt + 1) * 512], ONES,
                        XLN[:, hc, tt * 512:(tt + 1) * 512],
                        start=(hc == 0), stop=(hc == NHC - 1)), waits=w)
            tSum = p.cnt["T"]
            p.s(lambda e: e.activation(MB[:, :], pE[:, :], AF.Copy,
                                       scale=1.0 / DIM), waits=[("T", tSum)])
            st_["sMb"] = p.cnt["S"]
            guard["pE0"] = ("S", st_["sMb"])
            guard["pE1"] = ("S", st_["sMb"])
            p.v(lambda e: e.tensor_tensor(
                out=RB[:, :], in0=MB[:, :], in1=MB[:, :], op=ALU.mult),
                waits=[("S", st_["sMb"])])
            st_["vMsq"] = p.cnt["V"]

        def B_hc(hc):
            w = [("S", st_["sCopy"])]
            if st_.get("pair_mark"):
                w.append(("T", st_["pair_mark"]))
            p.s(lambda e, hc=hc: e.activation(
                SQS[:, :], XLN[:, hc, :], AF.Square), waits=w)
            sSq = p.cnt["S"]
            for tt in range(2):
                w2 = [("S", sSq)]
                if hc == 0 and tt == 0:
                    w2 += [("V", st_["vMsq"])]
                p.t(lambda e, tt=tt, hc=hc: e.matmul(
                    pE[:, tt * 512:(tt + 1) * 512], ONES,
                    SQS[:, tt * 512:(tt + 1) * 512],
                    start=(hc == 0), stop=(hc == NHC - 1)), waits=w2)
            st_["pair_mark"] = p.cnt["T"]

        def Btail():
            tSum2 = p.cnt["T"]
            st_["tSum2"] = tSum2
            p.vw(lambda e: e.scalar_tensor_tensor(
                out=RB[:, :], in0=RB[:, :], scalar=-float(DIM), in1=pE[:, :],
                op0=ALU.mult, op1=ALU.add), waits=[("T", st_["tSum2"])])
            vVar = p.cnt["V"]
            guard["pE0"] = ("V", vVar)
            guard["pE1"] = ("V", vVar)
            p.s(lambda e: e.activation(RB[:, :], RB[:, :], AF.Ln,
                                       bias=CF[:, CF_EPS:CF_EPS + 1],
                                       scale=1.0),
                waits=[("V", vVar), ("DS", dsConst)])
            p.s(lambda e: e.activation(RB[:, :], RB[:, :], AF.Exp,
                                       bias=CF[:, CF_LNC:CF_LNC + 1],
                                       scale=-0.5))
            sR = p.cnt["S"]
            out["vXlnT"] = [0, 0]
            for tt in range(2):
                sl = slice(tt * 512, (tt + 1) * 512)
                p.vw(lambda e, sl=sl: e.tensor_tensor(
                    out=XLN[:, :, sl], in0=XLN[:, :, sl],
                    in1=MB[:, sl].rearrange("p (a t) -> p a t", a=1)
                        .to_broadcast([128, 8, 512]), op=ALU.subtract),
                    waits=[("S", st_["sMb"]), ("T", st_["tSum2"])])
                p.vw(lambda e, sl=sl: e.tensor_tensor(
                    out=XLN[:, :, sl], in0=XLN[:, :, sl],
                    in1=RB[:, sl].rearrange("p (a t) -> p a t", a=1)
                        .to_broadcast([128, 8, 512]), op=ALU.mult),
                    waits=[("S", sR)])
                out["vXlnT"][tt] = p.cnt["V"]
            out["vXln"] = p.cnt["V"]

        def C():
            if do_routing:
                routing_emit(out["vXln"], st_, out.get("vXlnT"))

        def D():
            if not do_routing:
                out["sCwt"] = 0
                return
            vCw = st_["vCw"]
            for tc in range(8):
                w = [("V", vCw)] + (twait("pT") if tc == 0 else [])
                p.t(lambda e, tc=tc: e.transpose(
                    pT[0:16, tc * 128:(tc + 1) * 128],
                    CW16[:, tc * 16:(tc + 1) * 16], IDENT), waits=w)
            tTr = p.cnt["T"]
            p.s(lambda e: e.copy(CWT[0:16, :], pT[0:16, :]),
                waits=[("T", tTr)] + twait("CWT"))
            out["sCwt"] = p.cnt["S"]
            guard["pT"] = ("S", out["sCwt"])

        stages = {"A0": A0, "A": A, "Bh": B_hc, "Bt": Btail,
                  "C": C, "D": D}
        return stages, out

    def routing_emit(vXln, st_, vXlnT=None):
        for tc in range(8):
            for hc in range(NHC):
                w = []
                if tc == 0 and hc == 0:
                    w = [("V", vXlnT[0] if vXlnT else vXln),
                         ("DW", dwConst)] + twait("pC")
                elif tc == 4 and hc == 0 and vXlnT:
                    w = [("V", vXlnT[1])]
                p.t(lambda e, tc=tc, hc=hc: e.matmul(
                    pC[:, tc * 16:(tc + 1) * 16],
                    XLN[:, hc, tc * 128:(tc + 1) * 128], WGT[:, hc, :],
                    start=(hc == 0), stop=(hc == NHC - 1)), waits=w)
        tGate = p.cnt["T"]
        SCO = ARN[:, 0:128]
        SCH = ARN[:, 128:256]
        MSK = ARN[:, 256:384]
        WW = ARN[:, 384:512]
        GRP = ARN[:, 512:544]
        G2 = ARN[:, 544:576]
        TH = ARN[:, 576:584]
        RCP = ARN[:, 584:592]
        p.s(lambda e: e.activation(SCO, pC[:, 0:128], AF.Sigmoid),
            waits=[("T", tGate)])
        sSco = p.cnt["S"]
        guard["pC"] = ("S", sSco)

        def v3(ap, a, b):
            return ap.rearrange("p (a b) -> p a b", b=b)

        def dump(ap, width=128):
            if not DBG_ROUT:
                return
            i = dbg_i[0]
            dbg_i[0] += 1
            row = 6 + i // 16
            col = (i % 16) * 128
            p.vw(lambda e, ap=ap, row=row, col=col, width=width: e.tensor_copy(
                X[:, row, col:col + width], ap))

        p.vw(lambda e: e.tensor_tensor(
            out=v3(SCH, 8, 16), in0=v3(SCO, 8, 16),
            in1=CF[:, CF_BCB:CF_BCB + 16].rearrange("p (a b) -> p a b", a=1)
                .to_broadcast([128, 8, 16]), op=ALU.add),
            waits=[("S", sSco), ("DS", dsConst)])
        dump(SCH)
        p.vw(lambda e: e.tensor_reduce(
            out=GRP, in_=v3(SCH, 32, 4), axis=AX.X, op=ALU.max))
        dump(GRP, 32)
        p.vw(lambda e: e.tensor_tensor(
            out=v3(MSK, 32, 4), in0=v3(SCH, 32, 4),
            in1=v3(GRP, 32, 1).to_broadcast([128, 32, 4]), op=ALU.is_equal))
        p.vw(lambda e: e.scalar_tensor_tensor(
            out=WW, in0=MSK, scalar=-1e30, in1=SCH, op0=ALU.mult, op1=ALU.add))
        p.vw(lambda e: e.tensor_reduce(
            out=G2, in_=v3(WW, 32, 4), axis=AX.X, op=ALU.max))
        p.vw(lambda e: e.tensor_tensor(out=GRP, in0=GRP, in1=G2, op=ALU.add))
        p.vw(lambda e: e.tensor_reduce(
            out=TH, in_=v3(GRP, 8, 4), axis=AX.X, op=ALU.max))
        p.vw(lambda e: e.tensor_tensor(
            out=v3(G2, 8, 4), in0=v3(GRP, 8, 4),
            in1=v3(TH, 8, 1).to_broadcast([128, 8, 4]), op=ALU.is_equal))
        p.vw(lambda e: e.scalar_tensor_tensor(
            out=G2, in0=G2, scalar=-1e30, in1=GRP, op0=ALU.mult, op1=ALU.add))
        p.vw(lambda e: e.tensor_reduce(
            out=TH, in_=v3(G2, 8, 4), axis=AX.X, op=ALU.max))
        p.vw(lambda e: e.tensor_tensor(
            out=v3(G2, 8, 4), in0=v3(GRP, 8, 4),
            in1=v3(TH, 8, 1).to_broadcast([128, 8, 4]), op=ALU.is_ge))
        p.vw(lambda e: e.tensor_tensor(
            out=v3(MSK, 32, 4), in0=v3(SCH, 32, 4),
            in1=v3(G2, 32, 1).to_broadcast([128, 32, 4]), op=ALU.mult))
        dump(MSK)
        p.vw(lambda e: e.tensor_copy(WW, MSK))
        for k in range(4):
            p.vw(lambda e: e.tensor_reduce(
                out=TH, in_=v3(WW, 8, 16), axis=AX.X, op=ALU.max))
            if k < 3:
                p.vw(lambda e: e.tensor_tensor(
                    out=v3(SCH, 8, 16), in0=v3(WW, 8, 16),
                    in1=v3(TH, 8, 1).to_broadcast([128, 8, 16]),
                    op=ALU.is_equal))
                p.vw(lambda e: e.scalar_tensor_tensor(
                    out=WW, in0=SCH, scalar=-1e30, in1=WW,
                    op0=ALU.mult, op1=ALU.add))
        dump(TH, 8)
        p.vw(lambda e: e.tensor_tensor(
            out=v3(WW, 8, 16), in0=v3(MSK, 8, 16),
            in1=v3(TH, 8, 1).to_broadcast([128, 8, 16]), op=ALU.is_ge))
        p.vw(lambda e: e.tensor_tensor(out=WW, in0=SCO, in1=WW, op=ALU.mult))
        dump(WW)
        p.vw(lambda e: e.tensor_reduce(
            out=TH, in_=v3(WW, 8, 16), axis=AX.X, op=ALU.add))
        p.vw(lambda e: e.tensor_scalar(
            out=TH, in0=TH, scalar1=1e-20, scalar2=None, op0=ALU.add))
        p.vw(lambda e: e.reciprocal(RCP, TH))
        p.vw(lambda e: e.scalar_tensor_tensor(
            out=v3(CW16[:, :], 8, 16), in0=v3(WW, 8, 16), scalar=2.5,
            in1=v3(RCP, 8, 1).to_broadcast([128, 8, 16]),
            op0=ALU.mult, op1=ALU.mult))
        st_["vCw"] = p.cnt["V"]

    def ffn_body(tb, lrout_src, lr_spec, pending_d=None,
             pending_d2=None, pending_head=None):
        pool = [(pA, "pA"), (pB, "pB"), (pD, "pD")]
        pidx = [0]

        def next_tile():
            t_ = pool[pidx[0] % 3]
            pidx[0] += 1
            return t_

        w1_marks = [0] * E
        hid_mark = [[0, 0] for _ in range(16)]
        marks = {}
        lr_out = None
        for gi in range(NG + 1):
            shared = (gi == 0)
            g = gi - 1
            if shared and pending_head is not None:
                pending_head()
            experts = [None] if shared else list(range(g * EG, (g + 1) * EG))
            for e_ in experts:
                cw_marks = [0, 0]
                if not shared:
                    slot = e_ % 2
                    sCwt = lrout_src["sCwt"]
                    for tt in range(2):
                        tile, tname = next_tile()
                        p.t(lambda en, tile=tile, e_=e_, tt=tt: en.matmul(
                            tile[:, :], SELBe(e_),
                            CWT[0:16, tt * 512:(tt + 1) * 512],
                            start=True, stop=True),
                            waits=[("S", sCwt)] + twait(tname))
                        tCwb = p.cnt["T"]
                        wv = [("T", tCwb)]
                        if cwb_free[tt]:
                            wv.append(("V", cwb_free[tt]))
                        p.s(lambda en, tile=tile, tt=tt: en.copy(
                            CWB[:, tt, :], tile[:, :]), waits=wv)
                        guard[tname] = ("S", p.cnt["S"])
                        cw_marks[tt] = p.cnt["S"]
                nch = 8 if shared else 4
                for ic in range(nch):
                    for tt in range(2):
                        tile, tname = next_tile()
                        for hc in range(NHC):
                            w = []
                            if hc == 0:
                                w = twait(tname) + [
                                    ("V", lrout_src["vXlnT"][tt])]
                                w.append(("DW", state["ws1_half"] if shared
                                          else w1_ready[slot]))
                            elif hc == 4 and shared:
                                w = [("DW", state["ws1_ready"])]
                            lhs = (WS1B[:, hc, ic * 128:(ic + 1) * 128]
                                   if shared else
                                   W1B[:, slot, hc, ic * 128:(ic + 1) * 128])
                            p.t(lambda en, lhs=lhs, tile=tile, tt=tt, hc=hc:
                                en.matmul(tile[:, :], lhs,
                                          XLN[:, hc, tt * 512:(tt + 1) * 512],
                                          start=(hc == 0),
                                          stop=(hc == NHC - 1)), waits=w)
                        tFill = p.cnt["T"]
                        k = ic if shared else 4 * (e_ - g * EG) + ic
                        hs = HID[:, k, tt * 512:(tt + 1) * 512]
                        p.s(lambda en, hs=hs, tile=tile: en.activation(
                            hs, tile[:, :], AF.Relu), waits=[("T", tFill)])
                        sRelu = p.cnt["S"]
                        guard[tname] = ("S", sRelu)
                        p.v(lambda en, hs=hs: en.tensor_tensor(
                            out=hs, in0=hs, in1=hs, op=ALU.mult),
                            waits=[("S", sRelu)])
                        if not shared:
                            p.vw(lambda en, hs=hs, tt=tt: en.tensor_tensor(
                                out=hs, in0=hs, in1=CWB[:, tt, :],
                                op=ALU.mult), waits=[("S", cw_marks[tt])])
                        hid_mark[k][tt] = p.cnt["V"]
                if not shared:
                    w1_marks[e_] = p.cnt["T"]
                    cwb_free[0] = p.cnt["V"]
                    cwb_free[1] = p.cnt["V"]
                    nxt = e_ + 2
                    if nxt < E:
                        load_w1(nxt, nxt % 2, w1_marks[e_])
            if gi == 0 and pending_d is not None:
                pending_d()
                pending_d = None
            last_group = (gi == NG)
            if last_group:
                marks["t_xln_free"] = p.cnt["T"]
                guard["CWT"] = ("T", p.cnt["T"])
            nk = 8 if shared else 16
            lr_stages = None
            if last_group and lr_spec is not None:
                tb2, waitX2, do_rout2 = lr_spec
                lr_stages, lr_out = make_lr(tb2, waitX2, do_rout2,
                                            marks["t_xln_free"])
            tile_i = 0
            for oc in range(8):
                for tt in range(2):
                    if lr_stages is not None:
                        if tile_i == 0:
                            lr_stages["A0"]()
                        elif tile_i == 2:
                            lr_stages["A"]()
                        elif 3 <= tile_i <= 10:
                            lr_stages["Bh"](tile_i - 3)
                        elif tile_i == 11:
                            lr_stages["Bt"]()
                        elif tile_i == 13:
                            lr_stages["C"]()
                    tile_i += 1
                    tile, tname = next_tile()
                    for k in range(nk):
                        w = [("V", hid_mark[k][tt])]
                        if k == 0:
                            w += twait(tname) + [("DW", state["w2g_ready"])]
                        p.t(lambda en, tile=tile, k=k, oc=oc, tt=tt: en.matmul(
                            tile[:, :], W2GB[:, k, oc * 128:(oc + 1) * 128],
                            HID[:, k, tt * 512:(tt + 1) * 512],
                            start=(k == 0), stop=(k == nk - 1)), waits=w)
                    tFill = p.cnt["T"]
                    p.v(lambda en, tile=tile, oc=oc, tt=tt:
                        en.scalar_tensor_tensor(
                            out=X[:, oc, tb * TB + tt * 512:
                                  tb * TB + (tt + 1) * 512],
                            in0=tile[:, :], scalar=0.5,
                            in1=X[:, oc, tb * TB + tt * 512:
                                  tb * TB + (tt + 1) * 512],
                            op0=ALU.mult, op1=ALU.add), waits=[("T", tFill)])
                    guard[tname] = ("V", p.cnt["V"])
            tW2 = p.cnt["T"]
            if gi == 0 and pending_d2 is not None:
                pending_d2()
                pending_d2 = None
            if gi < NG:
                load_w2g(gi, tW2)
        marks["vX"] = p.cnt["V"]
        marks["tEnd"] = p.cnt["T"]
        if lr_stages is not None:
            marks["lr_d"] = lr_stages["D"]
        return marks, lr_out

    def attn_body(lra0_out, waitX_t1, lr_next_spec, pending_d=None):
        marks = {}
        vXlnT_cur = lra0_out["vXlnT"]
        ub_mark_s = [0, 0]
        u_mark = [[0, 0] for _ in range(8)]
        for tb in range(NTB):
            pool = [(pA, "pA"), (pB, "pB"), (pC, "pC"), (pD, "pD")]
            pi = [0]

            def nt():
                t_ = pool[pi[0] % 4]
                pi[0] += 1
                return t_

            for oc in range(8):
                for tt in range(2):
                    if pending_d is not None and (oc, tt) == (1, 0) and tb == 0:
                        pending_d()
                        pending_d = None
                    tileX, tnX = nt()
                    tileG, tnG = nt()
                    W1F = W1B[:, :, :, :].rearrange("p a b c -> p (a b c)")
                    for hc in range(NHC):
                        w = []
                        if hc == 0:
                            w = twait(tnX) + [("V", vXlnT_cur[tt]),
                                              ("DW", state["win_ready"])]
                        p.t(lambda en, W1F=W1F, oc=oc, tt=tt, hc=hc,
                            tileX=tileX:
                            en.matmul(tileX[:, :],
                                      W1F[:, hc * 1024 + oc * 128:
                                          hc * 1024 + (oc + 1) * 128],
                                      XLN[:, hc, tt * 512:(tt + 1) * 512],
                                      start=(hc == 0), stop=(hc == NHC - 1)),
                            waits=w)
                    tXin = p.cnt["T"]
                    for hc in range(NHC):
                        w = []
                        if hc == 0:
                            w = twait(tnG) + [("DW", state["wgate_ready"])]
                        p.t(lambda en, tileG=tileG, oc=oc, tt=tt, hc=hc:
                            en.matmul(tileG[:, :],
                                      W2GB[:, hc, oc * 128:(oc + 1) * 128],
                                      XLN[:, hc, tt * 512:(tt + 1) * 512],
                                      start=(hc == 0), stop=(hc == NHC - 1)),
                            waits=w)
                    tGat = p.cnt["T"]
                    hs = HID[:, oc, tt * 512:(tt + 1) * 512]
                    p.s(lambda en, hs=hs, tileX=tileX, oc=oc: en.activation(
                        hs, tileX[:, :], AF.Identity,
                        bias=CF[:, CF_BIN + oc:CF_BIN + oc + 1]),
                        waits=[("T", tXin), ("DS", dsConst)])
                    guard[tnX] = ("S", p.cnt["S"])
                    gpar = (oc * 2 + tt) % 2
                    wv = [("T", tGat)]
                    if gsc_free[gpar]:
                        wv.append(("V", gsc_free[gpar]))
                    p.s(lambda en, tileG=tileG, oc=oc, gpar=gpar: en.activation(
                        GSC[:, gpar, :], tileG[:, :], AF.Sigmoid,
                        bias=CF[:, CF_BGATE + oc:CF_BGATE + oc + 1]), waits=wv)
                    sG = p.cnt["S"]
                    guard[tnG] = ("S", sG)
                    p.v(lambda en, hs=hs, gpar=gpar: en.tensor_tensor(
                        out=hs, in0=hs, in1=GSC[:, gpar, :], op=ALU.mult),
                        waits=[("S", sG)])
                    gsc_free[gpar] = p.cnt["V"]
                    u_mark[oc][tt] = p.cnt["V"]
            vU = p.cnt["V"]
            if tb == 0:
                marks["t_a_xln0"] = p.cnt["T"]
                lr_t1, lr_t1_out = make_lr(1, waitX_t1, False,
                                           marks["t_a_xln0"])
                marks["lr_t1_out"] = lr_t1_out
                lr_t1["A0"]()
            else:
                marks["t_xln_free"] = p.cnt["T"]
            for tt in range(2):
                for hc in range(NHC):
                    w = [("V", u_mark[hc][tt])]
                    if hc == 0:
                        w += twait("pC") + [("DW", dwConst)]
                    p.t(lambda en, tt=tt, hc=hc: en.matmul(
                        pC[0:16, :], BMT[:, hc, :],
                        HID[:, hc, tt * 512:(tt + 1) * 512],
                        start=(hc == 0), stop=(hc == NHC - 1)), waits=w)
                tUb = p.cnt["T"]
                p.s(lambda en, tb=tb, tt=tt: en.copy(
                    UHS[0:16, tb * TB + tt * 512:tb * TB + (tt + 1) * 512],
                    pC[0:16, :]), waits=[("T", tUb)])
                guard["pC"] = ("S", p.cnt["S"])
            ub_mark_s[tb] = p.cnt["S"]

            if tb == 0:
                lr_t1["A"]()
                p.v(lambda e: e.tensor_copy(pow_slot(0), AT_),
                    waits=[("DW", dwConst)])
                vAT = p.cnt["V"]
                ppool = [(pC, "pC"), (pD, "pD")]
                par = [0]

                def npp():
                    t_ = ppool[par[0] % 2]
                    par[0] += 1
                    return t_

                spow = [0]

                def chain(lhs_fn, base, first_waits):
                    for k in range(1, 8):
                        tile, tn = npp()
                        w = (list(first_waits) if k == 1 else
                             [("V", spow[0])]) + twait(tn)
                        p.t(lambda e, tile=tile, k=k, base=base: e.matmul(
                            tile[0:16, 0:16], lhs_fn(),
                            pow_slot(base + k - 1), start=True, stop=True),
                            waits=w)
                        tPk = p.cnt["T"]
                        p.v(lambda e, tile=tile, k=k, base=base: e.tensor_copy(
                            pow_slot(base + k), tile[0:16, 0:16]),
                            waits=[("T", tPk)])
                        spow[0] = p.cnt["V"]
                        guard[tn] = ("V", spow[0])

                chain(lambda: A_, 0, [("V", vAT), ("DW", dwConst)])
                lr_t1["Bh"](0)
                lr_t1["Bh"](1)
                p.t(lambda e: e.transpose(pT[0:16, 0:16], pow_slot(7), ID16),
                    waits=[("V", spow[0])] + twait("pT"))
                tP = p.cnt["T"]
                p.v(lambda e: e.tensor_copy(pow_slot(24), pT[0:16, 0:16]),
                    waits=[("T", tP)])
                spow[0] = p.cnt["V"]
                guard["pT"] = ("V", spow[0])
                p.vw(lambda e: e.tensor_copy(pow_slot(8), pow_slot(7)))
                vC = p.cnt["V"]
                lr_t1["Bh"](2)
                lr_t1["Bh"](3)
                chain(lambda: pow_slot(24), 8, [("V", vC)])
                lr_t1["Bh"](4)
                lr_t1["Bh"](5)
                p.t(lambda e: e.transpose(pT[0:16, 16:32], pow_slot(15), ID16),
                    waits=[("V", spow[0])] + twait("pT"))
                tP = p.cnt["T"]
                p.v(lambda e: e.tensor_copy(pow_slot(25), pT[0:16, 16:32]),
                    waits=[("T", tP)])
                spow[0] = p.cnt["V"]
                guard["pT"] = ("V", spow[0])
                p.vw(lambda e: e.tensor_copy(pow_slot(16), pow_slot(15)))
                vC = p.cnt["V"]
                chain(lambda: pow_slot(25), 16, [("V", vC)])
                state["sPowAll"] = spow[0]
                lr_t1["Bh"](6)
                lr_t1["Bh"](7)
                lr_t1["Bt"]()

                def build_ker(K, src0):
                    first = True
                    for j in range(8):
                        for i in range(j + 1):
                            d = j - i
                            rhs = ID16 if d == 0 else pow_slot(src0 + d - 1)
                            w = ([("V", state["sPowAll"])] + twait("pD")
                                 if first else [])
                            first = False
                            p.t(lambda e, j=j, i=i, rhs=rhs: e.matmul(
                                pD[:, 16 * j:16 * (j + 1)], STK0l(i), rhs,
                                start=(i == 0), stop=(i == j)), waits=w)
                    tKer = p.cnt["T"]
                    p.v(lambda e, K=K: e.tensor_copy(K[:, :], pD[:, 0:128]),
                        waits=[("T", tKer)])
                    sK = p.cnt["V"]
                    guard["pD"] = ("V", sK)
                    return sK

                kmarks.clear()
                kmarks.append(build_ker(KER1, 0))
                kmarks.append(build_ker(KER2, 8))
                kmarks.append(build_ker(KER3, 16))
                ARNB = ARN[:, :].bitcast(BF16)
                for idx, src0 in enumerate((0, 8, 16)):
                    for k in range(8):
                        p.vw(lambda e, idx=idx, src0=src0, k=k: e.tensor_copy(
                            ARNB[0:16, idx * 128 + 16 * k:
                                 idx * 128 + 16 * (k + 1)],
                            pow_slot(src0 + k)),
                            waits=[("V", state["sPowAll"])])
                state["vPowT"] = p.cnt["V"]
                for l in range(8):
                    w = ([("S", ub_mark_s[0]), ("DW", dwConst)]
                         + twait("pE0") if l == 0 else [])
                    p.t(lambda e, l=l: e.matmul(
                        pE[:, 0:128], STK0l(l), UHS[0:16, l::8][:, 0:128],
                        start=(l == 0), stop=(l == 7)), waits=w)
                tB1a = p.cnt["T"]
                p.s(lambda e: e.copy(B1[:, 0:128], pE[:, 0:128]),
                    waits=[("T", tB1a)])
                sB1a = p.cnt["S"]
                p.t(lambda e: e.matmul(pE[:, 512:640], KER1[:, :],
                                       B1[:, 0:128], start=True, stop=False,
                                       skip_group_check=True),
                    waits=[("V", kmarks[0]), ("S", sB1a)] + twait("pE1"))
                tS1a = p.cnt["T"]
                p.s(lambda e: e.copy(S1SB[:, 0:128], pE[:, 512:640]),
                    waits=[("T", tS1a)])
                sS1a = p.cnt["S"]
                p.t(lambda e: e.matmul(pC[0:16, 0:128], UNSj(7),
                                       S1SB[:, 0:128], start=True, stop=True),
                    waits=[("S", sS1a)] + twait("pC"))
                tE1a = p.cnt["T"]
                p.s(lambda e: e.copy(UHS[0:16, 0:128],
                                     pC[0:16, 0:128]), waits=[("T", tE1a)])
                sE1a = p.cnt["S"]
                guard["pC"] = ("S", sE1a)
                for l in range(8):
                    w = ([("S", sE1a)] + twait("pD") if l == 0 else [])
                    p.t(lambda e, l=l: e.matmul(
                        pD[:, 32:48], STK0l(l),
                        UHS[0:16, 0:128][:, l::8][:, 0:16],
                        start=(l == 0), stop=(l == 7)), waits=w)
                tB2a = p.cnt["T"]
                p.s(lambda e: e.copy(B2[:, 0:16], pD[:, 32:48]),
                    waits=[("T", tB2a)])
                sB2a = p.cnt["S"]
                guard["pD"] = ("S", sB2a)
                p.t(lambda e: e.matmul(pE[:, 768:784], KER2[:, :],
                                       B2[:, 0:16], start=False, stop=False,
                                       skip_group_check=True),
                    waits=[("V", kmarks[1]), ("S", sB2a)])
                state["tS2a"] = p.cnt["T"]
                state["sB1a"] = sB1a
                vXlnT_cur = lr_t1_out["vXlnT"]

        ARNB = ARN[:, :].bitcast(BF16)

        def POWT(i):
            return ARNB[0:16, i * 128:(i + 1) * 128]

        for l in range(8):
            w = ([("S", ub_mark_s[1])] if l == 0 else [])
            p.t(lambda e, l=l: e.matmul(
                pE[:, 128:256], STK0l(l), UHS[0:16, l::8][:, 128:256],
                start=(l == 0), stop=(l == 7)), waits=w)
        tB1 = p.cnt["T"]
        p.s(lambda e: e.copy(B1[:, 128:256], pE[:, 128:256]),
            waits=[("T", tB1)])
        sB1 = p.cnt["S"]
        guard["pE0"] = ("S", sB1)
        p.t(lambda e: e.matmul(pE[:, 640:768], KER1[:, :], B1[:, 128:256],
                               start=False, stop=False, skip_group_check=True),
            waits=[("S", sB1)])
        tS1 = p.cnt["T"]
        p.s(lambda e: e.copy(S1SB[:, 128:256], pE[:, 640:768]),
            waits=[("T", tS1)])
        sS1 = p.cnt["S"]
        p.t(lambda e: e.matmul(pC[0:16, 128:256], UNSj(7), S1SB[:, 128:256],
                               start=True, stop=True),
            waits=[("S", sS1)] + twait("pC"))
        tE1 = p.cnt["T"]
        p.s(lambda e: e.copy(B1[0:16, 128:256], pC[0:16, 128:256]),
            waits=[("T", tE1)])
        sE1 = p.cnt["S"]
        guard["pC"] = ("S", sE1)
        for l in range(8):
            w = ([("S", sE1)] + twait("pD") if l == 0 else [])
            p.t(lambda e, l=l: e.matmul(
                pD[:, 0:16], STK0l(l), B1[0:16, 128:256][:, l::8][:, 0:16],
                start=(l == 0), stop=(l == 7)), waits=w)
        tB2 = p.cnt["T"]
        p.s(lambda e: e.copy(B2[:, 16:32], pD[:, 0:16]), waits=[("T", tB2)])
        sB2 = p.cnt["S"]
        guard["pD"] = ("S", sB2)
        p.t(lambda e: e.matmul(pE[:, 784:800], KER2[:, :], B2[:, 16:32],
                               start=False, stop=False, skip_group_check=True),
            waits=[("S", sB2)])
        tS2 = p.cnt["T"]
        p.s(lambda e: e.copy(S2SB[:, :], pE[:, 768:800]), waits=[("T", tS2)])
        sS2 = p.cnt["S"]
        p.t(lambda e: e.matmul(pC[0:16, 256:288], UNSj(7), S2SB[:, :],
                               start=True, stop=True),
            waits=[("S", sS2)] + twait("pC"))
        tE2 = p.cnt["T"]
        p.s(lambda e: e.copy(B1[0:16, 0:32], pC[0:16, 256:288]),
            waits=[("T", tE2)])
        sE2 = p.cnt["S"]
        guard["pC"] = ("S", sE2)
        for l in range(8):
            w = ([("S", sE2)] + twait("pD") if l == 0 else [])
            p.t(lambda e, l=l: e.matmul(
                pD[:, 64:68], STK0l(l), B1[0:16, l::8][:, 0:4],
                start=(l == 0), stop=(l == 7)), waits=w)
        tB3 = p.cnt["T"]
        p.s(lambda e: e.copy(B3[:, 0:4], pD[:, 64:68]), waits=[("T", tB3)])
        sB3 = p.cnt["S"]
        guard["pD"] = ("S", sB3)
        p.t(lambda e: e.matmul(pB[:, 0:4], KER3[:, :], B3[:, 0:4],
                               start=True, stop=False, skip_group_check=True),
            waits=[("V", kmarks[2]), ("S", sB3)] + twait("pB"))
        tS3 = p.cnt["T"]
        p.s(lambda e: e.copy(S3SB[:, 0:4], pB[:, 0:4]), waits=[("T", tS3)])
        sS3 = p.cnt["S"]
        p.t(lambda e: e.matmul(pC[0:16, 288:292], UNSj(7), S3SB[:, 0:4],
                               start=True, stop=True),
            waits=[("S", sS3)] + twait("pC"))
        tE3 = p.cnt["T"]
        p.s(lambda e: e.copy(E3[0:16, 0:4], pC[0:16, 288:292]),
            waits=[("T", tE3)])
        sE3 = p.cnt["S"]
        guard["pC"] = ("S", sE3)
        p.v(lambda e: e.tensor_copy(GC[0:16, 0:1], E3[0:16, 0:1]),
            waits=[("S", sE3)])
        vG = p.cnt["V"]
        for c in range(1, 4):
            p.t(lambda e, c=c: e.matmul(
                pC[0:16, 296 + c:297 + c], pow_slot(23), GC[0:16, c - 1:c],
                start=True, stop=True),
                waits=[("V", vG), ("V", state["sPowAll"])])
            tGc = p.cnt["T"]
            p.v(lambda e, c=c: e.tensor_tensor(
                out=GC[0:16, c:c + 1], in0=pC[0:16, 296 + c:297 + c],
                in1=E3[0:16, c:c + 1], op=ALU.add), waits=[("T", tGc)])
            vG = p.cnt["V"]
        guard["pC"] = ("V", vG)
        p.vw(lambda e: e.tensor_copy(CIN3[0:16, 1:4], GC[0:16, 0:3]),
             waits=[("V", vG), ("V", vInitCin)])
        vCin3 = p.cnt["V"]
        p.t(lambda e: e.matmul(pB[:, 0:4], POWT(2), CIN3[0:16, 0:4],
                               start=False, stop=True, skip_group_check=True),
            waits=[("V", vCin3), ("V", state["vPowT"])])
        tC3 = p.cnt["T"]
        p.s(lambda e: e.copy(S3SB[:, 0:4], pB[:, 0:4]), waits=[("T", tC3)])
        sS3t = p.cnt["S"]
        guard["pB"] = ("S", sS3t)
        for l3 in range(8):
            w = ([("S", sS3t)] + twait("pC") if l3 == 0 else [])
            p.t(lambda e, l3=l3: e.matmul(
                pC[0:16, 304 + 4 * l3:304 + 4 * (l3 + 1)], UNSj(l3),
                S3SB[:, 0:4], start=True, stop=True), waits=w)
        tCin2 = p.cnt["T"]
        for l3 in range(8):
            if l3 < 7:
                p.s(lambda e, l3=l3: e.copy(
                    CIN2[0:16, (l3 + 1)::8][:, 0:4],
                    pC[0:16, 304 + 4 * l3:304 + 4 * (l3 + 1)]),
                    waits=[("T", tCin2)] if l3 == 0 else [])
            else:
                p.s(lambda e: e.copy(CIN2[0:16, 8::8][:, 0:3],
                                     pC[0:16, 332:335]))
        sCin2 = p.cnt["S"]
        guard["pC"] = ("S", sCin2)
        p.t(lambda e: e.matmul(pE[:, 768:800], POWT(1), CIN2[0:16, :],
                               start=False, stop=True, skip_group_check=True),
            waits=[("S", sCin2)])
        tC2 = p.cnt["T"]
        p.s(lambda e: e.copy(S2SB[:, :], pE[:, 768:800]), waits=[("T", tC2)])
        sS2t = p.cnt["S"]
        for l2 in range(8):
            w = ([("S", sS2t)] + twait("pC") if l2 == 0 else [])
            p.t(lambda e, l2=l2: e.matmul(
                pC[0:16, 32 * l2:32 * (l2 + 1)], UNSj(l2), S2SB[:, :],
                start=True, stop=True), waits=w)
        tCin1 = p.cnt["T"]
        for l2 in range(8):
            if l2 < 7:
                p.s(lambda e, l2=l2: e.copy(
                    CIN1[0:16, (l2 + 1)::8][:, 0:32],
                    pC[0:16, 32 * l2:32 * (l2 + 1)]),
                    waits=[("T", tCin1)] if l2 == 0 else [])
            else:
                p.s(lambda e: e.copy(CIN1[0:16, 8::8][:, 0:31],
                                     pC[0:16, 224:255]))
        sCin1 = p.cnt["S"]
        guard["pC"] = ("S", sCin1)
        p.t(lambda e: e.matmul(pE[:, 512:768], POWT(0), CIN1[0:16, :],
                               start=False, stop=True, skip_group_check=True),
            waits=[("S", sCin1)])
        tC1 = p.cnt["T"]
        p.s(lambda e: e.copy(S1SB[:, :], pE[:, 512:768]), waits=[("T", tC1)])
        sHst = p.cnt["S"]
        guard["pE1"] = ("S", sHst)
        for j in range(8):
            tile, tn = (pC, "pC") if j % 2 == 0 else (pD, "pD")
            p.t(lambda e, j=j, tile=tile: e.matmul(
                tile[0:16, 256:512], UNSj(j), S1SB[:, :],
                start=True, stop=True), waits=[("S", sHst)] + twait(tn))
            tU = p.cnt["T"]
            p.s(lambda e, j=j, tile=tile: e.copy(
                UHS[0:16, j::8][:, 0:256], tile[0:16, 256:512]),
                waits=[("T", tU)])
            guard[tn] = ("S", p.cnt["S"])
        sHs = p.cnt["S"]

        pool2 = [(pA, "pA"), (pB, "pB"), (pD, "pD")]
        pj = [0]

        def nt2():
            t_ = pool2[pj[0] % 3]
            pj[0] += 1
            return t_

        lr_next = None
        lr_next_out = None
        for gt in range(4):
            ypar = gt % 2
            if lr_next_spec is not None and gt == 2:
                lr_next["A"]()
            for oc in range(8):
                tile, tn = nt2()
                w = ([("S", sHs), ("DW", dwConst)] + twait(tn) +
                     twait("Y%d" % ypar))
                p.t(lambda e, oc=oc, gt=gt, tile=tile: e.matmul(
                    tile[:, :], CMT[0:16, oc * 128:(oc + 1) * 128],
                    UHS[0:16, gt * 512:(gt + 1) * 512], start=True, stop=True),
                    waits=w)
                tY = p.cnt["T"]
                p.s(lambda e, oc=oc, ypar=ypar, tile=tile: e.copy(
                    HID[:, 8 + oc, ypar * 512:(ypar + 1) * 512], tile[:, :]),
                    waits=[("T", tY)])
                guard[tn] = ("S", p.cnt["S"])
            sY = p.cnt["S"]
            for oc in range(8):
                tile, tn = nt2()
                for hc in range(NHC):
                    w = []
                    if hc == 0:
                        w = twait(tn) + [("S", sY),
                                         ("DW", state["wout_ready"])]
                    p.t(lambda e, oc=oc, hc=hc, ypar=ypar, tile=tile: e.matmul(
                        tile[:, :], W2GB[:, 8 + hc, oc * 128:(oc + 1) * 128],
                        HID[:, 8 + hc, ypar * 512:(ypar + 1) * 512],
                        start=(hc == 0), stop=(hc == NHC - 1)), waits=w)
                tO = p.cnt["T"]
                p.v(lambda e, oc=oc, gt=gt, tile=tile: e.scalar_tensor_tensor(
                    out=X[:, oc, gt * 512:(gt + 1) * 512], in0=tile[:, :],
                    scalar=CF[:, CF_BOUT + oc:CF_BOUT + oc + 1],
                    in1=X[:, oc, gt * 512:(gt + 1) * 512],
                    op0=ALU.add, op1=ALU.add), waits=[("T", tO)])
                guard[tn] = ("V", p.cnt["V"])
                if lr_next is not None and gt == 2:
                    lr_next["Bh"](oc)
                    if oc == 7:
                        lr_next["Bt"]()

            guard["Y%d" % ypar] = ("T", p.cnt["T"])
            if gt == 1 and lr_next_spec is not None:
                vX01 = p.cnt["V"]
                tb2, do_rout2 = lr_next_spec
                lr_next, lr_next_out = make_lr(
                    tb2, [("V", vX01)], do_rout2, marks["t_xln_free"])
                lr_next["A0"]()
        if lr_next is not None:
            lr_next["C"]()
            marks["lr_d"] = lr_next["D"]
        marks["vX"] = p.cnt["V"]
        marks["tEnd"] = p.cnt["T"]
        return marks, lr_next_out

    lr0, lr0_out = make_lr(0, [("DS", dsX0)], True, 0)
    lr0["A0"]()
    lr0["A"]()
    for i_ in range(8):
        lr0["Bh"](i_)
    lr0["Bt"]()

    m_f1t0, lr1_out = ffn_body(0, lr0_out,
                               (1, [("DS", dsX1)], True),
                               pending_head=lr0["C"], pending_d2=lr0["D"])
    load_into_w2gb(d_ws2, 0, m_f1t0["tEnd"])
    load_w1(0, 0, m_f1t0["tEnd"])
    load_w1(1, 1, m_f1t0["tEnd"])

    m_f1t1, lra0_out = ffn_body(1, lr1_out,
                                (0, [("V", m_f1t0["vX"])], False),
                                pending_d=m_f1t0["lr_d"])
    p.g(lambda eng: eng.dma_start(
        out=W1B[:, :, :, :].rearrange("p a b c -> p (a b c)"), in_=d_win),
        waits=[("T", m_f1t1["t_xln_free"])])
    state["win_ready"] = p.cnt["DW"]
    state["wgate_ready"] = load_into_w2gb(d_wgate, 0, m_f1t1["tEnd"])
    state["wout_ready"] = load_into_w2gb(d_wout, 1, m_f1t1["tEnd"])

    m_attn, lrf2_out = attn_body(lra0_out,
                                 [("V", m_f1t1["vX"])], (0, True),
                                 pending_d=m_f1t1["lr_d"])
    load_into_w2gb(d_ws2, 0, m_attn["t_xln_free"])
    load_w1(0, 0, m_attn["t_xln_free"])
    load_w1(1, 1, m_attn["t_xln_free"])

    m_f2t0, lrf2t1_out = ffn_body(0, lrf2_out,
                                  (1, [("V", m_attn["vX"])], True),
                                  pending_d=m_attn["lr_d"])
    load_into_w2gb(d_ws2, 0, m_f2t0["tEnd"])
    load_w1(0, 0, m_f2t0["tEnd"])
    load_w1(1, 1, m_f2t0["tEnd"])

    ov = d_out.rearrange("(c p) t -> p c t", p=128)
    p.y(lambda e: e.dma_start(out=ov[:, :, 0:TB], in_=X[:, :, 0:TB]),
        waits=[("V", m_f2t0["vX"])])

    m_f2t1, _ = ffn_body(1, lrf2t1_out, None,
                         pending_d=m_f2t0["lr_d"])

    p.y(lambda e: e.dma_start(out=ov[:, :, TB:SEQ], in_=X[:, :, TB:SEQ]),
        waits=[("V", m_f2t1["vX"])])
    dsOut = p.cnt["DS"]
    p.ywait([("DS", dsOut)])

    with nc.Block() as block:
        p.emit(block, sems)
    ctx.close()
    return nc


def _host_prepare(inp):
    f32 = np.float32
    x = np.ascontiguousarray(inp["L0_attn_x"], dtype=f32)
    w1 = np.ascontiguousarray(inp["L0_ffn_W1"], dtype=f32)
    w2 = np.ascontiguousarray(inp["L0_ffn_W2"], dtype=f32)
    ws1 = np.ascontiguousarray(inp["L0_ffn_Ws1"], dtype=f32)
    ws2 = np.ascontiguousarray(inp["L0_ffn_Ws2"], dtype=f32)
    wg = np.ascontiguousarray(inp["L0_ffn_Wg"], dtype=f32)
    bias_corr = np.ascontiguousarray(inp["L0_ffn_bias_corr"], dtype=f32)
    win = np.ascontiguousarray(inp["L0_attn_W_in"], dtype=f32)
    b_in = np.ascontiguousarray(inp["L0_attn_b_in"], dtype=f32)
    a = np.ascontiguousarray(inp["L0_attn_A"], dtype=f32)
    bm = np.ascontiguousarray(inp["L0_attn_Bm"], dtype=f32)
    cm = np.ascontiguousarray(inp["L0_attn_Cm"], dtype=f32)
    wgate = np.ascontiguousarray(inp["L0_attn_W_gate"], dtype=f32)
    b_gate = np.ascontiguousarray(inp["L0_attn_b_gate"], dtype=f32)
    wout = np.ascontiguousarray(inp["L0_attn_W_out"], dtype=f32)
    b_out = np.ascontiguousarray(inp["L0_attn_b_out"], dtype=f32)

    def fm_chunks(wT):
        inf, outf = wT.shape
        return np.ascontiguousarray(
            wT.reshape(inf // 128, 128, outf).transpose(1, 0, 2).reshape(128, -1))

    w1t = np.stack([fm_chunks(w1[e].T) for e in range(E)])
    w2g = np.zeros((NG, 128, 16 * DIM), f32)
    for g in range(NG):
        for ei in range(EG):
            w2g[g][:, ei * 4 * DIM:(ei + 1) * 4 * DIM] = fm_chunks(
                w2[g * EG + ei].T)
    ws1t = fm_chunks(ws1.T)
    ws2t = fm_chunks(ws2.T)
    wint = fm_chunks(win[DIM:2 * DIM].T)
    wgatet = fm_chunks(wgate.T)
    woutt = fm_chunks(wout.T)
    wgt = fm_chunks(wg.T)
    bmt = fm_chunks(bm.T)
    cmt = np.ascontiguousarray(cm.T)

    cb = np.zeros((128, CBW), f32)
    cb[:, OFF_ONES:OFF_ONES + 128] = 1.0
    cb[:, OFF_ID:OFF_ID + 128] = np.eye(128)
    for j in range(8):
        blk = np.zeros((128, 16), f32)
        for m in range(16):
            blk[16 * j + m, m] = 1.0
        cb[:, OFF_UNS + 16 * j:OFF_UNS + 16 * (j + 1)] = blk
    for e in range(E):
        blk = np.zeros((16, 128), f32)
        blk[e, :] = 1.0
        cb[0:16, OFF_SELB + 128 * e:OFF_SELB + 128 * (e + 1)] = blk
    for l in range(8):
        blk = np.zeros((16, 128), f32)
        for n in range(16):
            blk[n, 16 * l + n] = 1.0
        cb[0:16, OFF_STK0 + 128 * l:OFF_STK0 + 128 * (l + 1)] = blk
    cb[0:16, OFF_A:OFF_A + 16] = a
    cb[0:16, OFF_AT:OFF_AT + 16] = a.T

    cf = np.zeros((128, CFW), f32)
    cf[:, CF_BCB:CF_BCB + 16] = np.broadcast_to(bias_corr, (128, 16))
    cf[:, CF_BIN:CF_BIN + 8] = b_in[DIM:2 * DIM].reshape(8, 128).T
    cf[:, CF_BGATE:CF_BGATE + 8] = b_gate.reshape(8, 128).T
    cf[:, CF_BOUT:CF_BOUT + 8] = b_out.reshape(8, 128).T
    cf[:, CF_EPS] = DIM * LNEPS
    cf[:, CF_LNC] = np.log(np.sqrt(float(DIM)))

    common = {
        "w1t": w1t, "w2g": w2g, "ws1t": ws1t, "ws2t": ws2t,
        "wint": wint, "wgatet": wgatet, "woutt": woutt,
        "wgt": wgt, "bmt": bmt, "cmt": cmt, "cb": cb, "cf": cf,
    }
    in_maps = []
    for b in range(NCORES):
        m = dict(common)
        m["xt"] = np.ascontiguousarray(x[b].T)
        in_maps.append(m)
    return in_maps


def kernel(**inputs):
    in_maps = _host_prepare(inputs)
    nc = build_program()
    res = run_bass_kernel_spmd(nc, in_maps, core_ids=list(range(NCORES)))
    out = np.empty((NCORES, SEQ, DIM), np.float32)
    for b in range(NCORES):
        out[b] = np.asarray(res.results[b]["yt"], np.float32).T
    return out


# version 14
# speedup vs baseline: 1.0173x; 1.0063x over previous
import numpy as np
import ml_dtypes

import concourse.bass as bass
import concourse.mybir as mybir
from concourse.bass_utils import run_bass_kernel_spmd

F32 = mybir.dt.float32
BF16 = mybir.dt.bfloat16
AF = mybir.ActivationFunctionType
ALU = mybir.AluOpType
AX = mybir.AxisListType

NCORES = 8
DIM = 1024
SEQ = 2048
NHC = 8
NTB = 2
TB = 1024
E = 16
EG = 4
NG = 4
I_EXP = 512
LNEPS = 1e-5

OFF_ONES = 0
OFF_ID = 128
OFF_UNS = 256
OFF_SELB = 384
OFF_STK0 = OFF_SELB + 2048
OFF_A = OFF_STK0 + 1024
OFF_AT = OFF_A + 16
CBW = OFF_AT + 16

CF_BCB = 0
CF_BIN = 16
CF_BGATE = 24
CF_BOUT = 32
CF_EPS = 40
CF_LNC = 41
CFW = 48

STAGES = "full"
DBG_ROUT = False


class Prog:
    ENGS = ("t", "v", "s", "g", "y")

    def __init__(self, nc):
        self.nc = nc
        self.streams = {e: [] for e in self.ENGS}
        self.cnt = {"T": 0, "V": 0, "S": 0, "DW": 0, "DS": 0}

    def _op(self, eng, fn, waits, inc, n):
        for s, v in waits:
            assert v <= self.cnt[s], f"wait {s}>={v} emitted={self.cnt[s]}"
        if inc is not None:
            self.cnt[inc] += n
        self.streams[eng].append((tuple(waits), fn, inc, n))
        return self.cnt[inc] if inc is not None else None

    def t(self, fn, waits=()):
        return self._op("t", fn, waits, "T", 1)

    def v(self, fn, waits=()):
        return self._op("v", fn, waits, "V", 1)

    def vw(self, fn, waits=()):
        w = list(waits) + [("V", self.cnt["V"])]
        return self._op("v", fn, w, "V", 1)

    def s(self, fn, waits=()):
        return self._op("s", fn, waits, "S", 1)

    def g(self, fn, waits=()):
        return self._op("g", fn, waits, "DW", 16)

    def y(self, fn, waits=()):
        return self._op("y", fn, waits, "DS", 16)

    def ywait(self, waits):
        self.streams["y"].append((tuple(waits), None, None, 0))

    def emit(self, block, sems):
        def mk(eng_key, deco):
            stream = self.streams[eng_key]

            def body(eng):
                seen = {k: 0 for k in self.cnt}
                for waits, fn, inc, n in stream:
                    for s, v in waits:
                        if v > seen[s]:
                            eng.wait_ge(sems[s], v)
                            seen[s] = v
                    if fn is None:
                        continue
                    inst = fn(eng)
                    if inc is not None:
                        inst.then_inc(sems[inc], n)
            deco(body)

        mk("t", block.tensor)
        mk("v", block.vector)
        mk("s", block.scalar)
        mk("g", block.gpsimd)
        mk("y", block.sync)


def build_program():
    nc = bass.Bass("TRN2", target_bir_lowering=False)

    d_x = nc.dram_tensor("xt", [DIM, SEQ], F32, kind="ExternalInput").ap()
    d_w1 = nc.dram_tensor("w1t", [E, 128, 8 * I_EXP], F32, kind="ExternalInput").ap()
    d_w2g = nc.dram_tensor("w2g", [NG, 128, 16 * DIM], F32, kind="ExternalInput").ap()
    d_ws1 = nc.dram_tensor("ws1t", [128, 8 * DIM], F32, kind="ExternalInput").ap()
    d_ws2 = nc.dram_tensor("ws2t", [128, 8 * DIM], F32, kind="ExternalInput").ap()
    d_win = nc.dram_tensor("wint", [128, 8 * DIM], F32, kind="ExternalInput").ap()
    d_wgate = nc.dram_tensor("wgatet", [128, 8 * DIM], F32, kind="ExternalInput").ap()
    d_wout = nc.dram_tensor("woutt", [128, 8 * DIM], F32, kind="ExternalInput").ap()
    d_wg = nc.dram_tensor("wgt", [128, 8 * 16], F32, kind="ExternalInput").ap()
    d_bm = nc.dram_tensor("bmt", [128, 8 * 16], F32, kind="ExternalInput").ap()
    d_cm = nc.dram_tensor("cmt", [16, DIM], F32, kind="ExternalInput").ap()
    d_cb = nc.dram_tensor("cb", [128, CBW], F32, kind="ExternalInput").ap()
    d_cf = nc.dram_tensor("cf", [128, CFW], F32, kind="ExternalInput").ap()
    d_out = nc.dram_tensor("yt", [DIM, SEQ], F32, kind="ExternalOutput").ap()

    from contextlib import ExitStack
    ctx = ExitStack()
    Eb = ctx.enter_context

    X = Eb(nc.sbuf_tensor("X", [128, NHC, SEQ], F32))
    XLN = Eb(nc.sbuf_tensor("XLN", [128, NHC, TB], BF16))
    HID = Eb(nc.sbuf_tensor("HID", [128, 16, TB], BF16))
    W1B = Eb(nc.sbuf_tensor("W1B", [128, 2, 8, I_EXP], BF16))
    W2GB = Eb(nc.sbuf_tensor("W2GB", [128, 16, DIM], BF16))
    WS1B = Eb(nc.sbuf_tensor("WS1B", [128, 8, DIM], BF16))
    CB = Eb(nc.sbuf_tensor("CB", [128, CBW], BF16))
    CF = Eb(nc.sbuf_tensor("CF", [128, CFW], F32))
    WGT = Eb(nc.sbuf_tensor("WGT", [128, 8, 16], BF16))
    BMT = Eb(nc.sbuf_tensor("BMT", [128, 8, 16], BF16))
    CMT = Eb(nc.sbuf_tensor("CMT", [16, DIM], BF16))
    MB = Eb(nc.sbuf_tensor("MB", [128, TB], BF16))
    RB = Eb(nc.sbuf_tensor("RB", [128, TB], BF16))
    SQS = Eb(nc.sbuf_tensor("SQS", [128, TB], BF16))
    CWB = Eb(nc.sbuf_tensor("CWB", [128, 2, 512], BF16))
    CWT = Eb(nc.sbuf_tensor("CWT", [16, TB], BF16))
    GSC = Eb(nc.sbuf_tensor("GSC", [128, 2, 512], BF16))
    ARN = Eb(nc.sbuf_tensor("ARN", [128, 592], F32))
    CW16 = Eb(nc.sbuf_tensor("CW16", [128, 128], BF16))
    POW = Eb(nc.sbuf_tensor("POW", [16, 26 * 16], BF16))
    KER1 = Eb(nc.sbuf_tensor("KER1", [128, 128], BF16))
    KER2 = Eb(nc.sbuf_tensor("KER2", [128, 128], BF16))
    KER3 = Eb(nc.sbuf_tensor("KER3", [128, 128], BF16))
    B1 = Eb(nc.sbuf_tensor("B1", [128, 256], BF16))
    S1SB = Eb(nc.sbuf_tensor("S1SB", [128, 256], BF16))
    B2 = Eb(nc.sbuf_tensor("B2", [128, 32], BF16))
    S2SB = Eb(nc.sbuf_tensor("S2SB", [128, 32], BF16))
    B3 = Eb(nc.sbuf_tensor("B3", [128, 8], BF16))
    S3SB = Eb(nc.sbuf_tensor("S3SB", [128, 8], BF16))
    E3 = Eb(nc.sbuf_tensor("E3", [16, 8], BF16))
    GC = Eb(nc.sbuf_tensor("GC", [16, 8], BF16))
    CIN3 = Eb(nc.sbuf_tensor("CIN3", [16, 8], BF16))
    CIN2 = Eb(nc.sbuf_tensor("CIN2", [16, 32], BF16))
    CIN1 = Eb(nc.sbuf_tensor("CIN1", [16, 256], BF16))
    UHS = Eb(nc.sbuf_tensor("UHS", [16, SEQ], BF16))

    pA = Eb(nc.psum_tensor("pA", [128, 512], F32))
    pB = Eb(nc.psum_tensor("pB", [128, 512], F32))
    pC = Eb(nc.psum_tensor("pC", [128, 512], F32))
    pD = Eb(nc.psum_tensor("pD", [128, 512], F32))
    pE = Eb(nc.psum_tensor("pE", [128, 1024], F32))
    pT = Eb(nc.psum_tensor("pT", [16, 1024], BF16))

    sem_t = Eb(nc.semaphore("semT"))
    sem_v = Eb(nc.semaphore("semV"))
    sem_s = Eb(nc.semaphore("semS"))
    sem_dw = Eb(nc.semaphore("semDW"))
    sem_ds = Eb(nc.semaphore("semDS"))
    sems = {"T": sem_t, "V": sem_v, "S": sem_s, "DW": sem_dw, "DS": sem_ds}

    p = Prog(nc)
    guard = {}

    def twait(*keys):
        return [guard[k] for k in keys if k in guard]

    ONES = CB[:, OFF_ONES:OFF_ONES + 128]
    IDENT = CB[:, OFF_ID:OFF_ID + 128]
    ID16 = CB[0:16, OFF_ID:OFF_ID + 16]
    A_ = CB[0:16, OFF_A:OFF_A + 16]
    AT_ = CB[0:16, OFF_AT:OFF_AT + 16]

    def UNSj(j):
        return CB[:, OFF_UNS + 16 * j: OFF_UNS + 16 * (j + 1)]

    def SELBe(e):
        return CB[0:16, OFF_SELB + 128 * e: OFF_SELB + 128 * (e + 1)]

    def STK0l(l):
        return CB[0:16, OFF_STK0 + 128 * l: OFF_STK0 + 128 * (l + 1)]

    def pow_slot(i):
        return POW[0:16, 16 * i:16 * (i + 1)]

    xv = d_x.rearrange("(c p) t -> p c t", p=128)
    p.y(lambda e: e.dma_start(out=X[:, :, 0:TB], in_=xv[:, :, 0:TB]))
    dsX0 = p.cnt["DS"]
    p.y(lambda e: e.dma_start(out=X[:, :, TB:SEQ], in_=xv[:, :, TB:SEQ]))
    dsX1 = p.cnt["DS"]
    p.y(lambda e: e.dma_start(out=CF[:, :], in_=d_cf))
    dsConst = p.cnt["DS"]
    p.g(lambda e: e.dma_start(out=CB[:, :], in_=d_cb))
    p.g(lambda e: e.dma_start(out=WGT[:, :, :].rearrange("p a b -> p (a b)"),
                              in_=d_wg))
    p.g(lambda e: e.dma_start(out=BMT[:, :, :].rearrange("p a b -> p (a b)"),
                              in_=d_bm))
    p.g(lambda e: e.dma_start(out=CMT[:, :], in_=d_cm))
    dwConst = p.cnt["DW"]

    w1_ready = [0, 0]
    state = {"w2g_ready": 0, "ws1_ready": 0, "wgate_ready": 0,
             "wout_ready": 0, "win_ready": 0, "sPowAll": 0, "vPowT": 0}
    kmarks = []

    def load_w1(e_, slot, wait_t):
        p.g(lambda eng: eng.dma_start(
            out=W1B[:, slot, :, :].rearrange("p a b -> p (a b)"), in_=d_w1[e_]),
            waits=[("T", wait_t)] if wait_t else [])
        w1_ready[slot] = p.cnt["DW"]

    def load_w2g(gidx, wait_t):
        p.g(lambda eng: eng.dma_start(
            out=W2GB[:, 0:8, :].rearrange("p a b -> p (a b)"),
            in_=d_w2g[gidx][:, 0:8 * DIM]),
            waits=[("T", wait_t)] if wait_t else [])
        p.g(lambda eng: eng.dma_start(
            out=W2GB[:, 8:16, :].rearrange("p a b -> p (a b)"),
            in_=d_w2g[gidx][:, 8 * DIM:16 * DIM]))
        state["w2g_ready"] = p.cnt["DW"]

    def load_into_w2gb(dram, half, wait_t):
        lo = 8 * half
        p.g(lambda eng: eng.dma_start(
            out=W2GB[:, lo:lo + 8, :].rearrange("p a b -> p (a b)"), in_=dram),
            waits=[("T", wait_t)] if wait_t else [])
        state["w2g_ready"] = p.cnt["DW"]
        return p.cnt["DW"]

    def load_ws1(dram, wait_t):
        p.g(lambda eng: eng.dma_start(
            out=WS1B[:, 0:4, :].rearrange("p a b -> p (a b)"),
            in_=dram[:, 0:4 * DIM]),
            waits=[("T", wait_t)] if wait_t else [])
        state["ws1_half"] = p.cnt["DW"]
        p.g(lambda eng: eng.dma_start(
            out=WS1B[:, 4:8, :].rearrange("p a b -> p (a b)"),
            in_=dram[:, 4 * DIM:8 * DIM]))
        state["ws1_ready"] = p.cnt["DW"]

    load_ws1(d_ws1, 0)
    load_into_w2gb(d_ws2, 0, 0)
    load_w1(0, 0, 0)
    load_w1(1, 1, 0)

    p.v(lambda e: e.memset(CIN3[:, :], 0.0))
    p.v(lambda e: e.memset(CIN2[:, :], 0.0))
    p.v(lambda e: e.memset(CIN1[:, :], 0.0))
    vInitCin = p.cnt["V"]

    cwb_free = [0, 0]
    gsc_free = [0, 0]
    dbg_i = [0]

    def make_lr(tb, waitX, do_routing, xln_free, sq_vec=False):
        out = {}
        st_ = {}

        def A0():
            st_["sCopyT"] = [0, 0]
            for tt in range(2):
                p.s(lambda e, tt=tt: e.copy(
                    XLN[:, :, tt * 512:(tt + 1) * 512],
                    X[:, :, tb * TB + tt * 512:tb * TB + (tt + 1) * 512]),
                    waits=(list(waitX) +
                           ([("T", xln_free)] if xln_free else [])
                           if tt == 0 else []))
                st_["sCopyT"][tt] = p.cnt["S"]
            st_["sCopy"] = p.cnt["S"]

        def A():
            for tt in range(2):
                for hc in range(NHC):
                    w = ([("S", st_["sCopyT"][0]), ("DW", dwConst)]
                         + twait("pE0", "pE1") if tt == 0 and hc == 0 else
                         ([("S", st_["sCopyT"][1])]
                          if tt == 1 and hc == 0 else []))
                    p.t(lambda e, tt=tt, hc=hc: e.matmul(
                        pE[:, tt * 512:(tt + 1) * 512], ONES,
                        XLN[:, hc, tt * 512:(tt + 1) * 512],
                        start=(hc == 0), stop=(hc == NHC - 1)), waits=w)
            tSum = p.cnt["T"]
            p.s(lambda e: e.activation(MB[:, :], pE[:, :], AF.Copy,
                                       scale=1.0 / DIM), waits=[("T", tSum)])
            st_["sMb"] = p.cnt["S"]
            guard["pE0"] = ("S", st_["sMb"])
            guard["pE1"] = ("S", st_["sMb"])
            p.v(lambda e: e.tensor_tensor(
                out=RB[:, :], in0=MB[:, :], in1=MB[:, :], op=ALU.mult),
                waits=[("S", st_["sMb"])])
            st_["vMsq"] = p.cnt["V"]

        def B_hc(hc):
            w = [("S", st_["sCopy"])]
            if st_.get("pair_mark"):
                w.append(("T", st_["pair_mark"]))
            if sq_vec:
                p.v(lambda e, hc=hc: e.tensor_tensor(
                    out=SQS[:, :], in0=XLN[:, hc, :], in1=XLN[:, hc, :],
                    op=ALU.mult), waits=w)
                sq_w = ("V", p.cnt["V"])
            else:
                p.s(lambda e, hc=hc: e.activation(
                    SQS[:, :], XLN[:, hc, :], AF.Square), waits=w)
                sq_w = ("S", p.cnt["S"])
            for tt in range(2):
                w2 = [sq_w]
                if hc == 0 and tt == 0:
                    w2 += [("V", st_["vMsq"])]
                p.t(lambda e, tt=tt, hc=hc: e.matmul(
                    pE[:, tt * 512:(tt + 1) * 512], ONES,
                    SQS[:, tt * 512:(tt + 1) * 512],
                    start=(hc == 0), stop=(hc == NHC - 1)), waits=w2)
            st_["pair_mark"] = p.cnt["T"]

        def Btail():
            tSum2 = p.cnt["T"]
            st_["tSum2"] = tSum2
            p.vw(lambda e: e.scalar_tensor_tensor(
                out=RB[:, :], in0=RB[:, :], scalar=-float(DIM), in1=pE[:, :],
                op0=ALU.mult, op1=ALU.add), waits=[("T", st_["tSum2"])])
            vVar = p.cnt["V"]
            guard["pE0"] = ("V", vVar)
            guard["pE1"] = ("V", vVar)
            p.s(lambda e: e.activation(RB[:, :], RB[:, :], AF.Ln,
                                       bias=CF[:, CF_EPS:CF_EPS + 1],
                                       scale=1.0),
                waits=[("V", vVar), ("DS", dsConst)])
            p.s(lambda e: e.activation(RB[:, :], RB[:, :], AF.Exp,
                                       bias=CF[:, CF_LNC:CF_LNC + 1],
                                       scale=-0.5))
            sR = p.cnt["S"]
            out["vXlnT"] = [0, 0]
            for tt in range(2):
                sl = slice(tt * 512, (tt + 1) * 512)
                p.vw(lambda e, sl=sl: e.tensor_tensor(
                    out=XLN[:, :, sl], in0=XLN[:, :, sl],
                    in1=MB[:, sl].rearrange("p (a t) -> p a t", a=1)
                        .to_broadcast([128, 8, 512]), op=ALU.subtract),
                    waits=[("S", st_["sMb"]), ("T", st_["tSum2"])])
                p.vw(lambda e, sl=sl: e.tensor_tensor(
                    out=XLN[:, :, sl], in0=XLN[:, :, sl],
                    in1=RB[:, sl].rearrange("p (a t) -> p a t", a=1)
                        .to_broadcast([128, 8, 512]), op=ALU.mult),
                    waits=[("S", sR)])
                out["vXlnT"][tt] = p.cnt["V"]
            out["vXln"] = p.cnt["V"]

        def C():
            if do_routing:
                routing_emit(out["vXln"], st_, out.get("vXlnT"))

        def D():
            if not do_routing:
                out["sCwt"] = 0
                return
            vCw = st_["vCw"]
            for tc in range(8):
                w = [("V", vCw)] + (twait("pT") if tc == 0 else [])
                p.t(lambda e, tc=tc: e.transpose(
                    pT[0:16, tc * 128:(tc + 1) * 128],
                    CW16[:, tc * 16:(tc + 1) * 16], IDENT), waits=w)
            tTr = p.cnt["T"]
            p.s(lambda e: e.copy(CWT[0:16, :], pT[0:16, :]),
                waits=[("T", tTr)] + twait("CWT"))
            out["sCwt"] = p.cnt["S"]
            guard["pT"] = ("S", out["sCwt"])

        stages = {"A0": A0, "A": A, "Bh": B_hc, "Bt": Btail,
                  "C": C, "D": D}
        return stages, out

    def routing_emit(vXln, st_, vXlnT=None):
        for tc in range(8):
            for hc in range(NHC):
                w = []
                if tc == 0 and hc == 0:
                    w = [("V", vXlnT[0] if vXlnT else vXln),
                         ("DW", dwConst)] + twait("pC")
                elif tc == 4 and hc == 0 and vXlnT:
                    w = [("V", vXlnT[1])]
                p.t(lambda e, tc=tc, hc=hc: e.matmul(
                    pC[:, tc * 16:(tc + 1) * 16],
                    XLN[:, hc, tc * 128:(tc + 1) * 128], WGT[:, hc, :],
                    start=(hc == 0), stop=(hc == NHC - 1)), waits=w)
        tGate = p.cnt["T"]
        SCO = ARN[:, 0:128]
        SCH = ARN[:, 128:256]
        MSK = ARN[:, 256:384]
        WW = ARN[:, 384:512]
        GRP = ARN[:, 512:544]
        G2 = ARN[:, 544:576]
        TH = ARN[:, 576:584]
        RCP = ARN[:, 584:592]
        p.s(lambda e: e.activation(SCO, pC[:, 0:128], AF.Sigmoid),
            waits=[("T", tGate)])
        sSco = p.cnt["S"]
        guard["pC"] = ("S", sSco)

        def v3(ap, a, b):
            return ap.rearrange("p (a b) -> p a b", b=b)

        def dump(ap, width=128):
            if not DBG_ROUT:
                return
            i = dbg_i[0]
            dbg_i[0] += 1
            row = 6 + i // 16
            col = (i % 16) * 128
            p.vw(lambda e, ap=ap, row=row, col=col, width=width: e.tensor_copy(
                X[:, row, col:col + width], ap))

        p.vw(lambda e: e.tensor_tensor(
            out=v3(SCH, 8, 16), in0=v3(SCO, 8, 16),
            in1=CF[:, CF_BCB:CF_BCB + 16].rearrange("p (a b) -> p a b", a=1)
                .to_broadcast([128, 8, 16]), op=ALU.add),
            waits=[("S", sSco), ("DS", dsConst)])
        dump(SCH)
        p.vw(lambda e: e.tensor_reduce(
            out=GRP, in_=v3(SCH, 32, 4), axis=AX.X, op=ALU.max))
        dump(GRP, 32)
        p.vw(lambda e: e.tensor_tensor(
            out=v3(MSK, 32, 4), in0=v3(SCH, 32, 4),
            in1=v3(GRP, 32, 1).to_broadcast([128, 32, 4]), op=ALU.is_equal))
        p.vw(lambda e: e.scalar_tensor_tensor(
            out=WW, in0=MSK, scalar=-1e30, in1=SCH, op0=ALU.mult, op1=ALU.add))
        p.vw(lambda e: e.tensor_reduce(
            out=G2, in_=v3(WW, 32, 4), axis=AX.X, op=ALU.max))
        p.vw(lambda e: e.tensor_tensor(out=GRP, in0=GRP, in1=G2, op=ALU.add))
        p.vw(lambda e: e.tensor_reduce(
            out=TH, in_=v3(GRP, 8, 4), axis=AX.X, op=ALU.max))
        p.vw(lambda e: e.tensor_tensor(
            out=v3(G2, 8, 4), in0=v3(GRP, 8, 4),
            in1=v3(TH, 8, 1).to_broadcast([128, 8, 4]), op=ALU.is_equal))
        p.vw(lambda e: e.scalar_tensor_tensor(
            out=G2, in0=G2, scalar=-1e30, in1=GRP, op0=ALU.mult, op1=ALU.add))
        p.vw(lambda e: e.tensor_reduce(
            out=TH, in_=v3(G2, 8, 4), axis=AX.X, op=ALU.max))
        p.vw(lambda e: e.tensor_tensor(
            out=v3(G2, 8, 4), in0=v3(GRP, 8, 4),
            in1=v3(TH, 8, 1).to_broadcast([128, 8, 4]), op=ALU.is_ge))
        p.vw(lambda e: e.tensor_tensor(
            out=v3(MSK, 32, 4), in0=v3(SCH, 32, 4),
            in1=v3(G2, 32, 1).to_broadcast([128, 32, 4]), op=ALU.mult))
        dump(MSK)
        p.vw(lambda e: e.tensor_copy(WW, MSK))
        for k in range(4):
            p.vw(lambda e: e.tensor_reduce(
                out=TH, in_=v3(WW, 8, 16), axis=AX.X, op=ALU.max))
            if k < 3:
                p.vw(lambda e: e.tensor_tensor(
                    out=v3(SCH, 8, 16), in0=v3(WW, 8, 16),
                    in1=v3(TH, 8, 1).to_broadcast([128, 8, 16]),
                    op=ALU.is_equal))
                p.vw(lambda e: e.scalar_tensor_tensor(
                    out=WW, in0=SCH, scalar=-1e30, in1=WW,
                    op0=ALU.mult, op1=ALU.add))
        dump(TH, 8)
        p.vw(lambda e: e.tensor_tensor(
            out=v3(WW, 8, 16), in0=v3(MSK, 8, 16),
            in1=v3(TH, 8, 1).to_broadcast([128, 8, 16]), op=ALU.is_ge))
        p.vw(lambda e: e.tensor_tensor(out=WW, in0=SCO, in1=WW, op=ALU.mult))
        dump(WW)
        p.vw(lambda e: e.tensor_reduce(
            out=TH, in_=v3(WW, 8, 16), axis=AX.X, op=ALU.add))
        p.vw(lambda e: e.tensor_scalar(
            out=TH, in0=TH, scalar1=1e-20, scalar2=None, op0=ALU.add))
        p.vw(lambda e: e.reciprocal(RCP, TH))
        p.vw(lambda e: e.scalar_tensor_tensor(
            out=v3(CW16[:, :], 8, 16), in0=v3(WW, 8, 16), scalar=2.5,
            in1=v3(RCP, 8, 1).to_broadcast([128, 8, 16]),
            op0=ALU.mult, op1=ALU.mult))
        st_["vCw"] = p.cnt["V"]

    def ffn_body(tb, lrout_src, lr_spec, pending_d=None,
             pending_d2=None, pending_head=None):
        pool = [(pA, "pA"), (pB, "pB"), (pD, "pD")]
        pidx = [0]

        def next_tile():
            t_ = pool[pidx[0] % 3]
            pidx[0] += 1
            return t_

        w1_marks = [0] * E
        hid_mark = [[0, 0] for _ in range(16)]
        marks = {}
        lr_out = None
        for gi in range(NG + 1):
            shared = (gi == 0)
            g = gi - 1
            if shared and pending_head is not None:
                pending_head()
            experts = [None] if shared else list(range(g * EG, (g + 1) * EG))
            for e_ in experts:
                cw_marks = [0, 0]
                if not shared:
                    slot = e_ % 2
                    sCwt = lrout_src["sCwt"]
                    for tt in range(2):
                        tile, tname = next_tile()
                        p.t(lambda en, tile=tile, e_=e_, tt=tt: en.matmul(
                            tile[:, :], SELBe(e_),
                            CWT[0:16, tt * 512:(tt + 1) * 512],
                            start=True, stop=True),
                            waits=[("S", sCwt)] + twait(tname))
                        tCwb = p.cnt["T"]
                        wv = [("T", tCwb)]
                        if cwb_free[tt]:
                            wv.append(("V", cwb_free[tt]))
                        p.s(lambda en, tile=tile, tt=tt: en.copy(
                            CWB[:, tt, :], tile[:, :]), waits=wv)
                        guard[tname] = ("S", p.cnt["S"])
                        cw_marks[tt] = p.cnt["S"]
                nch = 8 if shared else 4
                for ic in range(nch):
                    for tt in range(2):
                        tile, tname = next_tile()
                        for hc in range(NHC):
                            w = []
                            if hc == 0:
                                w = twait(tname) + [
                                    ("V", lrout_src["vXlnT"][tt])]
                                w.append(("DW", state["ws1_half"] if shared
                                          else w1_ready[slot]))
                            elif hc == 4 and shared:
                                w = [("DW", state["ws1_ready"])]
                            lhs = (WS1B[:, hc, ic * 128:(ic + 1) * 128]
                                   if shared else
                                   W1B[:, slot, hc, ic * 128:(ic + 1) * 128])
                            p.t(lambda en, lhs=lhs, tile=tile, tt=tt, hc=hc:
                                en.matmul(tile[:, :], lhs,
                                          XLN[:, hc, tt * 512:(tt + 1) * 512],
                                          start=(hc == 0),
                                          stop=(hc == NHC - 1)), waits=w)
                        tFill = p.cnt["T"]
                        k = ic if shared else 4 * (e_ - g * EG) + ic
                        hs = HID[:, k, tt * 512:(tt + 1) * 512]
                        p.s(lambda en, hs=hs, tile=tile: en.activation(
                            hs, tile[:, :], AF.Relu), waits=[("T", tFill)])
                        sRelu = p.cnt["S"]
                        guard[tname] = ("S", sRelu)
                        p.v(lambda en, hs=hs: en.tensor_tensor(
                            out=hs, in0=hs, in1=hs, op=ALU.mult),
                            waits=[("S", sRelu)])
                        if not shared:
                            p.vw(lambda en, hs=hs, tt=tt: en.tensor_tensor(
                                out=hs, in0=hs, in1=CWB[:, tt, :],
                                op=ALU.mult), waits=[("S", cw_marks[tt])])
                        hid_mark[k][tt] = p.cnt["V"]
                if not shared:
                    w1_marks[e_] = p.cnt["T"]
                    cwb_free[0] = p.cnt["V"]
                    cwb_free[1] = p.cnt["V"]
                    nxt = e_ + 2
                    if nxt < E:
                        load_w1(nxt, nxt % 2, w1_marks[e_])
            if gi == 0 and pending_d is not None:
                pending_d()
                pending_d = None
            last_group = (gi == NG)
            if last_group:
                marks["t_xln_free"] = p.cnt["T"]
                guard["CWT"] = ("T", p.cnt["T"])
            nk = 8 if shared else 16
            lr_stages = None
            if last_group and lr_spec is not None:
                tb2, waitX2, do_rout2 = lr_spec
                lr_stages, lr_out = make_lr(tb2, waitX2, do_rout2,
                                            marks["t_xln_free"])
            tile_i = 0
            for oc in range(8):
                for tt in range(2):
                    if lr_stages is not None:
                        if tile_i == 0:
                            lr_stages["A0"]()
                        elif tile_i == 2:
                            lr_stages["A"]()
                        elif 3 <= tile_i <= 10:
                            lr_stages["Bh"](tile_i - 3)
                        elif tile_i == 11:
                            lr_stages["Bt"]()
                        elif tile_i == 13:
                            lr_stages["C"]()
                    tile_i += 1
                    tile, tname = next_tile()
                    for k in range(nk):
                        w = [("V", hid_mark[k][tt])]
                        if k == 0:
                            w += twait(tname) + [("DW", state["w2g_ready"])]
                        p.t(lambda en, tile=tile, k=k, oc=oc, tt=tt: en.matmul(
                            tile[:, :], W2GB[:, k, oc * 128:(oc + 1) * 128],
                            HID[:, k, tt * 512:(tt + 1) * 512],
                            start=(k == 0), stop=(k == nk - 1)), waits=w)
                    tFill = p.cnt["T"]
                    p.v(lambda en, tile=tile, oc=oc, tt=tt:
                        en.scalar_tensor_tensor(
                            out=X[:, oc, tb * TB + tt * 512:
                                  tb * TB + (tt + 1) * 512],
                            in0=tile[:, :], scalar=0.5,
                            in1=X[:, oc, tb * TB + tt * 512:
                                  tb * TB + (tt + 1) * 512],
                            op0=ALU.mult, op1=ALU.add), waits=[("T", tFill)])
                    guard[tname] = ("V", p.cnt["V"])
            tW2 = p.cnt["T"]
            if gi == 0 and pending_d2 is not None:
                pending_d2()
                pending_d2 = None
            if gi < NG:
                load_w2g(gi, tW2)
        marks["vX"] = p.cnt["V"]
        marks["tEnd"] = p.cnt["T"]
        if lr_stages is not None:
            marks["lr_d"] = lr_stages["D"]
        return marks, lr_out

    def attn_body(lra0_out, waitX_t1, lr_next_spec, pending_d=None):
        marks = {}
        vXlnT_cur = lra0_out["vXlnT"]
        ub_mark_s = [0, 0]
        u_mark = [[0, 0] for _ in range(8)]
        for tb in range(NTB):
            pool = [(pA, "pA"), (pB, "pB"), (pC, "pC"), (pD, "pD")]
            pi = [0]

            def nt():
                t_ = pool[pi[0] % 4]
                pi[0] += 1
                return t_

            for oc in range(8):
                for tt in range(2):
                    if pending_d is not None and (oc, tt) == (1, 0) and tb == 0:
                        pending_d()
                        pending_d = None
                    tileX, tnX = nt()
                    tileG, tnG = nt()
                    W1F = W1B[:, :, :, :].rearrange("p a b c -> p (a b c)")
                    for hc in range(NHC):
                        w = []
                        if hc == 0:
                            w = twait(tnX) + [("V", vXlnT_cur[tt]),
                                              ("DW", state["win_ready"])]
                        p.t(lambda en, W1F=W1F, oc=oc, tt=tt, hc=hc,
                            tileX=tileX:
                            en.matmul(tileX[:, :],
                                      W1F[:, hc * 1024 + oc * 128:
                                          hc * 1024 + (oc + 1) * 128],
                                      XLN[:, hc, tt * 512:(tt + 1) * 512],
                                      start=(hc == 0), stop=(hc == NHC - 1)),
                            waits=w)
                    tXin = p.cnt["T"]
                    for hc in range(NHC):
                        w = []
                        if hc == 0:
                            w = twait(tnG) + [("DW", state["wgate_ready"])]
                        p.t(lambda en, tileG=tileG, oc=oc, tt=tt, hc=hc:
                            en.matmul(tileG[:, :],
                                      W2GB[:, hc, oc * 128:(oc + 1) * 128],
                                      XLN[:, hc, tt * 512:(tt + 1) * 512],
                                      start=(hc == 0), stop=(hc == NHC - 1)),
                            waits=w)
                    tGat = p.cnt["T"]
                    hs = HID[:, oc, tt * 512:(tt + 1) * 512]
                    p.s(lambda en, hs=hs, tileX=tileX, oc=oc: en.activation(
                        hs, tileX[:, :], AF.Identity,
                        bias=CF[:, CF_BIN + oc:CF_BIN + oc + 1]),
                        waits=[("T", tXin), ("DS", dsConst)])
                    guard[tnX] = ("S", p.cnt["S"])
                    gpar = (oc * 2 + tt) % 2
                    wv = [("T", tGat)]
                    if gsc_free[gpar]:
                        wv.append(("V", gsc_free[gpar]))
                    p.s(lambda en, tileG=tileG, oc=oc, gpar=gpar: en.activation(
                        GSC[:, gpar, :], tileG[:, :], AF.Sigmoid,
                        bias=CF[:, CF_BGATE + oc:CF_BGATE + oc + 1]), waits=wv)
                    sG = p.cnt["S"]
                    guard[tnG] = ("S", sG)
                    p.v(lambda en, hs=hs, gpar=gpar: en.tensor_tensor(
                        out=hs, in0=hs, in1=GSC[:, gpar, :], op=ALU.mult),
                        waits=[("S", sG)])
                    gsc_free[gpar] = p.cnt["V"]
                    u_mark[oc][tt] = p.cnt["V"]
            vU = p.cnt["V"]
            if tb == 0:
                marks["t_a_xln0"] = p.cnt["T"]
                lr_t1, lr_t1_out = make_lr(1, waitX_t1, False,
                                           marks["t_a_xln0"], sq_vec=True)
                marks["lr_t1_out"] = lr_t1_out
                lr_t1["A0"]()
            else:
                marks["t_xln_free"] = p.cnt["T"]
            for tt in range(2):
                for hc in range(NHC):
                    w = [("V", u_mark[hc][tt])]
                    if hc == 0:
                        w += twait("pC") + [("DW", dwConst)]
                    p.t(lambda en, tt=tt, hc=hc: en.matmul(
                        pC[0:16, :], BMT[:, hc, :],
                        HID[:, hc, tt * 512:(tt + 1) * 512],
                        start=(hc == 0), stop=(hc == NHC - 1)), waits=w)
                tUb = p.cnt["T"]
                p.s(lambda en, tb=tb, tt=tt: en.copy(
                    UHS[0:16, tb * TB + tt * 512:tb * TB + (tt + 1) * 512],
                    pC[0:16, :]), waits=[("T", tUb)])
                guard["pC"] = ("S", p.cnt["S"])
            ub_mark_s[tb] = p.cnt["S"]

            if tb == 0:
                lr_t1["A"]()
                p.v(lambda e: e.tensor_copy(pow_slot(0), AT_),
                    waits=[("DW", dwConst)])
                vAT = p.cnt["V"]
                ppool = [(pC, "pC"), (pD, "pD")]
                par = [0]

                def npp():
                    t_ = ppool[par[0] % 2]
                    par[0] += 1
                    return t_

                spow = [0]

                def chain(lhs_fn, base, first_waits):
                    for k in range(1, 8):
                        tile, tn = npp()
                        w = (list(first_waits) if k == 1 else
                             [("V", spow[0])]) + twait(tn)
                        p.t(lambda e, tile=tile, k=k, base=base: e.matmul(
                            tile[0:16, 0:16], lhs_fn(),
                            pow_slot(base + k - 1), start=True, stop=True),
                            waits=w)
                        tPk = p.cnt["T"]
                        p.v(lambda e, tile=tile, k=k, base=base: e.tensor_copy(
                            pow_slot(base + k), tile[0:16, 0:16]),
                            waits=[("T", tPk)])
                        spow[0] = p.cnt["V"]
                        guard[tn] = ("V", spow[0])

                chain(lambda: A_, 0, [("V", vAT), ("DW", dwConst)])
                lr_t1["Bh"](0)
                lr_t1["Bh"](1)
                p.t(lambda e: e.transpose(pT[0:16, 0:16], pow_slot(7), ID16),
                    waits=[("V", spow[0])] + twait("pT"))
                tP = p.cnt["T"]
                p.v(lambda e: e.tensor_copy(pow_slot(24), pT[0:16, 0:16]),
                    waits=[("T", tP)])
                spow[0] = p.cnt["V"]
                guard["pT"] = ("V", spow[0])
                p.vw(lambda e: e.tensor_copy(pow_slot(8), pow_slot(7)))
                vC = p.cnt["V"]
                lr_t1["Bh"](2)
                lr_t1["Bh"](3)
                chain(lambda: pow_slot(24), 8, [("V", vC)])
                lr_t1["Bh"](4)
                lr_t1["Bh"](5)
                p.t(lambda e: e.transpose(pT[0:16, 16:32], pow_slot(15), ID16),
                    waits=[("V", spow[0])] + twait("pT"))
                tP = p.cnt["T"]
                p.v(lambda e: e.tensor_copy(pow_slot(25), pT[0:16, 16:32]),
                    waits=[("T", tP)])
                spow[0] = p.cnt["V"]
                guard["pT"] = ("V", spow[0])
                p.vw(lambda e: e.tensor_copy(pow_slot(16), pow_slot(15)))
                vC = p.cnt["V"]
                chain(lambda: pow_slot(25), 16, [("V", vC)])
                state["sPowAll"] = spow[0]
                lr_t1["Bh"](6)
                lr_t1["Bh"](7)
                lr_t1["Bt"]()

                def build_ker(K, src0):
                    first = True
                    for j in range(8):
                        for i in range(j + 1):
                            d = j - i
                            rhs = ID16 if d == 0 else pow_slot(src0 + d - 1)
                            w = ([("V", state["sPowAll"])] + twait("pD")
                                 if first else [])
                            first = False
                            p.t(lambda e, j=j, i=i, rhs=rhs: e.matmul(
                                pD[:, 16 * j:16 * (j + 1)], STK0l(i), rhs,
                                start=(i == 0), stop=(i == j)), waits=w)
                    tKer = p.cnt["T"]
                    p.v(lambda e, K=K: e.tensor_copy(K[:, :], pD[:, 0:128]),
                        waits=[("T", tKer)])
                    sK = p.cnt["V"]
                    guard["pD"] = ("V", sK)
                    return sK

                kmarks.clear()
                kmarks.append(build_ker(KER1, 0))
                kmarks.append(build_ker(KER2, 8))
                kmarks.append(build_ker(KER3, 16))
                ARNB = ARN[:, :].bitcast(BF16)
                for idx, src0 in enumerate((0, 8, 16)):
                    for k in range(8):
                        p.vw(lambda e, idx=idx, src0=src0, k=k: e.tensor_copy(
                            ARNB[0:16, idx * 128 + 16 * k:
                                 idx * 128 + 16 * (k + 1)],
                            pow_slot(src0 + k)),
                            waits=[("V", state["sPowAll"])])
                state["vPowT"] = p.cnt["V"]
                for l in range(8):
                    w = ([("S", ub_mark_s[0]), ("DW", dwConst)]
                         + twait("pE0") if l == 0 else [])
                    p.t(lambda e, l=l: e.matmul(
                        pE[:, 0:128], STK0l(l), UHS[0:16, l::8][:, 0:128],
                        start=(l == 0), stop=(l == 7)), waits=w)
                tB1a = p.cnt["T"]
                p.s(lambda e: e.copy(B1[:, 0:128], pE[:, 0:128]),
                    waits=[("T", tB1a)])
                sB1a = p.cnt["S"]
                p.t(lambda e: e.matmul(pE[:, 512:640], KER1[:, :],
                                       B1[:, 0:128], start=True, stop=False,
                                       skip_group_check=True),
                    waits=[("V", kmarks[0]), ("S", sB1a)] + twait("pE1"))
                tS1a = p.cnt["T"]
                p.s(lambda e: e.copy(S1SB[:, 0:128], pE[:, 512:640]),
                    waits=[("T", tS1a)])
                sS1a = p.cnt["S"]
                p.t(lambda e: e.matmul(pC[0:16, 0:128], UNSj(7),
                                       S1SB[:, 0:128], start=True, stop=True),
                    waits=[("S", sS1a)] + twait("pC"))
                tE1a = p.cnt["T"]
                p.s(lambda e: e.copy(UHS[0:16, 0:128],
                                     pC[0:16, 0:128]), waits=[("T", tE1a)])
                sE1a = p.cnt["S"]
                guard["pC"] = ("S", sE1a)
                for l in range(8):
                    w = ([("S", sE1a)] + twait("pD") if l == 0 else [])
                    p.t(lambda e, l=l: e.matmul(
                        pD[:, 32:48], STK0l(l),
                        UHS[0:16, 0:128][:, l::8][:, 0:16],
                        start=(l == 0), stop=(l == 7)), waits=w)
                tB2a = p.cnt["T"]
                p.s(lambda e: e.copy(B2[:, 0:16], pD[:, 32:48]),
                    waits=[("T", tB2a)])
                sB2a = p.cnt["S"]
                guard["pD"] = ("S", sB2a)
                p.t(lambda e: e.matmul(pE[:, 768:784], KER2[:, :],
                                       B2[:, 0:16], start=False, stop=False,
                                       skip_group_check=True),
                    waits=[("V", kmarks[1]), ("S", sB2a)])
                state["tS2a"] = p.cnt["T"]
                state["sB1a"] = sB1a
                vXlnT_cur = lr_t1_out["vXlnT"]

        ARNB = ARN[:, :].bitcast(BF16)

        def POWT(i):
            return ARNB[0:16, i * 128:(i + 1) * 128]

        for l in range(8):
            w = ([("S", ub_mark_s[1])] if l == 0 else [])
            p.t(lambda e, l=l: e.matmul(
                pE[:, 128:256], STK0l(l), UHS[0:16, l::8][:, 128:256],
                start=(l == 0), stop=(l == 7)), waits=w)
        tB1 = p.cnt["T"]
        p.s(lambda e: e.copy(B1[:, 128:256], pE[:, 128:256]),
            waits=[("T", tB1)])
        sB1 = p.cnt["S"]
        guard["pE0"] = ("S", sB1)
        p.t(lambda e: e.matmul(pE[:, 640:768], KER1[:, :], B1[:, 128:256],
                               start=False, stop=False, skip_group_check=True),
            waits=[("S", sB1)])
        tS1 = p.cnt["T"]
        p.s(lambda e: e.copy(S1SB[:, 128:256], pE[:, 640:768]),
            waits=[("T", tS1)])
        sS1 = p.cnt["S"]
        p.t(lambda e: e.matmul(pC[0:16, 128:256], UNSj(7), S1SB[:, 128:256],
                               start=True, stop=True),
            waits=[("S", sS1)] + twait("pC"))
        tE1 = p.cnt["T"]
        p.s(lambda e: e.copy(B1[0:16, 128:256], pC[0:16, 128:256]),
            waits=[("T", tE1)])
        sE1 = p.cnt["S"]
        guard["pC"] = ("S", sE1)
        for l in range(8):
            w = ([("S", sE1)] + twait("pD") if l == 0 else [])
            p.t(lambda e, l=l: e.matmul(
                pD[:, 0:16], STK0l(l), B1[0:16, 128:256][:, l::8][:, 0:16],
                start=(l == 0), stop=(l == 7)), waits=w)
        tB2 = p.cnt["T"]
        p.s(lambda e: e.copy(B2[:, 16:32], pD[:, 0:16]), waits=[("T", tB2)])
        sB2 = p.cnt["S"]
        guard["pD"] = ("S", sB2)
        p.t(lambda e: e.matmul(pE[:, 784:800], KER2[:, :], B2[:, 16:32],
                               start=False, stop=False, skip_group_check=True),
            waits=[("S", sB2)])
        tS2 = p.cnt["T"]
        p.s(lambda e: e.copy(S2SB[:, :], pE[:, 768:800]), waits=[("T", tS2)])
        sS2 = p.cnt["S"]
        p.t(lambda e: e.matmul(pC[0:16, 256:288], UNSj(7), S2SB[:, :],
                               start=True, stop=True),
            waits=[("S", sS2)] + twait("pC"))
        tE2 = p.cnt["T"]
        p.s(lambda e: e.copy(B1[0:16, 0:32], pC[0:16, 256:288]),
            waits=[("T", tE2)])
        sE2 = p.cnt["S"]
        guard["pC"] = ("S", sE2)
        for l in range(8):
            w = ([("S", sE2)] + twait("pD") if l == 0 else [])
            p.t(lambda e, l=l: e.matmul(
                pD[:, 64:68], STK0l(l), B1[0:16, l::8][:, 0:4],
                start=(l == 0), stop=(l == 7)), waits=w)
        tB3 = p.cnt["T"]
        p.s(lambda e: e.copy(B3[:, 0:4], pD[:, 64:68]), waits=[("T", tB3)])
        sB3 = p.cnt["S"]
        guard["pD"] = ("S", sB3)
        p.t(lambda e: e.matmul(pB[:, 0:4], KER3[:, :], B3[:, 0:4],
                               start=True, stop=False, skip_group_check=True),
            waits=[("V", kmarks[2]), ("S", sB3)] + twait("pB"))
        tS3 = p.cnt["T"]
        p.s(lambda e: e.copy(S3SB[:, 0:4], pB[:, 0:4]), waits=[("T", tS3)])
        sS3 = p.cnt["S"]
        p.t(lambda e: e.matmul(pC[0:16, 288:292], UNSj(7), S3SB[:, 0:4],
                               start=True, stop=True),
            waits=[("S", sS3)] + twait("pC"))
        tE3 = p.cnt["T"]
        p.s(lambda e: e.copy(E3[0:16, 0:4], pC[0:16, 288:292]),
            waits=[("T", tE3)])
        sE3 = p.cnt["S"]
        guard["pC"] = ("S", sE3)
        p.v(lambda e: e.tensor_copy(GC[0:16, 0:1], E3[0:16, 0:1]),
            waits=[("S", sE3)])
        vG = p.cnt["V"]
        for c in range(1, 4):
            p.t(lambda e, c=c: e.matmul(
                pC[0:16, 296 + c:297 + c], pow_slot(23), GC[0:16, c - 1:c],
                start=True, stop=True),
                waits=[("V", vG), ("V", state["sPowAll"])])
            tGc = p.cnt["T"]
            p.v(lambda e, c=c: e.tensor_tensor(
                out=GC[0:16, c:c + 1], in0=pC[0:16, 296 + c:297 + c],
                in1=E3[0:16, c:c + 1], op=ALU.add), waits=[("T", tGc)])
            vG = p.cnt["V"]
        guard["pC"] = ("V", vG)
        p.vw(lambda e: e.tensor_copy(CIN3[0:16, 1:4], GC[0:16, 0:3]),
             waits=[("V", vG), ("V", vInitCin)])
        vCin3 = p.cnt["V"]
        p.t(lambda e: e.matmul(pB[:, 0:4], POWT(2), CIN3[0:16, 0:4],
                               start=False, stop=True, skip_group_check=True),
            waits=[("V", vCin3), ("V", state["vPowT"])])
        tC3 = p.cnt["T"]
        p.s(lambda e: e.copy(S3SB[:, 0:4], pB[:, 0:4]), waits=[("T", tC3)])
        sS3t = p.cnt["S"]
        guard["pB"] = ("S", sS3t)
        for l3 in range(8):
            w = ([("S", sS3t)] + twait("pC") if l3 == 0 else [])
            p.t(lambda e, l3=l3: e.matmul(
                pC[0:16, 304 + 4 * l3:304 + 4 * (l3 + 1)], UNSj(l3),
                S3SB[:, 0:4], start=True, stop=True), waits=w)
        tCin2 = p.cnt["T"]
        for l3 in range(8):
            if l3 < 7:
                p.s(lambda e, l3=l3: e.copy(
                    CIN2[0:16, (l3 + 1)::8][:, 0:4],
                    pC[0:16, 304 + 4 * l3:304 + 4 * (l3 + 1)]),
                    waits=[("T", tCin2)] if l3 == 0 else [])
            else:
                p.s(lambda e: e.copy(CIN2[0:16, 8::8][:, 0:3],
                                     pC[0:16, 332:335]))
        sCin2 = p.cnt["S"]
        guard["pC"] = ("S", sCin2)
        p.t(lambda e: e.matmul(pE[:, 768:800], POWT(1), CIN2[0:16, :],
                               start=False, stop=True, skip_group_check=True),
            waits=[("S", sCin2)])
        tC2 = p.cnt["T"]
        p.s(lambda e: e.copy(S2SB[:, :], pE[:, 768:800]), waits=[("T", tC2)])
        sS2t = p.cnt["S"]
        for l2 in range(8):
            w = ([("S", sS2t)] + twait("pC") if l2 == 0 else [])
            p.t(lambda e, l2=l2: e.matmul(
                pC[0:16, 32 * l2:32 * (l2 + 1)], UNSj(l2), S2SB[:, :],
                start=True, stop=True), waits=w)
        tCin1 = p.cnt["T"]
        for l2 in range(8):
            if l2 < 7:
                p.s(lambda e, l2=l2: e.copy(
                    CIN1[0:16, (l2 + 1)::8][:, 0:32],
                    pC[0:16, 32 * l2:32 * (l2 + 1)]),
                    waits=[("T", tCin1)] if l2 == 0 else [])
            else:
                p.s(lambda e: e.copy(CIN1[0:16, 8::8][:, 0:31],
                                     pC[0:16, 224:255]))
        sCin1 = p.cnt["S"]
        guard["pC"] = ("S", sCin1)
        p.t(lambda e: e.matmul(pE[:, 512:768], POWT(0), CIN1[0:16, :],
                               start=False, stop=True, skip_group_check=True),
            waits=[("S", sCin1)])
        tC1 = p.cnt["T"]
        p.s(lambda e: e.copy(S1SB[:, :], pE[:, 512:768]), waits=[("T", tC1)])
        sHst = p.cnt["S"]
        guard["pE1"] = ("S", sHst)
        for j in range(8):
            tile, tn = (pC, "pC") if j % 2 == 0 else (pD, "pD")
            p.t(lambda e, j=j, tile=tile: e.matmul(
                tile[0:16, 256:512], UNSj(j), S1SB[:, :],
                start=True, stop=True), waits=[("S", sHst)] + twait(tn))
            tU = p.cnt["T"]
            if j % 2 == 0:
                p.s(lambda e, j=j, tile=tile: e.copy(
                    UHS[0:16, j::8][:, 0:256], tile[0:16, 256:512]),
                    waits=[("T", tU)])
                guard[tn] = ("S", p.cnt["S"])
                sHs_s = p.cnt["S"]
            else:
                p.v(lambda e, j=j, tile=tile: e.tensor_copy(
                    UHS[0:16, j::8][:, 0:256], tile[0:16, 256:512]),
                    waits=[("T", tU)])
                guard[tn] = ("V", p.cnt["V"])
                sHs_v = p.cnt["V"]
        sHs = ("S", sHs_s)
        sHs2 = ("V", sHs_v)

        pool2 = [(pA, "pA"), (pB, "pB"), (pD, "pD")]
        pj = [0]

        def nt2():
            t_ = pool2[pj[0] % 3]
            pj[0] += 1
            return t_

        lr_next = None
        lr_next_out = None
        for gt in range(4):
            ypar = gt % 2
            if lr_next_spec is not None and gt == 2:
                lr_next["A"]()
            for oc in range(8):
                tile, tn = nt2()
                w = ([sHs, sHs2, ("DW", dwConst)] + twait(tn) +
                     twait("Y%d" % ypar))
                p.t(lambda e, oc=oc, gt=gt, tile=tile: e.matmul(
                    tile[:, :], CMT[0:16, oc * 128:(oc + 1) * 128],
                    UHS[0:16, gt * 512:(gt + 1) * 512], start=True, stop=True),
                    waits=w)
                tY = p.cnt["T"]
                if oc % 2 == 0:
                    p.s(lambda e, oc=oc, ypar=ypar, tile=tile: e.copy(
                        HID[:, 8 + oc, ypar * 512:(ypar + 1) * 512],
                        tile[:, :]), waits=[("T", tY)])
                    guard[tn] = ("S", p.cnt["S"])
                    sY_s = p.cnt["S"]
                else:
                    p.v(lambda e, oc=oc, ypar=ypar, tile=tile: e.tensor_copy(
                        HID[:, 8 + oc, ypar * 512:(ypar + 1) * 512],
                        tile[:, :]), waits=[("T", tY)])
                    guard[tn] = ("V", p.cnt["V"])
                    sY_v = p.cnt["V"]
            sY = ("S", sY_s)
            sY2 = ("V", sY_v)
            for oc in range(8):
                tile, tn = nt2()
                for hc in range(NHC):
                    w = []
                    if hc == 0:
                        w = twait(tn) + [sY, sY2,
                                         ("DW", state["wout_ready"])]
                    p.t(lambda e, oc=oc, hc=hc, ypar=ypar, tile=tile: e.matmul(
                        tile[:, :], W2GB[:, 8 + hc, oc * 128:(oc + 1) * 128],
                        HID[:, 8 + hc, ypar * 512:(ypar + 1) * 512],
                        start=(hc == 0), stop=(hc == NHC - 1)), waits=w)
                tO = p.cnt["T"]
                p.v(lambda e, oc=oc, gt=gt, tile=tile: e.scalar_tensor_tensor(
                    out=X[:, oc, gt * 512:(gt + 1) * 512], in0=tile[:, :],
                    scalar=CF[:, CF_BOUT + oc:CF_BOUT + oc + 1],
                    in1=X[:, oc, gt * 512:(gt + 1) * 512],
                    op0=ALU.add, op1=ALU.add), waits=[("T", tO)])
                guard[tn] = ("V", p.cnt["V"])
                if lr_next is not None and gt == 2:
                    lr_next["Bh"](oc)
                    if oc == 7:
                        lr_next["Bt"]()

            guard["Y%d" % ypar] = ("T", p.cnt["T"])
            if gt == 1 and lr_next_spec is not None:
                vX01 = p.cnt["V"]
                tb2, do_rout2 = lr_next_spec
                lr_next, lr_next_out = make_lr(
                    tb2, [("V", vX01)], do_rout2, marks["t_xln_free"])
                lr_next["A0"]()
        if lr_next is not None:
            lr_next["C"]()
            marks["lr_d"] = lr_next["D"]
        marks["vX"] = p.cnt["V"]
        marks["tEnd"] = p.cnt["T"]
        return marks, lr_next_out

    lr0, lr0_out = make_lr(0, [("DS", dsX0)], True, 0)
    lr0["A0"]()
    lr0["A"]()
    for i_ in range(8):
        lr0["Bh"](i_)
    lr0["Bt"]()

    m_f1t0, lr1_out = ffn_body(0, lr0_out,
                               (1, [("DS", dsX1)], True),
                               pending_head=lr0["C"], pending_d2=lr0["D"])
    load_into_w2gb(d_ws2, 0, m_f1t0["tEnd"])
    load_w1(0, 0, m_f1t0["tEnd"])
    load_w1(1, 1, m_f1t0["tEnd"])

    m_f1t1, lra0_out = ffn_body(1, lr1_out,
                                (0, [("V", m_f1t0["vX"])], False),
                                pending_d=m_f1t0["lr_d"])
    p.g(lambda eng: eng.dma_start(
        out=W1B[:, :, :, :].rearrange("p a b c -> p (a b c)"), in_=d_win),
        waits=[("T", m_f1t1["t_xln_free"])])
    state["win_ready"] = p.cnt["DW"]
    state["wgate_ready"] = load_into_w2gb(d_wgate, 0, m_f1t1["tEnd"])
    state["wout_ready"] = load_into_w2gb(d_wout, 1, m_f1t1["tEnd"])

    m_attn, lrf2_out = attn_body(lra0_out,
                                 [("V", m_f1t1["vX"])], (0, True),
                                 pending_d=m_f1t1["lr_d"])
    load_into_w2gb(d_ws2, 0, m_attn["t_xln_free"])
    load_w1(0, 0, m_attn["t_xln_free"])
    load_w1(1, 1, m_attn["t_xln_free"])

    m_f2t0, lrf2t1_out = ffn_body(0, lrf2_out,
                                  (1, [("V", m_attn["vX"])], True),
                                  pending_d=m_attn["lr_d"])
    load_into_w2gb(d_ws2, 0, m_f2t0["tEnd"])
    load_w1(0, 0, m_f2t0["tEnd"])
    load_w1(1, 1, m_f2t0["tEnd"])

    ov = d_out.rearrange("(c p) t -> p c t", p=128)
    p.y(lambda e: e.dma_start(out=ov[:, :, 0:TB], in_=X[:, :, 0:TB]),
        waits=[("V", m_f2t0["vX"])])

    m_f2t1, _ = ffn_body(1, lrf2t1_out, None,
                         pending_d=m_f2t0["lr_d"])

    p.y(lambda e: e.dma_start(out=ov[:, :, TB:SEQ], in_=X[:, :, TB:SEQ]),
        waits=[("V", m_f2t1["vX"])])
    dsOut = p.cnt["DS"]
    p.ywait([("DS", dsOut)])

    with nc.Block() as block:
        p.emit(block, sems)
    ctx.close()
    return nc


def _host_prepare(inp):
    f32 = np.float32
    x = np.ascontiguousarray(inp["L0_attn_x"], dtype=f32)
    w1 = np.ascontiguousarray(inp["L0_ffn_W1"], dtype=f32)
    w2 = np.ascontiguousarray(inp["L0_ffn_W2"], dtype=f32)
    ws1 = np.ascontiguousarray(inp["L0_ffn_Ws1"], dtype=f32)
    ws2 = np.ascontiguousarray(inp["L0_ffn_Ws2"], dtype=f32)
    wg = np.ascontiguousarray(inp["L0_ffn_Wg"], dtype=f32)
    bias_corr = np.ascontiguousarray(inp["L0_ffn_bias_corr"], dtype=f32)
    win = np.ascontiguousarray(inp["L0_attn_W_in"], dtype=f32)
    b_in = np.ascontiguousarray(inp["L0_attn_b_in"], dtype=f32)
    a = np.ascontiguousarray(inp["L0_attn_A"], dtype=f32)
    bm = np.ascontiguousarray(inp["L0_attn_Bm"], dtype=f32)
    cm = np.ascontiguousarray(inp["L0_attn_Cm"], dtype=f32)
    wgate = np.ascontiguousarray(inp["L0_attn_W_gate"], dtype=f32)
    b_gate = np.ascontiguousarray(inp["L0_attn_b_gate"], dtype=f32)
    wout = np.ascontiguousarray(inp["L0_attn_W_out"], dtype=f32)
    b_out = np.ascontiguousarray(inp["L0_attn_b_out"], dtype=f32)

    def fm_chunks(wT):
        inf, outf = wT.shape
        return np.ascontiguousarray(
            wT.reshape(inf // 128, 128, outf).transpose(1, 0, 2).reshape(128, -1))

    w1t = np.stack([fm_chunks(w1[e].T) for e in range(E)])
    w2g = np.zeros((NG, 128, 16 * DIM), f32)
    for g in range(NG):
        for ei in range(EG):
            w2g[g][:, ei * 4 * DIM:(ei + 1) * 4 * DIM] = fm_chunks(
                w2[g * EG + ei].T)
    ws1t = fm_chunks(ws1.T)
    ws2t = fm_chunks(ws2.T)
    wint = fm_chunks(win[DIM:2 * DIM].T)
    wgatet = fm_chunks(wgate.T)
    woutt = fm_chunks(wout.T)
    wgt = fm_chunks(wg.T)
    bmt = fm_chunks(bm.T)
    cmt = np.ascontiguousarray(cm.T)

    cb = np.zeros((128, CBW), f32)
    cb[:, OFF_ONES:OFF_ONES + 128] = 1.0
    cb[:, OFF_ID:OFF_ID + 128] = np.eye(128)
    for j in range(8):
        blk = np.zeros((128, 16), f32)
        for m in range(16):
            blk[16 * j + m, m] = 1.0
        cb[:, OFF_UNS + 16 * j:OFF_UNS + 16 * (j + 1)] = blk
    for e in range(E):
        blk = np.zeros((16, 128), f32)
        blk[e, :] = 1.0
        cb[0:16, OFF_SELB + 128 * e:OFF_SELB + 128 * (e + 1)] = blk
    for l in range(8):
        blk = np.zeros((16, 128), f32)
        for n in range(16):
            blk[n, 16 * l + n] = 1.0
        cb[0:16, OFF_STK0 + 128 * l:OFF_STK0 + 128 * (l + 1)] = blk
    cb[0:16, OFF_A:OFF_A + 16] = a
    cb[0:16, OFF_AT:OFF_AT + 16] = a.T

    cf = np.zeros((128, CFW), f32)
    cf[:, CF_BCB:CF_BCB + 16] = np.broadcast_to(bias_corr, (128, 16))
    cf[:, CF_BIN:CF_BIN + 8] = b_in[DIM:2 * DIM].reshape(8, 128).T
    cf[:, CF_BGATE:CF_BGATE + 8] = b_gate.reshape(8, 128).T
    cf[:, CF_BOUT:CF_BOUT + 8] = b_out.reshape(8, 128).T
    cf[:, CF_EPS] = DIM * LNEPS
    cf[:, CF_LNC] = np.log(np.sqrt(float(DIM)))

    common = {
        "w1t": w1t, "w2g": w2g, "ws1t": ws1t, "ws2t": ws2t,
        "wint": wint, "wgatet": wgatet, "woutt": woutt,
        "wgt": wgt, "bmt": bmt, "cmt": cmt, "cb": cb, "cf": cf,
    }
    in_maps = []
    for b in range(NCORES):
        m = dict(common)
        m["xt"] = np.ascontiguousarray(x[b].T)
        in_maps.append(m)
    return in_maps


def kernel(**inputs):
    in_maps = _host_prepare(inputs)
    nc = build_program()
    res = run_bass_kernel_spmd(nc, in_maps, core_ids=list(range(NCORES)))
    out = np.empty((NCORES, SEQ, DIM), np.float32)
    for b in range(NCORES):
        out[b] = np.asarray(res.results[b]["yt"], np.float32).T
    return out


# version 15
# speedup vs baseline: 1.0188x; 1.0015x over previous
import numpy as np
import ml_dtypes

import concourse.bass as bass
import concourse.mybir as mybir
from concourse.bass_utils import run_bass_kernel_spmd

F32 = mybir.dt.float32
BF16 = mybir.dt.bfloat16
AF = mybir.ActivationFunctionType
ALU = mybir.AluOpType
AX = mybir.AxisListType

NCORES = 8
DIM = 1024
SEQ = 2048
NHC = 8
NTB = 2
TB = 1024
E = 16
EG = 4
NG = 4
I_EXP = 512
LNEPS = 1e-5

OFF_ONES = 0
OFF_ID = 128
OFF_UNS = 256
OFF_SELB = 384
OFF_STK0 = OFF_SELB + 2048
OFF_A = OFF_STK0 + 1024
OFF_AT = OFF_A + 16
CBW = OFF_AT + 16

CF_BCB = 0
CF_BIN = 16
CF_BGATE = 24
CF_BOUT = 32
CF_EPS = 40
CF_LNC = 41
CFW = 48

STAGES = "full"
DBG_ROUT = False


class Prog:
    ENGS = ("t", "v", "s", "g", "y")

    def __init__(self, nc):
        self.nc = nc
        self.streams = {e: [] for e in self.ENGS}
        self.cnt = {"T": 0, "V": 0, "S": 0, "DW": 0, "DS": 0}

    def _op(self, eng, fn, waits, inc, n):
        for s, v in waits:
            assert v <= self.cnt[s], f"wait {s}>={v} emitted={self.cnt[s]}"
        if inc is not None:
            self.cnt[inc] += n
        self.streams[eng].append((tuple(waits), fn, inc, n))
        return self.cnt[inc] if inc is not None else None

    def t(self, fn, waits=()):
        return self._op("t", fn, waits, "T", 1)

    def v(self, fn, waits=()):
        return self._op("v", fn, waits, "V", 1)

    def vw(self, fn, waits=()):
        w = list(waits) + [("V", self.cnt["V"])]
        return self._op("v", fn, w, "V", 1)

    def s(self, fn, waits=()):
        return self._op("s", fn, waits, "S", 1)

    def g(self, fn, waits=()):
        return self._op("g", fn, waits, "DW", 16)

    def y(self, fn, waits=()):
        return self._op("y", fn, waits, "DS", 16)

    def ywait(self, waits):
        self.streams["y"].append((tuple(waits), None, None, 0))

    def emit(self, block, sems):
        def mk(eng_key, deco):
            stream = self.streams[eng_key]

            def body(eng):
                seen = {k: 0 for k in self.cnt}
                for waits, fn, inc, n in stream:
                    for s, v in waits:
                        if v > seen[s]:
                            eng.wait_ge(sems[s], v)
                            seen[s] = v
                    if fn is None:
                        continue
                    inst = fn(eng)
                    if inc is not None:
                        inst.then_inc(sems[inc], n)
            deco(body)

        mk("t", block.tensor)
        mk("v", block.vector)
        mk("s", block.scalar)
        mk("g", block.gpsimd)
        mk("y", block.sync)


def build_program():
    nc = bass.Bass("TRN2", target_bir_lowering=False)

    d_x = nc.dram_tensor("xt", [DIM, SEQ], F32, kind="ExternalInput").ap()
    d_w1 = nc.dram_tensor("w1t", [E, 128, 8 * I_EXP], F32, kind="ExternalInput").ap()
    d_w2g = nc.dram_tensor("w2g", [NG, 128, 16 * DIM], F32, kind="ExternalInput").ap()
    d_ws1 = nc.dram_tensor("ws1t", [128, 8 * DIM], F32, kind="ExternalInput").ap()
    d_ws2 = nc.dram_tensor("ws2t", [128, 8 * DIM], F32, kind="ExternalInput").ap()
    d_win = nc.dram_tensor("wint", [128, 8 * DIM], F32, kind="ExternalInput").ap()
    d_wgate = nc.dram_tensor("wgatet", [128, 8 * DIM], F32, kind="ExternalInput").ap()
    d_wout = nc.dram_tensor("woutt", [128, 8 * DIM], F32, kind="ExternalInput").ap()
    d_wg = nc.dram_tensor("wgt", [128, 8 * 16], F32, kind="ExternalInput").ap()
    d_bm = nc.dram_tensor("bmt", [128, 8 * 16], F32, kind="ExternalInput").ap()
    d_cm = nc.dram_tensor("cmt", [16, DIM], F32, kind="ExternalInput").ap()
    d_cb = nc.dram_tensor("cb", [128, CBW], F32, kind="ExternalInput").ap()
    d_cf = nc.dram_tensor("cf", [128, CFW], F32, kind="ExternalInput").ap()
    d_out = nc.dram_tensor("yt", [DIM, SEQ], F32, kind="ExternalOutput").ap()

    from contextlib import ExitStack
    ctx = ExitStack()
    Eb = ctx.enter_context

    X = Eb(nc.sbuf_tensor("X", [128, NHC, SEQ], F32))
    XLN = Eb(nc.sbuf_tensor("XLN", [128, NHC, TB], BF16))
    HID = Eb(nc.sbuf_tensor("HID", [128, 16, TB], BF16))
    W1B = Eb(nc.sbuf_tensor("W1B", [128, 2, 8, I_EXP], BF16))
    W2GB = Eb(nc.sbuf_tensor("W2GB", [128, 16, DIM], BF16))
    WS1B = Eb(nc.sbuf_tensor("WS1B", [128, 8, DIM], BF16))
    CB = Eb(nc.sbuf_tensor("CB", [128, CBW], BF16))
    CF = Eb(nc.sbuf_tensor("CF", [128, CFW], F32))
    WGT = Eb(nc.sbuf_tensor("WGT", [128, 8, 16], BF16))
    BMT = Eb(nc.sbuf_tensor("BMT", [128, 8, 16], BF16))
    CMT = Eb(nc.sbuf_tensor("CMT", [16, DIM], BF16))
    MB = Eb(nc.sbuf_tensor("MB", [128, TB], BF16))
    RB = Eb(nc.sbuf_tensor("RB", [128, TB], BF16))
    SQS = Eb(nc.sbuf_tensor("SQS", [128, TB], BF16))
    CWB = Eb(nc.sbuf_tensor("CWB", [128, 2, 512], BF16))
    CWT = Eb(nc.sbuf_tensor("CWT", [16, TB], BF16))
    GSC = Eb(nc.sbuf_tensor("GSC", [128, 2, 512], BF16))
    ARN = Eb(nc.sbuf_tensor("ARN", [128, 592], F32))
    CW16 = Eb(nc.sbuf_tensor("CW16", [128, 128], BF16))
    POW = Eb(nc.sbuf_tensor("POW", [16, 26 * 16], BF16))
    KER1 = Eb(nc.sbuf_tensor("KER1", [128, 128], BF16))
    KER2 = Eb(nc.sbuf_tensor("KER2", [128, 128], BF16))
    KER3 = Eb(nc.sbuf_tensor("KER3", [128, 128], BF16))
    B1 = Eb(nc.sbuf_tensor("B1", [128, 256], BF16))
    S1SB = Eb(nc.sbuf_tensor("S1SB", [128, 256], BF16))
    B2 = Eb(nc.sbuf_tensor("B2", [128, 32], BF16))
    S2SB = Eb(nc.sbuf_tensor("S2SB", [128, 32], BF16))
    B3 = Eb(nc.sbuf_tensor("B3", [128, 8], BF16))
    S3SB = Eb(nc.sbuf_tensor("S3SB", [128, 8], BF16))
    E3 = Eb(nc.sbuf_tensor("E3", [16, 8], BF16))
    GC = Eb(nc.sbuf_tensor("GC", [16, 8], BF16))
    CIN3 = Eb(nc.sbuf_tensor("CIN3", [16, 8], BF16))
    CIN2 = Eb(nc.sbuf_tensor("CIN2", [16, 32], BF16))
    CIN1 = Eb(nc.sbuf_tensor("CIN1", [16, 256], BF16))
    UHS = Eb(nc.sbuf_tensor("UHS", [16, SEQ], BF16))

    pA = Eb(nc.psum_tensor("pA", [128, 512], F32))
    pB = Eb(nc.psum_tensor("pB", [128, 512], F32))
    pC = Eb(nc.psum_tensor("pC", [128, 512], F32))
    pD = Eb(nc.psum_tensor("pD", [128, 512], F32))
    pE = Eb(nc.psum_tensor("pE", [128, 1024], F32))
    pT = Eb(nc.psum_tensor("pT", [16, 1024], BF16))

    sem_t = Eb(nc.semaphore("semT"))
    sem_v = Eb(nc.semaphore("semV"))
    sem_s = Eb(nc.semaphore("semS"))
    sem_dw = Eb(nc.semaphore("semDW"))
    sem_ds = Eb(nc.semaphore("semDS"))
    sems = {"T": sem_t, "V": sem_v, "S": sem_s, "DW": sem_dw, "DS": sem_ds}

    p = Prog(nc)
    guard = {}

    def twait(*keys):
        return [guard[k] for k in keys if k in guard]

    ONES = CB[:, OFF_ONES:OFF_ONES + 128]
    IDENT = CB[:, OFF_ID:OFF_ID + 128]
    ID16 = CB[0:16, OFF_ID:OFF_ID + 16]
    A_ = CB[0:16, OFF_A:OFF_A + 16]
    AT_ = CB[0:16, OFF_AT:OFF_AT + 16]

    def UNSj(j):
        return CB[:, OFF_UNS + 16 * j: OFF_UNS + 16 * (j + 1)]

    def SELBe(e):
        return CB[0:16, OFF_SELB + 128 * e: OFF_SELB + 128 * (e + 1)]

    def STK0l(l):
        return CB[0:16, OFF_STK0 + 128 * l: OFF_STK0 + 128 * (l + 1)]

    def pow_slot(i):
        return POW[0:16, 16 * i:16 * (i + 1)]

    xv = d_x.rearrange("(c p) t -> p c t", p=128)
    p.y(lambda e: e.dma_start(out=X[:, :, 0:TB], in_=xv[:, :, 0:TB]))
    dsX0 = p.cnt["DS"]
    p.y(lambda e: e.dma_start(out=X[:, :, TB:SEQ], in_=xv[:, :, TB:SEQ]))
    dsX1 = p.cnt["DS"]
    p.y(lambda e: e.dma_start(out=CF[:, :], in_=d_cf))
    dsConst = p.cnt["DS"]
    p.g(lambda e: e.dma_start(out=CB[:, :], in_=d_cb))
    p.g(lambda e: e.dma_start(out=WGT[:, :, :].rearrange("p a b -> p (a b)"),
                              in_=d_wg))
    p.g(lambda e: e.dma_start(out=BMT[:, :, :].rearrange("p a b -> p (a b)"),
                              in_=d_bm))
    p.g(lambda e: e.dma_start(out=CMT[:, :], in_=d_cm))
    dwConst = p.cnt["DW"]

    w1_ready = [0, 0]
    state = {"w2g_ready": 0, "ws1_ready": 0, "wgate_ready": 0,
             "wout_ready": 0, "win_ready": 0, "sPowAll": 0, "vPowT": 0}
    kmarks = []

    def load_w1(e_, slot, wait_t):
        p.g(lambda eng: eng.dma_start(
            out=W1B[:, slot, :, :].rearrange("p a b -> p (a b)"), in_=d_w1[e_]),
            waits=[("T", wait_t)] if wait_t else [])
        w1_ready[slot] = p.cnt["DW"]

    def load_w2g(gidx, wait_t):
        p.g(lambda eng: eng.dma_start(
            out=W2GB[:, 0:8, :].rearrange("p a b -> p (a b)"),
            in_=d_w2g[gidx][:, 0:8 * DIM]),
            waits=[("T", wait_t)] if wait_t else [])
        p.g(lambda eng: eng.dma_start(
            out=W2GB[:, 8:16, :].rearrange("p a b -> p (a b)"),
            in_=d_w2g[gidx][:, 8 * DIM:16 * DIM]))
        state["w2g_ready"] = p.cnt["DW"]

    def load_into_w2gb(dram, half, wait_t):
        lo = 8 * half
        p.g(lambda eng: eng.dma_start(
            out=W2GB[:, lo:lo + 8, :].rearrange("p a b -> p (a b)"), in_=dram),
            waits=[("T", wait_t)] if wait_t else [])
        state["w2g_ready"] = p.cnt["DW"]
        return p.cnt["DW"]

    def load_ws1(dram, wait_t):
        p.g(lambda eng: eng.dma_start(
            out=WS1B[:, 0:4, :].rearrange("p a b -> p (a b)"),
            in_=dram[:, 0:4 * DIM]),
            waits=[("T", wait_t)] if wait_t else [])
        state["ws1_half"] = p.cnt["DW"]
        p.g(lambda eng: eng.dma_start(
            out=WS1B[:, 4:8, :].rearrange("p a b -> p (a b)"),
            in_=dram[:, 4 * DIM:8 * DIM]))
        state["ws1_ready"] = p.cnt["DW"]

    load_ws1(d_ws1, 0)
    load_into_w2gb(d_ws2, 0, 0)
    load_w1(0, 0, 0)
    load_w1(1, 1, 0)

    p.v(lambda e: e.memset(CIN3[:, :], 0.0))
    p.v(lambda e: e.memset(CIN2[:, :], 0.0))
    p.v(lambda e: e.memset(CIN1[:, :], 0.0))
    vInitCin = p.cnt["V"]

    cwb_free = [0, 0]
    gsc_free = [0, 0]
    dbg_i = [0]

    def make_lr(tb, waitX, do_routing, xln_free, sq_vec=False):
        out = {}
        st_ = {}

        def A0():
            st_["sCopyT"] = [0, 0]
            for tt in range(2):
                p.s(lambda e, tt=tt: e.copy(
                    XLN[:, :, tt * 512:(tt + 1) * 512],
                    X[:, :, tb * TB + tt * 512:tb * TB + (tt + 1) * 512]),
                    waits=(list(waitX) +
                           ([("T", xln_free)] if xln_free else [])
                           if tt == 0 else []))
                st_["sCopyT"][tt] = p.cnt["S"]
            st_["sCopy"] = p.cnt["S"]

        def A():
            for tt in range(2):
                for hc in range(NHC):
                    w = ([("S", st_["sCopyT"][0]), ("DW", dwConst)]
                         + twait("pE0", "pE1") if tt == 0 and hc == 0 else
                         ([("S", st_["sCopyT"][1])]
                          if tt == 1 and hc == 0 else []))
                    p.t(lambda e, tt=tt, hc=hc: e.matmul(
                        pE[:, tt * 512:(tt + 1) * 512], ONES,
                        XLN[:, hc, tt * 512:(tt + 1) * 512],
                        start=(hc == 0), stop=(hc == NHC - 1)), waits=w)
            tSum = p.cnt["T"]
            p.s(lambda e: e.activation(MB[:, :], pE[:, :], AF.Copy,
                                       scale=1.0 / DIM), waits=[("T", tSum)])
            st_["sMb"] = p.cnt["S"]
            guard["pE0"] = ("S", st_["sMb"])
            guard["pE1"] = ("S", st_["sMb"])
            p.v(lambda e: e.tensor_tensor(
                out=RB[:, :], in0=MB[:, :], in1=MB[:, :], op=ALU.mult),
                waits=[("S", st_["sMb"])])
            st_["vMsq"] = p.cnt["V"]

        def B_hc(hc):
            w = [("S", st_["sCopy"])]
            if st_.get("pair_mark"):
                w.append(("T", st_["pair_mark"]))
            if sq_vec:
                p.v(lambda e, hc=hc: e.tensor_tensor(
                    out=SQS[:, :], in0=XLN[:, hc, :], in1=XLN[:, hc, :],
                    op=ALU.mult), waits=w)
                sq_w = ("V", p.cnt["V"])
            else:
                p.s(lambda e, hc=hc: e.activation(
                    SQS[:, :], XLN[:, hc, :], AF.Square), waits=w)
                sq_w = ("S", p.cnt["S"])
            for tt in range(2):
                w2 = [sq_w]
                if hc == 0 and tt == 0:
                    w2 += [("V", st_["vMsq"])]
                p.t(lambda e, tt=tt, hc=hc: e.matmul(
                    pE[:, tt * 512:(tt + 1) * 512], ONES,
                    SQS[:, tt * 512:(tt + 1) * 512],
                    start=(hc == 0), stop=(hc == NHC - 1)), waits=w2)
            st_["pair_mark"] = p.cnt["T"]

        def Btail():
            tSum2 = p.cnt["T"]
            st_["tSum2"] = tSum2
            p.vw(lambda e: e.scalar_tensor_tensor(
                out=RB[:, :], in0=RB[:, :], scalar=-float(DIM), in1=pE[:, :],
                op0=ALU.mult, op1=ALU.add), waits=[("T", st_["tSum2"])])
            vVar = p.cnt["V"]
            guard["pE0"] = ("V", vVar)
            guard["pE1"] = ("V", vVar)
            p.s(lambda e: e.activation(RB[:, :], RB[:, :], AF.Ln,
                                       bias=CF[:, CF_EPS:CF_EPS + 1],
                                       scale=1.0),
                waits=[("V", vVar), ("DS", dsConst)])
            p.s(lambda e: e.activation(RB[:, :], RB[:, :], AF.Exp,
                                       bias=CF[:, CF_LNC:CF_LNC + 1],
                                       scale=-0.5))
            sR = p.cnt["S"]
            out["vXlnT"] = [0, 0]
            for tt in range(2):
                sl = slice(tt * 512, (tt + 1) * 512)
                p.vw(lambda e, sl=sl: e.tensor_tensor(
                    out=XLN[:, :, sl], in0=XLN[:, :, sl],
                    in1=MB[:, sl].rearrange("p (a t) -> p a t", a=1)
                        .to_broadcast([128, 8, 512]), op=ALU.subtract),
                    waits=[("S", st_["sMb"]), ("T", st_["tSum2"])])
                p.vw(lambda e, sl=sl: e.tensor_tensor(
                    out=XLN[:, :, sl], in0=XLN[:, :, sl],
                    in1=RB[:, sl].rearrange("p (a t) -> p a t", a=1)
                        .to_broadcast([128, 8, 512]), op=ALU.mult),
                    waits=[("S", sR)])
                out["vXlnT"][tt] = p.cnt["V"]
            out["vXln"] = p.cnt["V"]

        def C():
            if do_routing:
                routing_emit(out["vXln"], st_, out.get("vXlnT"))

        def D():
            if not do_routing:
                out["sCwt"] = 0
                return
            vCw = st_["vCw"]
            for tc in range(8):
                w = [("V", vCw)] + (twait("pT") if tc == 0 else [])
                p.t(lambda e, tc=tc: e.transpose(
                    pT[0:16, tc * 128:(tc + 1) * 128],
                    CW16[:, tc * 16:(tc + 1) * 16], IDENT), waits=w)
            tTr = p.cnt["T"]
            p.s(lambda e: e.copy(CWT[0:16, :], pT[0:16, :]),
                waits=[("T", tTr)] + twait("CWT"))
            out["sCwt"] = p.cnt["S"]
            guard["pT"] = ("S", out["sCwt"])

        stages = {"A0": A0, "A": A, "Bh": B_hc, "Bt": Btail,
                  "C": C, "D": D}
        return stages, out

    def routing_emit(vXln, st_, vXlnT=None):
        for tc in range(8):
            for hc in range(NHC):
                w = []
                if tc == 0 and hc == 0:
                    w = [("V", vXlnT[0] if vXlnT else vXln),
                         ("DW", dwConst)] + twait("pC")
                elif tc == 4 and hc == 0 and vXlnT:
                    w = [("V", vXlnT[1])]
                p.t(lambda e, tc=tc, hc=hc: e.matmul(
                    pC[:, tc * 16:(tc + 1) * 16],
                    XLN[:, hc, tc * 128:(tc + 1) * 128], WGT[:, hc, :],
                    start=(hc == 0), stop=(hc == NHC - 1)), waits=w)
        tGate = p.cnt["T"]
        SCO = ARN[:, 0:128]
        SCH = ARN[:, 128:256]
        MSK = ARN[:, 256:384]
        WW = ARN[:, 384:512]
        GRP = ARN[:, 512:544]
        G2 = ARN[:, 544:576]
        TH = ARN[:, 576:584]
        RCP = ARN[:, 584:592]
        p.s(lambda e: e.activation(SCO, pC[:, 0:128], AF.Sigmoid),
            waits=[("T", tGate)])
        sSco = p.cnt["S"]
        guard["pC"] = ("S", sSco)

        def v3(ap, a, b):
            return ap.rearrange("p (a b) -> p a b", b=b)

        def dump(ap, width=128):
            if not DBG_ROUT:
                return
            i = dbg_i[0]
            dbg_i[0] += 1
            row = 6 + i // 16
            col = (i % 16) * 128
            p.vw(lambda e, ap=ap, row=row, col=col, width=width: e.tensor_copy(
                X[:, row, col:col + width], ap))

        p.vw(lambda e: e.tensor_tensor(
            out=v3(SCH, 8, 16), in0=v3(SCO, 8, 16),
            in1=CF[:, CF_BCB:CF_BCB + 16].rearrange("p (a b) -> p a b", a=1)
                .to_broadcast([128, 8, 16]), op=ALU.add),
            waits=[("S", sSco), ("DS", dsConst)])
        dump(SCH)
        p.vw(lambda e: e.tensor_reduce(
            out=GRP, in_=v3(SCH, 32, 4), axis=AX.X, op=ALU.max))
        dump(GRP, 32)
        p.vw(lambda e: e.tensor_tensor(
            out=v3(MSK, 32, 4), in0=v3(SCH, 32, 4),
            in1=v3(GRP, 32, 1).to_broadcast([128, 32, 4]), op=ALU.is_equal))
        p.vw(lambda e: e.scalar_tensor_tensor(
            out=WW, in0=MSK, scalar=-1e30, in1=SCH, op0=ALU.mult, op1=ALU.add))
        p.vw(lambda e: e.tensor_reduce(
            out=G2, in_=v3(WW, 32, 4), axis=AX.X, op=ALU.max))
        p.vw(lambda e: e.tensor_tensor(out=GRP, in0=GRP, in1=G2, op=ALU.add))
        p.vw(lambda e: e.tensor_reduce(
            out=TH, in_=v3(GRP, 8, 4), axis=AX.X, op=ALU.max))
        p.vw(lambda e: e.tensor_tensor(
            out=v3(G2, 8, 4), in0=v3(GRP, 8, 4),
            in1=v3(TH, 8, 1).to_broadcast([128, 8, 4]), op=ALU.is_equal))
        p.vw(lambda e: e.scalar_tensor_tensor(
            out=G2, in0=G2, scalar=-1e30, in1=GRP, op0=ALU.mult, op1=ALU.add))
        p.vw(lambda e: e.tensor_reduce(
            out=TH, in_=v3(G2, 8, 4), axis=AX.X, op=ALU.max))
        p.vw(lambda e: e.tensor_tensor(
            out=v3(G2, 8, 4), in0=v3(GRP, 8, 4),
            in1=v3(TH, 8, 1).to_broadcast([128, 8, 4]), op=ALU.is_ge))
        p.vw(lambda e: e.tensor_tensor(
            out=v3(MSK, 32, 4), in0=v3(SCH, 32, 4),
            in1=v3(G2, 32, 1).to_broadcast([128, 32, 4]), op=ALU.mult))
        dump(MSK)
        p.vw(lambda e: e.tensor_copy(WW, MSK))
        for k in range(4):
            p.vw(lambda e: e.tensor_reduce(
                out=TH, in_=v3(WW, 8, 16), axis=AX.X, op=ALU.max))
            if k < 3:
                p.vw(lambda e: e.tensor_tensor(
                    out=v3(SCH, 8, 16), in0=v3(WW, 8, 16),
                    in1=v3(TH, 8, 1).to_broadcast([128, 8, 16]),
                    op=ALU.is_equal))
                p.vw(lambda e: e.scalar_tensor_tensor(
                    out=WW, in0=SCH, scalar=-1e30, in1=WW,
                    op0=ALU.mult, op1=ALU.add))
        dump(TH, 8)
        p.vw(lambda e: e.tensor_tensor(
            out=v3(WW, 8, 16), in0=v3(MSK, 8, 16),
            in1=v3(TH, 8, 1).to_broadcast([128, 8, 16]), op=ALU.is_ge))
        p.vw(lambda e: e.tensor_tensor(out=WW, in0=SCO, in1=WW, op=ALU.mult))
        dump(WW)
        p.vw(lambda e: e.tensor_reduce(
            out=TH, in_=v3(WW, 8, 16), axis=AX.X, op=ALU.add))
        p.vw(lambda e: e.tensor_scalar(
            out=TH, in0=TH, scalar1=1e-20, scalar2=None, op0=ALU.add))
        p.vw(lambda e: e.reciprocal(RCP, TH))
        p.vw(lambda e: e.scalar_tensor_tensor(
            out=v3(CW16[:, :], 8, 16), in0=v3(WW, 8, 16), scalar=2.5,
            in1=v3(RCP, 8, 1).to_broadcast([128, 8, 16]),
            op0=ALU.mult, op1=ALU.mult))
        st_["vCw"] = p.cnt["V"]

    def ffn_body(tb, lrout_src, lr_spec, pending_d=None,
             pending_d2=None, pending_head=None):
        pool = [(pA, "pA"), (pB, "pB"), (pD, "pD")]
        pidx = [0]

        def next_tile():
            t_ = pool[pidx[0] % 3]
            pidx[0] += 1
            return t_

        w1_marks = [0] * E
        hid_mark = [[0, 0] for _ in range(16)]
        marks = {}
        lr_out = None
        for gi in range(NG + 1):
            shared = (gi == 0)
            g = gi - 1
            if shared and pending_head is not None:
                pending_head()
            experts = [None] if shared else list(range(g * EG, (g + 1) * EG))
            for e_ in experts:
                cw_marks = [0, 0]
                if not shared:
                    slot = e_ % 2
                    sCwt = lrout_src["sCwt"]
                    for tt in range(2):
                        tile, tname = next_tile()
                        p.t(lambda en, tile=tile, e_=e_, tt=tt: en.matmul(
                            tile[:, :], SELBe(e_),
                            CWT[0:16, tt * 512:(tt + 1) * 512],
                            start=True, stop=True),
                            waits=[("S", sCwt)] + twait(tname))
                        tCwb = p.cnt["T"]
                        wv = [("T", tCwb)]
                        if cwb_free[tt]:
                            wv.append(("V", cwb_free[tt]))
                        p.s(lambda en, tile=tile, tt=tt: en.copy(
                            CWB[:, tt, :], tile[:, :]), waits=wv)
                        guard[tname] = ("S", p.cnt["S"])
                        cw_marks[tt] = p.cnt["S"]
                nch = 8 if shared else 4
                for ic in range(nch):
                    for tt in range(2):
                        tile, tname = next_tile()
                        for hc in range(NHC):
                            w = []
                            if hc == 0:
                                w = twait(tname) + [
                                    ("V", lrout_src["vXlnT"][tt])]
                                w.append(("DW", state["ws1_half"] if shared
                                          else w1_ready[slot]))
                            elif hc == 4 and shared:
                                w = [("DW", state["ws1_ready"])]
                            lhs = (WS1B[:, hc, ic * 128:(ic + 1) * 128]
                                   if shared else
                                   W1B[:, slot, hc, ic * 128:(ic + 1) * 128])
                            p.t(lambda en, lhs=lhs, tile=tile, tt=tt, hc=hc:
                                en.matmul(tile[:, :], lhs,
                                          XLN[:, hc, tt * 512:(tt + 1) * 512],
                                          start=(hc == 0),
                                          stop=(hc == NHC - 1)), waits=w)
                        tFill = p.cnt["T"]
                        k = ic if shared else 4 * (e_ - g * EG) + ic
                        hs = HID[:, k, tt * 512:(tt + 1) * 512]
                        p.s(lambda en, hs=hs, tile=tile: en.activation(
                            hs, tile[:, :], AF.Relu), waits=[("T", tFill)])
                        sRelu = p.cnt["S"]
                        guard[tname] = ("S", sRelu)
                        p.v(lambda en, hs=hs: en.tensor_tensor(
                            out=hs, in0=hs, in1=hs, op=ALU.mult),
                            waits=[("S", sRelu)])
                        if not shared:
                            p.vw(lambda en, hs=hs, tt=tt: en.tensor_tensor(
                                out=hs, in0=hs, in1=CWB[:, tt, :],
                                op=ALU.mult), waits=[("S", cw_marks[tt])])
                        hid_mark[k][tt] = p.cnt["V"]
                if not shared:
                    w1_marks[e_] = p.cnt["T"]
                    cwb_free[0] = p.cnt["V"]
                    cwb_free[1] = p.cnt["V"]
                    nxt = e_ + 2
                    if nxt < E:
                        load_w1(nxt, nxt % 2, w1_marks[e_])
            if gi == 0 and pending_d is not None:
                pending_d()
                pending_d = None
            last_group = (gi == NG)
            if last_group:
                marks["t_xln_free"] = p.cnt["T"]
                guard["CWT"] = ("T", p.cnt["T"])
            nk = 8 if shared else 16
            lr_stages = None
            if last_group and lr_spec is not None:
                tb2, waitX2, do_rout2 = lr_spec
                lr_stages, lr_out = make_lr(tb2, waitX2, do_rout2,
                                            marks["t_xln_free"])
            tile_i = 0
            for oc in range(8):
                for tt in range(2):
                    if lr_stages is not None:
                        if tile_i == 0:
                            lr_stages["A0"]()
                        elif tile_i == 2:
                            lr_stages["A"]()
                        elif 3 <= tile_i <= 10:
                            lr_stages["Bh"](tile_i - 3)
                        elif tile_i == 11:
                            lr_stages["Bt"]()
                        elif tile_i == 13:
                            lr_stages["C"]()
                    tile_i += 1
                    tile, tname = next_tile()
                    for k in range(nk):
                        w = [("V", hid_mark[k][tt])]
                        if k == 0:
                            w += twait(tname) + [("DW", state["w2g_ready"])]
                        p.t(lambda en, tile=tile, k=k, oc=oc, tt=tt: en.matmul(
                            tile[:, :], W2GB[:, k, oc * 128:(oc + 1) * 128],
                            HID[:, k, tt * 512:(tt + 1) * 512],
                            start=(k == 0), stop=(k == nk - 1)), waits=w)
                    tFill = p.cnt["T"]
                    p.v(lambda en, tile=tile, oc=oc, tt=tt:
                        en.scalar_tensor_tensor(
                            out=X[:, oc, tb * TB + tt * 512:
                                  tb * TB + (tt + 1) * 512],
                            in0=tile[:, :], scalar=0.5,
                            in1=X[:, oc, tb * TB + tt * 512:
                                  tb * TB + (tt + 1) * 512],
                            op0=ALU.mult, op1=ALU.add), waits=[("T", tFill)])
                    guard[tname] = ("V", p.cnt["V"])
            tW2 = p.cnt["T"]
            if gi == 0 and pending_d2 is not None:
                pending_d2()
                pending_d2 = None
            if gi < NG:
                load_w2g(gi, tW2)
        marks["vX"] = p.cnt["V"]
        marks["tEnd"] = p.cnt["T"]
        if lr_stages is not None:
            marks["lr_d"] = lr_stages["D"]
        return marks, lr_out

    def attn_body(lra0_out, waitX_t1, lr_next_spec, pending_d=None):
        marks = {}
        vXlnT_cur = lra0_out["vXlnT"]
        ub_mark_s = [0, 0]
        u_mark = [[0, 0] for _ in range(8)]
        for tb in range(NTB):
            pool = [(pA, "pA"), (pB, "pB"), (pC, "pC"), (pD, "pD")]
            pi = [0]

            def nt():
                t_ = pool[pi[0] % 4]
                pi[0] += 1
                return t_

            W1F = W1B[:, :, :, :].rearrange("p a b c -> p (a b c)")
            for oc in range(8):
                for tt in range(2):
                    if pending_d is not None and (oc, tt) == (1, 0) and tb == 0:
                        pending_d()
                        pending_d = None
                    tileX, tnX = nt()
                    for hc in range(NHC):
                        w = []
                        if hc == 0:
                            w = twait(tnX) + [("V", vXlnT_cur[tt]),
                                              ("DW", state["win_ready"])]
                        p.t(lambda en, W1F=W1F, oc=oc, tt=tt, hc=hc,
                            tileX=tileX:
                            en.matmul(tileX[:, :],
                                      W1F[:, hc * 1024 + oc * 128:
                                          hc * 1024 + (oc + 1) * 128],
                                      XLN[:, hc, tt * 512:(tt + 1) * 512],
                                      start=(hc == 0), stop=(hc == NHC - 1)),
                            waits=w)
                    tXin = p.cnt["T"]
                    hs = HID[:, oc, tt * 512:(tt + 1) * 512]
                    p.s(lambda en, hs=hs, tileX=tileX, oc=oc: en.activation(
                        hs, tileX[:, :], AF.Identity,
                        bias=CF[:, CF_BIN + oc:CF_BIN + oc + 1]),
                        waits=[("T", tXin), ("DS", dsConst)])
                    guard[tnX] = ("S", p.cnt["S"])
            for oc in range(8):
                for tt in range(2):
                    tileG, tnG = nt()
                    for hc in range(NHC):
                        w = []
                        if hc == 0:
                            w = twait(tnG) + [("DW", state["wgate_ready"])]
                        p.t(lambda en, tileG=tileG, oc=oc, tt=tt, hc=hc:
                            en.matmul(tileG[:, :],
                                      W2GB[:, hc, oc * 128:(oc + 1) * 128],
                                      XLN[:, hc, tt * 512:(tt + 1) * 512],
                                      start=(hc == 0), stop=(hc == NHC - 1)),
                            waits=w)
                    tGat = p.cnt["T"]
                    hs = HID[:, oc, tt * 512:(tt + 1) * 512]
                    gpar = (oc * 2 + tt) % 2
                    wv = [("T", tGat)]
                    if gsc_free[gpar]:
                        wv.append(("V", gsc_free[gpar]))
                    p.s(lambda en, tileG=tileG, oc=oc, gpar=gpar: en.activation(
                        GSC[:, gpar, :], tileG[:, :], AF.Sigmoid,
                        bias=CF[:, CF_BGATE + oc:CF_BGATE + oc + 1]), waits=wv)
                    sG = p.cnt["S"]
                    guard[tnG] = ("S", sG)
                    p.v(lambda en, hs=hs, gpar=gpar: en.tensor_tensor(
                        out=hs, in0=hs, in1=GSC[:, gpar, :], op=ALU.mult),
                        waits=[("S", sG)])
                    gsc_free[gpar] = p.cnt["V"]
                    u_mark[oc][tt] = p.cnt["V"]
            vU = p.cnt["V"]
            if tb == 0:
                marks["t_a_xln0"] = p.cnt["T"]
                lr_t1, lr_t1_out = make_lr(1, waitX_t1, False,
                                           marks["t_a_xln0"], sq_vec=True)
                marks["lr_t1_out"] = lr_t1_out
                lr_t1["A0"]()
            else:
                marks["t_xln_free"] = p.cnt["T"]
            for tt in range(2):
                for hc in range(NHC):
                    w = [("V", u_mark[hc][tt])]
                    if hc == 0:
                        w += twait("pC") + [("DW", dwConst)]
                    p.t(lambda en, tt=tt, hc=hc: en.matmul(
                        pC[0:16, :], BMT[:, hc, :],
                        HID[:, hc, tt * 512:(tt + 1) * 512],
                        start=(hc == 0), stop=(hc == NHC - 1)), waits=w)
                tUb = p.cnt["T"]
                p.s(lambda en, tb=tb, tt=tt: en.copy(
                    UHS[0:16, tb * TB + tt * 512:tb * TB + (tt + 1) * 512],
                    pC[0:16, :]), waits=[("T", tUb)])
                guard["pC"] = ("S", p.cnt["S"])
            ub_mark_s[tb] = p.cnt["S"]

            if tb == 0:
                lr_t1["A"]()
                p.v(lambda e: e.tensor_copy(pow_slot(0), AT_),
                    waits=[("DW", dwConst)])
                vAT = p.cnt["V"]
                ppool = [(pC, "pC"), (pD, "pD")]
                par = [0]

                def npp():
                    t_ = ppool[par[0] % 2]
                    par[0] += 1
                    return t_

                spow = [0]

                def chain(lhs_fn, base, first_waits):
                    for k in range(1, 8):
                        tile, tn = npp()
                        w = (list(first_waits) if k == 1 else
                             [("V", spow[0])]) + twait(tn)
                        p.t(lambda e, tile=tile, k=k, base=base: e.matmul(
                            tile[0:16, 0:16], lhs_fn(),
                            pow_slot(base + k - 1), start=True, stop=True),
                            waits=w)
                        tPk = p.cnt["T"]
                        p.v(lambda e, tile=tile, k=k, base=base: e.tensor_copy(
                            pow_slot(base + k), tile[0:16, 0:16]),
                            waits=[("T", tPk)])
                        spow[0] = p.cnt["V"]
                        guard[tn] = ("V", spow[0])

                chain(lambda: A_, 0, [("V", vAT), ("DW", dwConst)])
                lr_t1["Bh"](0)
                lr_t1["Bh"](1)
                p.t(lambda e: e.transpose(pT[0:16, 0:16], pow_slot(7), ID16),
                    waits=[("V", spow[0])] + twait("pT"))
                tP = p.cnt["T"]
                p.v(lambda e: e.tensor_copy(pow_slot(24), pT[0:16, 0:16]),
                    waits=[("T", tP)])
                spow[0] = p.cnt["V"]
                guard["pT"] = ("V", spow[0])
                p.vw(lambda e: e.tensor_copy(pow_slot(8), pow_slot(7)))
                vC = p.cnt["V"]
                lr_t1["Bh"](2)
                lr_t1["Bh"](3)
                chain(lambda: pow_slot(24), 8, [("V", vC)])
                lr_t1["Bh"](4)
                lr_t1["Bh"](5)
                p.t(lambda e: e.transpose(pT[0:16, 16:32], pow_slot(15), ID16),
                    waits=[("V", spow[0])] + twait("pT"))
                tP = p.cnt["T"]
                p.v(lambda e: e.tensor_copy(pow_slot(25), pT[0:16, 16:32]),
                    waits=[("T", tP)])
                spow[0] = p.cnt["V"]
                guard["pT"] = ("V", spow[0])
                p.vw(lambda e: e.tensor_copy(pow_slot(16), pow_slot(15)))
                vC = p.cnt["V"]
                chain(lambda: pow_slot(25), 16, [("V", vC)])
                state["sPowAll"] = spow[0]
                lr_t1["Bh"](6)
                lr_t1["Bh"](7)
                lr_t1["Bt"]()

                def build_ker(K, src0):
                    first = True
                    for j in range(8):
                        for i in range(j + 1):
                            d = j - i
                            rhs = ID16 if d == 0 else pow_slot(src0 + d - 1)
                            w = ([("V", state["sPowAll"])] + twait("pD")
                                 if first else [])
                            first = False
                            p.t(lambda e, j=j, i=i, rhs=rhs: e.matmul(
                                pD[:, 16 * j:16 * (j + 1)], STK0l(i), rhs,
                                start=(i == 0), stop=(i == j)), waits=w)
                    tKer = p.cnt["T"]
                    p.v(lambda e, K=K: e.tensor_copy(K[:, :], pD[:, 0:128]),
                        waits=[("T", tKer)])
                    sK = p.cnt["V"]
                    guard["pD"] = ("V", sK)
                    return sK

                kmarks.clear()
                kmarks.append(build_ker(KER1, 0))
                kmarks.append(build_ker(KER2, 8))
                kmarks.append(build_ker(KER3, 16))
                ARNB = ARN[:, :].bitcast(BF16)
                for idx, src0 in enumerate((0, 8, 16)):
                    for k in range(8):
                        p.vw(lambda e, idx=idx, src0=src0, k=k: e.tensor_copy(
                            ARNB[0:16, idx * 128 + 16 * k:
                                 idx * 128 + 16 * (k + 1)],
                            pow_slot(src0 + k)),
                            waits=[("V", state["sPowAll"])])
                state["vPowT"] = p.cnt["V"]
                for l in range(8):
                    w = ([("S", ub_mark_s[0]), ("DW", dwConst)]
                         + twait("pE0") if l == 0 else [])
                    p.t(lambda e, l=l: e.matmul(
                        pE[:, 0:128], STK0l(l), UHS[0:16, l::8][:, 0:128],
                        start=(l == 0), stop=(l == 7)), waits=w)
                tB1a = p.cnt["T"]
                p.s(lambda e: e.copy(B1[:, 0:128], pE[:, 0:128]),
                    waits=[("T", tB1a)])
                sB1a = p.cnt["S"]
                p.t(lambda e: e.matmul(pE[:, 512:640], KER1[:, :],
                                       B1[:, 0:128], start=True, stop=False,
                                       skip_group_check=True),
                    waits=[("V", kmarks[0]), ("S", sB1a)] + twait("pE1"))
                tS1a = p.cnt["T"]
                p.s(lambda e: e.copy(S1SB[:, 0:128], pE[:, 512:640]),
                    waits=[("T", tS1a)])
                sS1a = p.cnt["S"]
                p.t(lambda e: e.matmul(pC[0:16, 0:128], UNSj(7),
                                       S1SB[:, 0:128], start=True, stop=True),
                    waits=[("S", sS1a)] + twait("pC"))
                tE1a = p.cnt["T"]
                p.s(lambda e: e.copy(UHS[0:16, 0:128],
                                     pC[0:16, 0:128]), waits=[("T", tE1a)])
                sE1a = p.cnt["S"]
                guard["pC"] = ("S", sE1a)
                for l in range(8):
                    w = ([("S", sE1a)] + twait("pD") if l == 0 else [])
                    p.t(lambda e, l=l: e.matmul(
                        pD[:, 32:48], STK0l(l),
                        UHS[0:16, 0:128][:, l::8][:, 0:16],
                        start=(l == 0), stop=(l == 7)), waits=w)
                tB2a = p.cnt["T"]
                p.s(lambda e: e.copy(B2[:, 0:16], pD[:, 32:48]),
                    waits=[("T", tB2a)])
                sB2a = p.cnt["S"]
                guard["pD"] = ("S", sB2a)
                p.t(lambda e: e.matmul(pE[:, 768:784], KER2[:, :],
                                       B2[:, 0:16], start=False, stop=False,
                                       skip_group_check=True),
                    waits=[("V", kmarks[1]), ("S", sB2a)])
                state["tS2a"] = p.cnt["T"]
                state["sB1a"] = sB1a
                vXlnT_cur = lr_t1_out["vXlnT"]

        ARNB = ARN[:, :].bitcast(BF16)

        def POWT(i):
            return ARNB[0:16, i * 128:(i + 1) * 128]

        for l in range(8):
            w = ([("S", ub_mark_s[1])] if l == 0 else [])
            p.t(lambda e, l=l: e.matmul(
                pE[:, 128:256], STK0l(l), UHS[0:16, l::8][:, 128:256],
                start=(l == 0), stop=(l == 7)), waits=w)
        tB1 = p.cnt["T"]
        p.s(lambda e: e.copy(B1[:, 128:256], pE[:, 128:256]),
            waits=[("T", tB1)])
        sB1 = p.cnt["S"]
        guard["pE0"] = ("S", sB1)
        p.t(lambda e: e.matmul(pE[:, 640:768], KER1[:, :], B1[:, 128:256],
                               start=False, stop=False, skip_group_check=True),
            waits=[("S", sB1)])
        tS1 = p.cnt["T"]
        p.s(lambda e: e.copy(S1SB[:, 128:256], pE[:, 640:768]),
            waits=[("T", tS1)])
        sS1 = p.cnt["S"]
        p.t(lambda e: e.matmul(pC[0:16, 128:256], UNSj(7), S1SB[:, 128:256],
                               start=True, stop=True),
            waits=[("S", sS1)] + twait("pC"))
        tE1 = p.cnt["T"]
        p.s(lambda e: e.copy(B1[0:16, 128:256], pC[0:16, 128:256]),
            waits=[("T", tE1)])
        sE1 = p.cnt["S"]
        guard["pC"] = ("S", sE1)
        for l in range(8):
            w = ([("S", sE1)] + twait("pD") if l == 0 else [])
            p.t(lambda e, l=l: e.matmul(
                pD[:, 0:16], STK0l(l), B1[0:16, 128:256][:, l::8][:, 0:16],
                start=(l == 0), stop=(l == 7)), waits=w)
        tB2 = p.cnt["T"]
        p.s(lambda e: e.copy(B2[:, 16:32], pD[:, 0:16]), waits=[("T", tB2)])
        sB2 = p.cnt["S"]
        guard["pD"] = ("S", sB2)
        p.t(lambda e: e.matmul(pE[:, 784:800], KER2[:, :], B2[:, 16:32],
                               start=False, stop=False, skip_group_check=True),
            waits=[("S", sB2)])
        tS2 = p.cnt["T"]
        p.s(lambda e: e.copy(S2SB[:, :], pE[:, 768:800]), waits=[("T", tS2)])
        sS2 = p.cnt["S"]
        p.t(lambda e: e.matmul(pC[0:16, 256:288], UNSj(7), S2SB[:, :],
                               start=True, stop=True),
            waits=[("S", sS2)] + twait("pC"))
        tE2 = p.cnt["T"]
        p.s(lambda e: e.copy(B1[0:16, 0:32], pC[0:16, 256:288]),
            waits=[("T", tE2)])
        sE2 = p.cnt["S"]
        guard["pC"] = ("S", sE2)
        for l in range(8):
            w = ([("S", sE2)] + twait("pD") if l == 0 else [])
            p.t(lambda e, l=l: e.matmul(
                pD[:, 64:68], STK0l(l), B1[0:16, l::8][:, 0:4],
                start=(l == 0), stop=(l == 7)), waits=w)
        tB3 = p.cnt["T"]
        p.s(lambda e: e.copy(B3[:, 0:4], pD[:, 64:68]), waits=[("T", tB3)])
        sB3 = p.cnt["S"]
        guard["pD"] = ("S", sB3)
        p.t(lambda e: e.matmul(pB[:, 0:4], KER3[:, :], B3[:, 0:4],
                               start=True, stop=False, skip_group_check=True),
            waits=[("V", kmarks[2]), ("S", sB3)] + twait("pB"))
        tS3 = p.cnt["T"]
        p.s(lambda e: e.copy(S3SB[:, 0:4], pB[:, 0:4]), waits=[("T", tS3)])
        sS3 = p.cnt["S"]
        p.t(lambda e: e.matmul(pC[0:16, 288:292], UNSj(7), S3SB[:, 0:4],
                               start=True, stop=True),
            waits=[("S", sS3)] + twait("pC"))
        tE3 = p.cnt["T"]
        p.s(lambda e: e.copy(E3[0:16, 0:4], pC[0:16, 288:292]),
            waits=[("T", tE3)])
        sE3 = p.cnt["S"]
        guard["pC"] = ("S", sE3)
        p.v(lambda e: e.tensor_copy(GC[0:16, 0:1], E3[0:16, 0:1]),
            waits=[("S", sE3)])
        vG = p.cnt["V"]
        for c in range(1, 4):
            p.t(lambda e, c=c: e.matmul(
                pC[0:16, 296 + c:297 + c], pow_slot(23), GC[0:16, c - 1:c],
                start=True, stop=True),
                waits=[("V", vG), ("V", state["sPowAll"])])
            tGc = p.cnt["T"]
            p.v(lambda e, c=c: e.tensor_tensor(
                out=GC[0:16, c:c + 1], in0=pC[0:16, 296 + c:297 + c],
                in1=E3[0:16, c:c + 1], op=ALU.add), waits=[("T", tGc)])
            vG = p.cnt["V"]
        guard["pC"] = ("V", vG)
        p.vw(lambda e: e.tensor_copy(CIN3[0:16, 1:4], GC[0:16, 0:3]),
             waits=[("V", vG), ("V", vInitCin)])
        vCin3 = p.cnt["V"]
        p.t(lambda e: e.matmul(pB[:, 0:4], POWT(2), CIN3[0:16, 0:4],
                               start=False, stop=True, skip_group_check=True),
            waits=[("V", vCin3), ("V", state["vPowT"])])
        tC3 = p.cnt["T"]
        p.s(lambda e: e.copy(S3SB[:, 0:4], pB[:, 0:4]), waits=[("T", tC3)])
        sS3t = p.cnt["S"]
        guard["pB"] = ("S", sS3t)
        for l3 in range(8):
            w = ([("S", sS3t)] + twait("pC") if l3 == 0 else [])
            p.t(lambda e, l3=l3: e.matmul(
                pC[0:16, 304 + 4 * l3:304 + 4 * (l3 + 1)], UNSj(l3),
                S3SB[:, 0:4], start=True, stop=True), waits=w)
        tCin2 = p.cnt["T"]
        for l3 in range(8):
            if l3 < 7:
                p.s(lambda e, l3=l3: e.copy(
                    CIN2[0:16, (l3 + 1)::8][:, 0:4],
                    pC[0:16, 304 + 4 * l3:304 + 4 * (l3 + 1)]),
                    waits=[("T", tCin2)] if l3 == 0 else [])
            else:
                p.s(lambda e: e.copy(CIN2[0:16, 8::8][:, 0:3],
                                     pC[0:16, 332:335]))
        sCin2 = p.cnt["S"]
        guard["pC"] = ("S", sCin2)
        p.t(lambda e: e.matmul(pE[:, 768:800], POWT(1), CIN2[0:16, :],
                               start=False, stop=True, skip_group_check=True),
            waits=[("S", sCin2)])
        tC2 = p.cnt["T"]
        p.s(lambda e: e.copy(S2SB[:, :], pE[:, 768:800]), waits=[("T", tC2)])
        sS2t = p.cnt["S"]
        for l2 in range(8):
            w = ([("S", sS2t)] + twait("pC") if l2 == 0 else [])
            p.t(lambda e, l2=l2: e.matmul(
                pC[0:16, 32 * l2:32 * (l2 + 1)], UNSj(l2), S2SB[:, :],
                start=True, stop=True), waits=w)
        tCin1 = p.cnt["T"]
        for l2 in range(8):
            if l2 < 7:
                p.s(lambda e, l2=l2: e.copy(
                    CIN1[0:16, (l2 + 1)::8][:, 0:32],
                    pC[0:16, 32 * l2:32 * (l2 + 1)]),
                    waits=[("T", tCin1)] if l2 == 0 else [])
            else:
                p.s(lambda e: e.copy(CIN1[0:16, 8::8][:, 0:31],
                                     pC[0:16, 224:255]))
        sCin1 = p.cnt["S"]
        guard["pC"] = ("S", sCin1)
        p.t(lambda e: e.matmul(pE[:, 512:768], POWT(0), CIN1[0:16, :],
                               start=False, stop=True, skip_group_check=True),
            waits=[("S", sCin1)])
        tC1 = p.cnt["T"]
        p.s(lambda e: e.copy(S1SB[:, :], pE[:, 512:768]), waits=[("T", tC1)])
        sHst = p.cnt["S"]
        guard["pE1"] = ("S", sHst)
        for j in range(8):
            tile, tn = (pC, "pC") if j % 2 == 0 else (pD, "pD")
            p.t(lambda e, j=j, tile=tile: e.matmul(
                tile[0:16, 256:512], UNSj(j), S1SB[:, :],
                start=True, stop=True), waits=[("S", sHst)] + twait(tn))
            tU = p.cnt["T"]
            if j % 2 == 0:
                p.s(lambda e, j=j, tile=tile: e.copy(
                    UHS[0:16, j::8][:, 0:256], tile[0:16, 256:512]),
                    waits=[("T", tU)])
                guard[tn] = ("S", p.cnt["S"])
                sHs_s = p.cnt["S"]
            else:
                p.v(lambda e, j=j, tile=tile: e.tensor_copy(
                    UHS[0:16, j::8][:, 0:256], tile[0:16, 256:512]),
                    waits=[("T", tU)])
                guard[tn] = ("V", p.cnt["V"])
                sHs_v = p.cnt["V"]
        sHs = ("S", sHs_s)
        sHs2 = ("V", sHs_v)

        pool2 = [(pA, "pA"), (pB, "pB"), (pD, "pD")]
        pj = [0]

        def nt2():
            t_ = pool2[pj[0] % 3]
            pj[0] += 1
            return t_

        lr_next = None
        lr_next_out = None
        for gt in range(4):
            ypar = gt % 2
            if lr_next_spec is not None and gt == 2:
                lr_next["A"]()
            for oc in range(8):
                tile, tn = nt2()
                w = ([sHs, sHs2, ("DW", dwConst)] + twait(tn) +
                     twait("Y%d" % ypar))
                p.t(lambda e, oc=oc, gt=gt, tile=tile: e.matmul(
                    tile[:, :], CMT[0:16, oc * 128:(oc + 1) * 128],
                    UHS[0:16, gt * 512:(gt + 1) * 512], start=True, stop=True),
                    waits=w)
                tY = p.cnt["T"]
                if oc % 2 == 0:
                    p.s(lambda e, oc=oc, ypar=ypar, tile=tile: e.copy(
                        HID[:, 8 + oc, ypar * 512:(ypar + 1) * 512],
                        tile[:, :]), waits=[("T", tY)])
                    guard[tn] = ("S", p.cnt["S"])
                    sY_s = p.cnt["S"]
                else:
                    p.v(lambda e, oc=oc, ypar=ypar, tile=tile: e.tensor_copy(
                        HID[:, 8 + oc, ypar * 512:(ypar + 1) * 512],
                        tile[:, :]), waits=[("T", tY)])
                    guard[tn] = ("V", p.cnt["V"])
                    sY_v = p.cnt["V"]
            sY = ("S", sY_s)
            sY2 = ("V", sY_v)
            for oc in range(8):
                tile, tn = nt2()
                for hc in range(NHC):
                    w = []
                    if hc == 0:
                        w = twait(tn) + [sY, sY2,
                                         ("DW", state["wout_ready"])]
                    p.t(lambda e, oc=oc, hc=hc, ypar=ypar, tile=tile: e.matmul(
                        tile[:, :], W2GB[:, 8 + hc, oc * 128:(oc + 1) * 128],
                        HID[:, 8 + hc, ypar * 512:(ypar + 1) * 512],
                        start=(hc == 0), stop=(hc == NHC - 1)), waits=w)
                tO = p.cnt["T"]
                p.v(lambda e, oc=oc, gt=gt, tile=tile: e.scalar_tensor_tensor(
                    out=X[:, oc, gt * 512:(gt + 1) * 512], in0=tile[:, :],
                    scalar=CF[:, CF_BOUT + oc:CF_BOUT + oc + 1],
                    in1=X[:, oc, gt * 512:(gt + 1) * 512],
                    op0=ALU.add, op1=ALU.add), waits=[("T", tO)])
                guard[tn] = ("V", p.cnt["V"])
                if lr_next is not None and gt == 2:
                    lr_next["Bh"](oc)
                    if oc == 7:
                        lr_next["Bt"]()

            guard["Y%d" % ypar] = ("T", p.cnt["T"])
            if gt == 1 and lr_next_spec is not None:
                vX01 = p.cnt["V"]
                tb2, do_rout2 = lr_next_spec
                lr_next, lr_next_out = make_lr(
                    tb2, [("V", vX01)], do_rout2, marks["t_xln_free"])
                lr_next["A0"]()
        if lr_next is not None:
            lr_next["C"]()
            marks["lr_d"] = lr_next["D"]
        marks["vX"] = p.cnt["V"]
        marks["tEnd"] = p.cnt["T"]
        return marks, lr_next_out

    lr0, lr0_out = make_lr(0, [("DS", dsX0)], True, 0)
    lr0["A0"]()
    lr0["A"]()
    for i_ in range(8):
        lr0["Bh"](i_)
    lr0["Bt"]()

    m_f1t0, lr1_out = ffn_body(0, lr0_out,
                               (1, [("DS", dsX1)], True),
                               pending_head=lr0["C"], pending_d2=lr0["D"])
    load_into_w2gb(d_ws2, 0, m_f1t0["tEnd"])
    load_w1(0, 0, m_f1t0["tEnd"])
    load_w1(1, 1, m_f1t0["tEnd"])

    m_f1t1, lra0_out = ffn_body(1, lr1_out,
                                (0, [("V", m_f1t0["vX"])], False),
                                pending_d=m_f1t0["lr_d"])
    p.g(lambda eng: eng.dma_start(
        out=W1B[:, :, :, :].rearrange("p a b c -> p (a b c)"), in_=d_win),
        waits=[("T", m_f1t1["t_xln_free"])])
    state["win_ready"] = p.cnt["DW"]
    state["wgate_ready"] = load_into_w2gb(d_wgate, 0, m_f1t1["tEnd"])
    state["wout_ready"] = load_into_w2gb(d_wout, 1, m_f1t1["tEnd"])

    m_attn, lrf2_out = attn_body(lra0_out,
                                 [("V", m_f1t1["vX"])], (0, True),
                                 pending_d=m_f1t1["lr_d"])
    load_into_w2gb(d_ws2, 0, m_attn["t_xln_free"])
    load_w1(0, 0, m_attn["t_xln_free"])
    load_w1(1, 1, m_attn["t_xln_free"])

    m_f2t0, lrf2t1_out = ffn_body(0, lrf2_out,
                                  (1, [("V", m_attn["vX"])], True),
                                  pending_d=m_attn["lr_d"])
    load_into_w2gb(d_ws2, 0, m_f2t0["tEnd"])
    load_w1(0, 0, m_f2t0["tEnd"])
    load_w1(1, 1, m_f2t0["tEnd"])

    ov = d_out.rearrange("(c p) t -> p c t", p=128)
    p.y(lambda e: e.dma_start(out=ov[:, :, 0:TB], in_=X[:, :, 0:TB]),
        waits=[("V", m_f2t0["vX"])])

    m_f2t1, _ = ffn_body(1, lrf2t1_out, None,
                         pending_d=m_f2t0["lr_d"])

    p.y(lambda e: e.dma_start(out=ov[:, :, TB:SEQ], in_=X[:, :, TB:SEQ]),
        waits=[("V", m_f2t1["vX"])])
    dsOut = p.cnt["DS"]
    p.ywait([("DS", dsOut)])

    with nc.Block() as block:
        p.emit(block, sems)
    ctx.close()
    return nc


def _host_prepare(inp):
    f32 = np.float32
    x = np.ascontiguousarray(inp["L0_attn_x"], dtype=f32)
    w1 = np.ascontiguousarray(inp["L0_ffn_W1"], dtype=f32)
    w2 = np.ascontiguousarray(inp["L0_ffn_W2"], dtype=f32)
    ws1 = np.ascontiguousarray(inp["L0_ffn_Ws1"], dtype=f32)
    ws2 = np.ascontiguousarray(inp["L0_ffn_Ws2"], dtype=f32)
    wg = np.ascontiguousarray(inp["L0_ffn_Wg"], dtype=f32)
    bias_corr = np.ascontiguousarray(inp["L0_ffn_bias_corr"], dtype=f32)
    win = np.ascontiguousarray(inp["L0_attn_W_in"], dtype=f32)
    b_in = np.ascontiguousarray(inp["L0_attn_b_in"], dtype=f32)
    a = np.ascontiguousarray(inp["L0_attn_A"], dtype=f32)
    bm = np.ascontiguousarray(inp["L0_attn_Bm"], dtype=f32)
    cm = np.ascontiguousarray(inp["L0_attn_Cm"], dtype=f32)
    wgate = np.ascontiguousarray(inp["L0_attn_W_gate"], dtype=f32)
    b_gate = np.ascontiguousarray(inp["L0_attn_b_gate"], dtype=f32)
    wout = np.ascontiguousarray(inp["L0_attn_W_out"], dtype=f32)
    b_out = np.ascontiguousarray(inp["L0_attn_b_out"], dtype=f32)

    def fm_chunks(wT):
        inf, outf = wT.shape
        return np.ascontiguousarray(
            wT.reshape(inf // 128, 128, outf).transpose(1, 0, 2).reshape(128, -1))

    w1t = np.stack([fm_chunks(w1[e].T) for e in range(E)])
    w2g = np.zeros((NG, 128, 16 * DIM), f32)
    for g in range(NG):
        for ei in range(EG):
            w2g[g][:, ei * 4 * DIM:(ei + 1) * 4 * DIM] = fm_chunks(
                w2[g * EG + ei].T)
    ws1t = fm_chunks(ws1.T)
    ws2t = fm_chunks(ws2.T)
    wint = fm_chunks(win[DIM:2 * DIM].T)
    wgatet = fm_chunks(wgate.T)
    woutt = fm_chunks(wout.T)
    wgt = fm_chunks(wg.T)
    bmt = fm_chunks(bm.T)
    cmt = np.ascontiguousarray(cm.T)

    cb = np.zeros((128, CBW), f32)
    cb[:, OFF_ONES:OFF_ONES + 128] = 1.0
    cb[:, OFF_ID:OFF_ID + 128] = np.eye(128)
    for j in range(8):
        blk = np.zeros((128, 16), f32)
        for m in range(16):
            blk[16 * j + m, m] = 1.0
        cb[:, OFF_UNS + 16 * j:OFF_UNS + 16 * (j + 1)] = blk
    for e in range(E):
        blk = np.zeros((16, 128), f32)
        blk[e, :] = 1.0
        cb[0:16, OFF_SELB + 128 * e:OFF_SELB + 128 * (e + 1)] = blk
    for l in range(8):
        blk = np.zeros((16, 128), f32)
        for n in range(16):
            blk[n, 16 * l + n] = 1.0
        cb[0:16, OFF_STK0 + 128 * l:OFF_STK0 + 128 * (l + 1)] = blk
    cb[0:16, OFF_A:OFF_A + 16] = a
    cb[0:16, OFF_AT:OFF_AT + 16] = a.T

    cf = np.zeros((128, CFW), f32)
    cf[:, CF_BCB:CF_BCB + 16] = np.broadcast_to(bias_corr, (128, 16))
    cf[:, CF_BIN:CF_BIN + 8] = b_in[DIM:2 * DIM].reshape(8, 128).T
    cf[:, CF_BGATE:CF_BGATE + 8] = b_gate.reshape(8, 128).T
    cf[:, CF_BOUT:CF_BOUT + 8] = b_out.reshape(8, 128).T
    cf[:, CF_EPS] = DIM * LNEPS
    cf[:, CF_LNC] = np.log(np.sqrt(float(DIM)))

    common = {
        "w1t": w1t, "w2g": w2g, "ws1t": ws1t, "ws2t": ws2t,
        "wint": wint, "wgatet": wgatet, "woutt": woutt,
        "wgt": wgt, "bmt": bmt, "cmt": cmt, "cb": cb, "cf": cf,
    }
    in_maps = []
    for b in range(NCORES):
        m = dict(common)
        m["xt"] = np.ascontiguousarray(x[b].T)
        in_maps.append(m)
    return in_maps


def kernel(**inputs):
    in_maps = _host_prepare(inputs)
    nc = build_program()
    res = run_bass_kernel_spmd(nc, in_maps, core_ids=list(range(NCORES)))
    out = np.empty((NCORES, SEQ, DIM), np.float32)
    for b in range(NCORES):
        out[b] = np.asarray(res.results[b]["yt"], np.float32).T
    return out


# version 16
# speedup vs baseline: 1.0397x; 1.0205x over previous
import numpy as np
import ml_dtypes

import concourse.bass as bass
import concourse.mybir as mybir
from concourse.bass_utils import run_bass_kernel_spmd

F32 = mybir.dt.float32
BF16 = mybir.dt.bfloat16
AF = mybir.ActivationFunctionType
ALU = mybir.AluOpType
AX = mybir.AxisListType

NCORES = 8
DIM = 1024
SEQ = 2048
NHC = 8
NTB = 2
TB = 1024
E = 16
EG = 4
NG = 4
I_EXP = 512
LNEPS = 1e-5

OFF_ONES = 0
OFF_ID = 128
OFF_UNS = 256
OFF_SELB = 384
OFF_STK0 = OFF_SELB + 2048
OFF_A = OFF_STK0 + 1024
OFF_AT = OFF_A + 16
CBW = OFF_AT + 16

CF_BCB = 0
CF_BIN = 16
CF_BGATE = 24
CF_BOUT = 32
CF_EPS = 40
CF_LNC = 41
CFW = 48

STAGES = "full"
DBG_ROUT = False


class Prog:
    ENGS = ("t", "v", "s", "g", "y")

    def __init__(self, nc):
        self.nc = nc
        self.streams = {e: [] for e in self.ENGS}
        self.cnt = {"T": 0, "V": 0, "S": 0, "DW": 0, "DS": 0}

    def _op(self, eng, fn, waits, inc, n):
        for s, v in waits:
            assert v <= self.cnt[s], f"wait {s}>={v} emitted={self.cnt[s]}"
        if inc is not None:
            self.cnt[inc] += n
        self.streams[eng].append((tuple(waits), fn, inc, n))
        return self.cnt[inc] if inc is not None else None

    def t(self, fn, waits=()):
        return self._op("t", fn, waits, "T", 1)

    def v(self, fn, waits=()):
        return self._op("v", fn, waits, "V", 1)

    def vw(self, fn, waits=()):
        w = list(waits) + [("V", self.cnt["V"])]
        return self._op("v", fn, w, "V", 1)

    def s(self, fn, waits=()):
        return self._op("s", fn, waits, "S", 1)

    def g(self, fn, waits=()):
        return self._op("g", fn, waits, "DW", 16)

    def y(self, fn, waits=()):
        return self._op("y", fn, waits, "DS", 16)

    def ywait(self, waits):
        self.streams["y"].append((tuple(waits), None, None, 0))

    def emit(self, block, sems):
        def mk(eng_key, deco):
            stream = self.streams[eng_key]

            def body(eng):
                seen = {k: 0 for k in self.cnt}
                for waits, fn, inc, n in stream:
                    for s, v in waits:
                        if v > seen[s]:
                            eng.wait_ge(sems[s], v)
                            seen[s] = v
                    if fn is None:
                        continue
                    inst = fn(eng)
                    if inc is not None:
                        inst.then_inc(sems[inc], n)
            deco(body)

        mk("t", block.tensor)
        mk("v", block.vector)
        mk("s", block.scalar)
        mk("g", block.gpsimd)
        mk("y", block.sync)


def build_program():
    nc = bass.Bass("TRN2", target_bir_lowering=False)

    d_x = nc.dram_tensor("xt", [DIM, SEQ], F32, kind="ExternalInput").ap()
    d_w1 = nc.dram_tensor("w1t", [E, 128, 8 * I_EXP], F32, kind="ExternalInput").ap()
    d_w2g = nc.dram_tensor("w2g", [NG, 128, 16 * DIM], F32, kind="ExternalInput").ap()
    d_ws1 = nc.dram_tensor("ws1t", [128, 8 * DIM], F32, kind="ExternalInput").ap()
    d_ws2 = nc.dram_tensor("ws2t", [128, 8 * DIM], F32, kind="ExternalInput").ap()
    d_win = nc.dram_tensor("wint", [128, 8 * DIM], F32, kind="ExternalInput").ap()
    d_wgate = nc.dram_tensor("wgatet", [128, 8 * DIM], F32, kind="ExternalInput").ap()
    d_wout = nc.dram_tensor("woutt", [128, 8 * DIM], F32, kind="ExternalInput").ap()
    d_wg = nc.dram_tensor("wgt", [128, 8 * 16], F32, kind="ExternalInput").ap()
    d_bm = nc.dram_tensor("bmt", [128, 8 * 16], F32, kind="ExternalInput").ap()
    d_cm = nc.dram_tensor("cmt", [16, DIM], F32, kind="ExternalInput").ap()
    d_cb = nc.dram_tensor("cb", [128, CBW], F32, kind="ExternalInput").ap()
    d_cf = nc.dram_tensor("cf", [128, CFW], F32, kind="ExternalInput").ap()
    d_out = nc.dram_tensor("yt", [DIM, SEQ], F32, kind="ExternalOutput").ap()

    from contextlib import ExitStack
    ctx = ExitStack()
    Eb = ctx.enter_context

    X = Eb(nc.sbuf_tensor("X", [128, NHC, SEQ], F32))
    XLN = Eb(nc.sbuf_tensor("XLN", [128, NHC, TB], BF16))
    HID = Eb(nc.sbuf_tensor("HID", [128, 16, TB], BF16))
    W1B = Eb(nc.sbuf_tensor("W1B", [128, 2, 8, I_EXP], BF16))
    W2GB = Eb(nc.sbuf_tensor("W2GB", [128, 16, DIM], BF16))
    WS1B = Eb(nc.sbuf_tensor("WS1B", [128, 8, DIM], BF16))
    CB = Eb(nc.sbuf_tensor("CB", [128, CBW], BF16))
    CF = Eb(nc.sbuf_tensor("CF", [128, CFW], F32))
    WGT = Eb(nc.sbuf_tensor("WGT", [128, 8, 16], BF16))
    BMT = Eb(nc.sbuf_tensor("BMT", [128, 8, 16], BF16))
    CMT = Eb(nc.sbuf_tensor("CMT", [16, DIM], BF16))
    MB = Eb(nc.sbuf_tensor("MB", [128, TB], BF16))
    RB = Eb(nc.sbuf_tensor("RB", [128, TB], BF16))
    SQS = Eb(nc.sbuf_tensor("SQS", [128, TB], BF16))
    CWB = Eb(nc.sbuf_tensor("CWB", [128, 2, 512], BF16))
    CWT = Eb(nc.sbuf_tensor("CWT", [16, TB], BF16))
    GSC = Eb(nc.sbuf_tensor("GSC", [128, 2, 512], BF16))
    ARN = Eb(nc.sbuf_tensor("ARN", [128, 592], F32))
    CW16 = Eb(nc.sbuf_tensor("CW16", [128, 128], BF16))
    POW = Eb(nc.sbuf_tensor("POW", [16, 26 * 16], BF16))
    KER1 = Eb(nc.sbuf_tensor("KER1", [128, 128], BF16))
    KER2 = Eb(nc.sbuf_tensor("KER2", [128, 128], BF16))
    KER3 = Eb(nc.sbuf_tensor("KER3", [128, 128], BF16))
    B1 = Eb(nc.sbuf_tensor("B1", [128, 256], BF16))
    S1SB = Eb(nc.sbuf_tensor("S1SB", [128, 256], BF16))
    B2 = Eb(nc.sbuf_tensor("B2", [128, 32], BF16))
    S2SB = Eb(nc.sbuf_tensor("S2SB", [128, 32], BF16))
    B3 = Eb(nc.sbuf_tensor("B3", [128, 8], BF16))
    S3SB = Eb(nc.sbuf_tensor("S3SB", [128, 8], BF16))
    E3 = Eb(nc.sbuf_tensor("E3", [16, 8], BF16))
    GC = Eb(nc.sbuf_tensor("GC", [16, 8], BF16))
    CIN3 = Eb(nc.sbuf_tensor("CIN3", [16, 8], BF16))
    CIN2 = Eb(nc.sbuf_tensor("CIN2", [16, 32], BF16))
    CIN1 = Eb(nc.sbuf_tensor("CIN1", [16, 256], BF16))
    UHS = Eb(nc.sbuf_tensor("UHS", [16, SEQ], BF16))

    pA = Eb(nc.psum_tensor("pA", [128, 512], F32))
    pB = Eb(nc.psum_tensor("pB", [128, 512], F32))
    pC = Eb(nc.psum_tensor("pC", [128, 512], F32))
    pD = Eb(nc.psum_tensor("pD", [128, 512], F32))
    pE = Eb(nc.psum_tensor("pE", [128, 1024], F32))
    pT = Eb(nc.psum_tensor("pT", [16, 1024], BF16))

    sem_t = Eb(nc.semaphore("semT"))
    sem_v = Eb(nc.semaphore("semV"))
    sem_s = Eb(nc.semaphore("semS"))
    sem_dw = Eb(nc.semaphore("semDW"))
    sem_ds = Eb(nc.semaphore("semDS"))
    sems = {"T": sem_t, "V": sem_v, "S": sem_s, "DW": sem_dw, "DS": sem_ds}

    p = Prog(nc)
    guard = {}

    def twait(*keys):
        return [guard[k] for k in keys if k in guard]

    ONES = CB[:, OFF_ONES:OFF_ONES + 128]
    IDENT = CB[:, OFF_ID:OFF_ID + 128]
    ID16 = CB[0:16, OFF_ID:OFF_ID + 16]
    A_ = CB[0:16, OFF_A:OFF_A + 16]
    AT_ = CB[0:16, OFF_AT:OFF_AT + 16]

    def UNSj(j):
        return CB[:, OFF_UNS + 16 * j: OFF_UNS + 16 * (j + 1)]

    def SELBe(e):
        return CB[0:16, OFF_SELB + 128 * e: OFF_SELB + 128 * (e + 1)]

    def STK0l(l):
        return CB[0:16, OFF_STK0 + 128 * l: OFF_STK0 + 128 * (l + 1)]

    def pow_slot(i):
        return POW[0:16, 16 * i:16 * (i + 1)]

    xv = d_x.rearrange("(c p) t -> p c t", p=128)
    p.y(lambda e: e.dma_start(out=X[:, :, 0:TB], in_=xv[:, :, 0:TB]))
    dsX0 = p.cnt["DS"]
    p.y(lambda e: e.dma_start(out=X[:, :, TB:SEQ], in_=xv[:, :, TB:SEQ]))
    dsX1 = p.cnt["DS"]
    p.y(lambda e: e.dma_start(out=CF[:, :], in_=d_cf))
    dsConst = p.cnt["DS"]
    p.g(lambda e: e.dma_start(out=CB[:, :], in_=d_cb))
    p.g(lambda e: e.dma_start(out=WGT[:, :, :].rearrange("p a b -> p (a b)"),
                              in_=d_wg))
    p.g(lambda e: e.dma_start(out=BMT[:, :, :].rearrange("p a b -> p (a b)"),
                              in_=d_bm))
    p.g(lambda e: e.dma_start(out=CMT[:, :], in_=d_cm))
    dwConst = p.cnt["DW"]

    w1_ready = [0, 0]
    state = {"w2g_ready": 0, "ws1_ready": 0, "wgate_ready": 0,
             "wout_ready": 0, "win_ready": 0, "sPowAll": 0, "vPowT": 0}
    kmarks = []

    def load_w1(e_, slot, wait_t):
        p.g(lambda eng: eng.dma_start(
            out=W1B[:, slot, :, :].rearrange("p a b -> p (a b)"), in_=d_w1[e_]),
            waits=[("T", wait_t)] if wait_t else [])
        w1_ready[slot] = p.cnt["DW"]

    def load_w2g(gidx, wait_t):
        p.g(lambda eng: eng.dma_start(
            out=W2GB[:, 0:8, :].rearrange("p a b -> p (a b)"),
            in_=d_w2g[gidx][:, 0:8 * DIM]),
            waits=[("T", wait_t)] if wait_t else [])
        p.g(lambda eng: eng.dma_start(
            out=W2GB[:, 8:16, :].rearrange("p a b -> p (a b)"),
            in_=d_w2g[gidx][:, 8 * DIM:16 * DIM]))
        state["w2g_ready"] = p.cnt["DW"]

    def load_into_w2gb(dram, half, wait_t):
        lo = 8 * half
        p.g(lambda eng: eng.dma_start(
            out=W2GB[:, lo:lo + 8, :].rearrange("p a b -> p (a b)"), in_=dram),
            waits=[("T", wait_t)] if wait_t else [])
        state["w2g_ready"] = p.cnt["DW"]
        return p.cnt["DW"]

    def load_ws1(dram, wait_t):
        p.g(lambda eng: eng.dma_start(
            out=WS1B[:, 0:4, :].rearrange("p a b -> p (a b)"),
            in_=dram[:, 0:4 * DIM]),
            waits=[("T", wait_t)] if wait_t else [])
        state["ws1_half"] = p.cnt["DW"]
        p.g(lambda eng: eng.dma_start(
            out=WS1B[:, 4:8, :].rearrange("p a b -> p (a b)"),
            in_=dram[:, 4 * DIM:8 * DIM]))
        state["ws1_ready"] = p.cnt["DW"]

    load_ws1(d_ws1, 0)
    load_into_w2gb(d_ws2, 0, 0)
    load_w1(0, 0, 0)
    load_w1(1, 1, 0)

    p.v(lambda e: e.memset(CIN3[:, :], 0.0))
    p.v(lambda e: e.memset(CIN2[:, :], 0.0))
    p.v(lambda e: e.memset(CIN1[:, :], 0.0))
    vInitCin = p.cnt["V"]

    cwb_free = [0, 0]
    gsc_free = [0, 0]
    dbg_i = [0]

    def make_lr(tb, waitX, do_routing, xln_free, sq_vec=False):
        out = {}
        st_ = {}

        def A0():
            st_["sCopyT"] = [0, 0]
            for tt in range(2):
                p.s(lambda e, tt=tt: e.copy(
                    XLN[:, :, tt * 512:(tt + 1) * 512],
                    X[:, :, tb * TB + tt * 512:tb * TB + (tt + 1) * 512]),
                    waits=(list(waitX) +
                           ([("T", xln_free)] if xln_free else [])
                           if tt == 0 else []))
                st_["sCopyT"][tt] = p.cnt["S"]
            st_["sCopy"] = p.cnt["S"]

        def A():
            for tt in range(2):
                for hc in range(NHC):
                    w = ([("S", st_["sCopyT"][0]), ("DW", dwConst)]
                         + twait("pE0", "pE1") if tt == 0 and hc == 0 else
                         ([("S", st_["sCopyT"][1])]
                          if tt == 1 and hc == 0 else []))
                    p.t(lambda e, tt=tt, hc=hc: e.matmul(
                        pE[:, tt * 512:(tt + 1) * 512], ONES,
                        XLN[:, hc, tt * 512:(tt + 1) * 512],
                        start=(hc == 0), stop=(hc == NHC - 1)), waits=w)
            tSum = p.cnt["T"]
            p.s(lambda e: e.activation(MB[:, :], pE[:, :], AF.Copy,
                                       scale=1.0 / DIM), waits=[("T", tSum)])
            st_["sMb"] = p.cnt["S"]
            guard["pE0"] = ("S", st_["sMb"])
            guard["pE1"] = ("S", st_["sMb"])
            p.v(lambda e: e.tensor_tensor(
                out=RB[:, :], in0=MB[:, :], in1=MB[:, :], op=ALU.mult),
                waits=[("S", st_["sMb"])])
            st_["vMsq"] = p.cnt["V"]

        def B_hc(hc):
            w = [("S", st_["sCopy"])]
            if st_.get("pair_mark"):
                w.append(("T", st_["pair_mark"]))
            if sq_vec:
                p.v(lambda e, hc=hc: e.tensor_tensor(
                    out=SQS[:, :], in0=XLN[:, hc, :], in1=XLN[:, hc, :],
                    op=ALU.mult), waits=w)
                sq_w = ("V", p.cnt["V"])
            else:
                p.s(lambda e, hc=hc: e.activation(
                    SQS[:, :], XLN[:, hc, :], AF.Square), waits=w)
                sq_w = ("S", p.cnt["S"])
            for tt in range(2):
                w2 = [sq_w]
                if hc == 0 and tt == 0:
                    w2 += [("V", st_["vMsq"])]
                p.t(lambda e, tt=tt, hc=hc: e.matmul(
                    pE[:, tt * 512:(tt + 1) * 512], ONES,
                    SQS[:, tt * 512:(tt + 1) * 512],
                    start=(hc == 0), stop=(hc == NHC - 1)), waits=w2)
            st_["pair_mark"] = p.cnt["T"]

        def Btail():
            tSum2 = p.cnt["T"]
            st_["tSum2"] = tSum2
            p.vw(lambda e: e.scalar_tensor_tensor(
                out=RB[:, :], in0=RB[:, :], scalar=-float(DIM), in1=pE[:, :],
                op0=ALU.mult, op1=ALU.add), waits=[("T", st_["tSum2"])])
            vVar = p.cnt["V"]
            guard["pE0"] = ("V", vVar)
            guard["pE1"] = ("V", vVar)
            p.s(lambda e: e.activation(RB[:, :], RB[:, :], AF.Ln,
                                       bias=CF[:, CF_EPS:CF_EPS + 1],
                                       scale=1.0),
                waits=[("V", vVar), ("DS", dsConst)])
            p.s(lambda e: e.activation(RB[:, :], RB[:, :], AF.Exp,
                                       bias=CF[:, CF_LNC:CF_LNC + 1],
                                       scale=-0.5))
            sR = p.cnt["S"]
            out["vXlnT"] = [0, 0]
            for tt in range(2):
                sl = slice(tt * 512, (tt + 1) * 512)
                p.vw(lambda e, sl=sl: e.tensor_tensor(
                    out=XLN[:, :, sl], in0=XLN[:, :, sl],
                    in1=MB[:, sl].rearrange("p (a t) -> p a t", a=1)
                        .to_broadcast([128, 8, 512]), op=ALU.subtract),
                    waits=[("S", st_["sMb"]), ("T", st_["tSum2"])])
                p.vw(lambda e, sl=sl: e.tensor_tensor(
                    out=XLN[:, :, sl], in0=XLN[:, :, sl],
                    in1=RB[:, sl].rearrange("p (a t) -> p a t", a=1)
                        .to_broadcast([128, 8, 512]), op=ALU.mult),
                    waits=[("S", sR)])
                out["vXlnT"][tt] = p.cnt["V"]
            out["vXln"] = p.cnt["V"]

        def C():
            if do_routing:
                routing_emit(out["vXln"], st_, out.get("vXlnT"))

        def D():
            if not do_routing:
                out["sCwt"] = 0
                return
            vCw = st_["vCw"]
            for tc in range(8):
                w = [("V", vCw)] + (twait("pT") if tc == 0 else [])
                p.t(lambda e, tc=tc: e.transpose(
                    pT[0:16, tc * 128:(tc + 1) * 128],
                    CW16[:, tc * 16:(tc + 1) * 16], IDENT), waits=w)
            tTr = p.cnt["T"]
            p.s(lambda e: e.copy(CWT[0:16, :], pT[0:16, :]),
                waits=[("T", tTr)] + twait("CWT"))
            out["sCwt"] = p.cnt["S"]
            guard["pT"] = ("S", out["sCwt"])

        stages = {"A0": A0, "A": A, "Bh": B_hc, "Bt": Btail,
                  "C": C, "D": D}
        return stages, out

    def routing_emit(vXln, st_, vXlnT=None):
        for tc in range(8):
            for hc in range(NHC):
                w = []
                if tc == 0 and hc == 0:
                    w = [("V", vXlnT[0] if vXlnT else vXln),
                         ("DW", dwConst)] + twait("pC")
                elif tc == 4 and hc == 0 and vXlnT:
                    w = [("V", vXlnT[1])]
                p.t(lambda e, tc=tc, hc=hc: e.matmul(
                    pC[:, tc * 16:(tc + 1) * 16],
                    XLN[:, hc, tc * 128:(tc + 1) * 128], WGT[:, hc, :],
                    start=(hc == 0), stop=(hc == NHC - 1)), waits=w)
        tGate = p.cnt["T"]
        SCO = ARN[:, 0:128]
        SCH = ARN[:, 128:256]
        MSK = ARN[:, 256:384]
        WW = ARN[:, 384:512]
        GRP = ARN[:, 512:544]
        G2 = ARN[:, 544:576]
        TH = ARN[:, 576:584]
        RCP = ARN[:, 584:592]
        p.s(lambda e: e.activation(SCO, pC[:, 0:128], AF.Sigmoid),
            waits=[("T", tGate)])
        sSco = p.cnt["S"]
        guard["pC"] = ("S", sSco)

        def v3(ap, a, b):
            return ap.rearrange("p (a b) -> p a b", b=b)

        def dump(ap, width=128):
            if not DBG_ROUT:
                return
            i = dbg_i[0]
            dbg_i[0] += 1
            row = 6 + i // 16
            col = (i % 16) * 128
            p.vw(lambda e, ap=ap, row=row, col=col, width=width: e.tensor_copy(
                X[:, row, col:col + width], ap))

        p.vw(lambda e: e.tensor_tensor(
            out=v3(SCH, 8, 16), in0=v3(SCO, 8, 16),
            in1=CF[:, CF_BCB:CF_BCB + 16].rearrange("p (a b) -> p a b", a=1)
                .to_broadcast([128, 8, 16]), op=ALU.add),
            waits=[("S", sSco), ("DS", dsConst)])
        dump(SCH)
        p.vw(lambda e: e.tensor_reduce(
            out=GRP, in_=v3(SCH, 32, 4), axis=AX.X, op=ALU.max))
        dump(GRP, 32)
        p.vw(lambda e: e.tensor_tensor(
            out=v3(MSK, 32, 4), in0=v3(SCH, 32, 4),
            in1=v3(GRP, 32, 1).to_broadcast([128, 32, 4]), op=ALU.is_equal))
        p.vw(lambda e: e.scalar_tensor_tensor(
            out=WW, in0=MSK, scalar=-1e30, in1=SCH, op0=ALU.mult, op1=ALU.add))
        p.vw(lambda e: e.tensor_reduce(
            out=G2, in_=v3(WW, 32, 4), axis=AX.X, op=ALU.max))
        p.vw(lambda e: e.tensor_tensor(out=GRP, in0=GRP, in1=G2, op=ALU.add))
        p.vw(lambda e: e.tensor_reduce(
            out=TH, in_=v3(GRP, 8, 4), axis=AX.X, op=ALU.max))
        p.vw(lambda e: e.tensor_tensor(
            out=v3(G2, 8, 4), in0=v3(GRP, 8, 4),
            in1=v3(TH, 8, 1).to_broadcast([128, 8, 4]), op=ALU.is_equal))
        p.vw(lambda e: e.scalar_tensor_tensor(
            out=G2, in0=G2, scalar=-1e30, in1=GRP, op0=ALU.mult, op1=ALU.add))
        p.vw(lambda e: e.tensor_reduce(
            out=TH, in_=v3(G2, 8, 4), axis=AX.X, op=ALU.max))
        p.vw(lambda e: e.tensor_tensor(
            out=v3(G2, 8, 4), in0=v3(GRP, 8, 4),
            in1=v3(TH, 8, 1).to_broadcast([128, 8, 4]), op=ALU.is_ge))
        p.vw(lambda e: e.tensor_tensor(
            out=v3(MSK, 32, 4), in0=v3(SCH, 32, 4),
            in1=v3(G2, 32, 1).to_broadcast([128, 32, 4]), op=ALU.mult))
        dump(MSK)
        p.vw(lambda e: e.tensor_copy(WW, MSK))
        for k in range(4):
            p.vw(lambda e: e.tensor_reduce(
                out=TH, in_=v3(WW, 8, 16), axis=AX.X, op=ALU.max))
            if k < 3:
                p.vw(lambda e: e.tensor_tensor(
                    out=v3(SCH, 8, 16), in0=v3(WW, 8, 16),
                    in1=v3(TH, 8, 1).to_broadcast([128, 8, 16]),
                    op=ALU.is_equal))
                p.vw(lambda e: e.scalar_tensor_tensor(
                    out=WW, in0=SCH, scalar=-1e30, in1=WW,
                    op0=ALU.mult, op1=ALU.add))
        dump(TH, 8)
        p.vw(lambda e: e.tensor_tensor(
            out=v3(WW, 8, 16), in0=v3(MSK, 8, 16),
            in1=v3(TH, 8, 1).to_broadcast([128, 8, 16]), op=ALU.is_ge))
        p.vw(lambda e: e.tensor_tensor(out=WW, in0=SCO, in1=WW, op=ALU.mult))
        dump(WW)
        p.vw(lambda e: e.tensor_reduce(
            out=TH, in_=v3(WW, 8, 16), axis=AX.X, op=ALU.add))
        p.vw(lambda e: e.tensor_scalar(
            out=TH, in0=TH, scalar1=1e-20, scalar2=None, op0=ALU.add))
        p.vw(lambda e: e.reciprocal(RCP, TH))
        p.vw(lambda e: e.scalar_tensor_tensor(
            out=v3(CW16[:, :], 8, 16), in0=v3(WW, 8, 16), scalar=2.5,
            in1=v3(RCP, 8, 1).to_broadcast([128, 8, 16]),
            op0=ALU.mult, op1=ALU.mult))
        st_["vCw"] = p.cnt["V"]

    def ffn_body(tb, lrout_src, lr_spec, pending_d=None,
             pending_d2=None, pending_head=None):
        pool = [(pA, "pA"), (pB, "pB"), (pD, "pD")]
        pidx = [0]

        def next_tile():
            t_ = pool[pidx[0] % 3]
            pidx[0] += 1
            return t_

        w1_marks = [0] * E
        hid_mark = [[0, 0] for _ in range(16)]
        marks = {}
        lr_out = None
        for gi in range(NG + 1):
            shared = (gi == 0)
            g = gi - 1
            if shared and pending_head is not None:
                pending_head()
            experts = [None] if shared else list(range(g * EG, (g + 1) * EG))
            for e_ in experts:
                cw_marks = [0, 0]
                if not shared:
                    slot = e_ % 2
                    sCwt = lrout_src["sCwt"]
                    for tt in range(2):
                        tile, tname = next_tile()
                        p.t(lambda en, tile=tile, e_=e_, tt=tt: en.matmul(
                            tile[:, :], SELBe(e_),
                            CWT[0:16, tt * 512:(tt + 1) * 512],
                            start=True, stop=True),
                            waits=[("S", sCwt)] + twait(tname))
                        tCwb = p.cnt["T"]
                        wv = [("T", tCwb)]
                        if cwb_free[tt]:
                            wv.append(("V", cwb_free[tt]))
                        p.s(lambda en, tile=tile, tt=tt: en.copy(
                            CWB[:, tt, :], tile[:, :]), waits=wv)
                        guard[tname] = ("S", p.cnt["S"])
                        cw_marks[tt] = p.cnt["S"]
                nch = 8 if shared else 4
                for ic in range(nch):
                    for tt in range(2):
                        tile, tname = next_tile()
                        for hc in range(NHC):
                            w = []
                            if hc == 0:
                                w = twait(tname) + [
                                    ("V", lrout_src["vXlnT"][tt])]
                                w.append(("DW", state["ws1_half"] if shared
                                          else w1_ready[slot]))
                            elif hc == 4 and shared:
                                w = [("DW", state["ws1_ready"])]
                            lhs = (WS1B[:, hc, ic * 128:(ic + 1) * 128]
                                   if shared else
                                   W1B[:, slot, hc, ic * 128:(ic + 1) * 128])
                            p.t(lambda en, lhs=lhs, tile=tile, tt=tt, hc=hc:
                                en.matmul(tile[:, :], lhs,
                                          XLN[:, hc, tt * 512:(tt + 1) * 512],
                                          start=(hc == 0),
                                          stop=(hc == NHC - 1)), waits=w)
                        tFill = p.cnt["T"]
                        k = ic if shared else 4 * (e_ - g * EG) + ic
                        hs = HID[:, k, tt * 512:(tt + 1) * 512]
                        p.s(lambda en, hs=hs, tile=tile: en.activation(
                            hs, tile[:, :], AF.Relu), waits=[("T", tFill)])
                        sRelu = p.cnt["S"]
                        guard[tname] = ("S", sRelu)
                        p.v(lambda en, hs=hs: en.tensor_tensor(
                            out=hs, in0=hs, in1=hs, op=ALU.mult),
                            waits=[("S", sRelu)])
                        if not shared:
                            p.vw(lambda en, hs=hs, tt=tt: en.tensor_tensor(
                                out=hs, in0=hs, in1=CWB[:, tt, :],
                                op=ALU.mult), waits=[("S", cw_marks[tt])])
                        hid_mark[k][tt] = p.cnt["V"]
                if not shared:
                    w1_marks[e_] = p.cnt["T"]
                    cwb_free[0] = p.cnt["V"]
                    cwb_free[1] = p.cnt["V"]
                    nxt = e_ + 2
                    if nxt < E:
                        load_w1(nxt, nxt % 2, w1_marks[e_])
            if gi == 0 and pending_d is not None:
                pending_d()
                pending_d = None
            last_group = (gi == NG)
            if last_group:
                marks["t_xln_free"] = p.cnt["T"]
                guard["CWT"] = ("T", p.cnt["T"])
            nk = 8 if shared else 16
            lr_stages = None
            if last_group and lr_spec is not None:
                tb2, waitX2, do_rout2 = lr_spec
                lr_stages, lr_out = make_lr(tb2, waitX2, do_rout2,
                                            marks["t_xln_free"])
            tile_i = 0
            for oc in range(8):
                for tt in range(2):
                    if lr_stages is not None:
                        if tile_i == 0:
                            lr_stages["A0"]()
                        elif tile_i == 2:
                            lr_stages["A"]()
                        elif 3 <= tile_i <= 10:
                            lr_stages["Bh"](tile_i - 3)
                        elif tile_i == 11:
                            lr_stages["Bt"]()
                        elif tile_i == 13:
                            lr_stages["C"]()
                    tile_i += 1
                    tile, tname = next_tile()
                    for k in range(nk):
                        w = [("V", hid_mark[k][tt])]
                        if k == 0:
                            w += twait(tname) + [("DW", state["w2g_ready"])]
                        p.t(lambda en, tile=tile, k=k, oc=oc, tt=tt: en.matmul(
                            tile[:, :], W2GB[:, k, oc * 128:(oc + 1) * 128],
                            HID[:, k, tt * 512:(tt + 1) * 512],
                            start=(k == 0), stop=(k == nk - 1)), waits=w)
                    tFill = p.cnt["T"]
                    p.v(lambda en, tile=tile, oc=oc, tt=tt:
                        en.scalar_tensor_tensor(
                            out=X[:, oc, tb * TB + tt * 512:
                                  tb * TB + (tt + 1) * 512],
                            in0=tile[:, :], scalar=0.5,
                            in1=X[:, oc, tb * TB + tt * 512:
                                  tb * TB + (tt + 1) * 512],
                            op0=ALU.mult, op1=ALU.add), waits=[("T", tFill)])
                    guard[tname] = ("V", p.cnt["V"])
            tW2 = p.cnt["T"]
            if gi == 0 and pending_d2 is not None:
                pending_d2()
                pending_d2 = None
            if gi < NG:
                load_w2g(gi, tW2)
        marks["vX"] = p.cnt["V"]
        marks["tEnd"] = p.cnt["T"]
        if lr_stages is not None:
            marks["lr_d"] = lr_stages["D"]
        return marks, lr_out

    def attn_body(lra0_out, waitX_t1, lr_next_spec, pending_d=None):
        marks = {}
        vXlnT_cur = lra0_out["vXlnT"]
        ub_mark_s = [0, 0]
        u_mark = [[0, 0] for _ in range(8)]
        for tb in range(NTB):
            pool = [(pA, "pA"), (pB, "pB"), (pC, "pC"), (pD, "pD")]
            pi = [0]

            def nt():
                t_ = pool[pi[0] % 4]
                pi[0] += 1
                return t_

            W1F = W1B[:, :, :, :].rearrange("p a b c -> p (a b c)")
            for oc in range(8):
                for tt in range(2):
                    if pending_d is not None and (oc, tt) == (1, 0) and tb == 0:
                        pending_d()
                        pending_d = None
                    tileX, tnX = nt()
                    for hc in range(NHC):
                        w = []
                        if hc == 0:
                            w = twait(tnX) + [("V", vXlnT_cur[tt]),
                                              ("DW", state["win_ready"])]
                        p.t(lambda en, W1F=W1F, oc=oc, tt=tt, hc=hc,
                            tileX=tileX:
                            en.matmul(tileX[:, :],
                                      W1F[:, hc * 1024 + oc * 128:
                                          hc * 1024 + (oc + 1) * 128],
                                      XLN[:, hc, tt * 512:(tt + 1) * 512],
                                      start=(hc == 0), stop=(hc == NHC - 1)),
                            waits=w)
                    tXin = p.cnt["T"]
                    hs = HID[:, oc, tt * 512:(tt + 1) * 512]
                    p.s(lambda en, hs=hs, tileX=tileX, oc=oc: en.activation(
                        hs, tileX[:, :], AF.Identity,
                        bias=CF[:, CF_BIN + oc:CF_BIN + oc + 1]),
                        waits=[("T", tXin), ("DS", dsConst)])
                    guard[tnX] = ("S", p.cnt["S"])
            for oc in range(8):
                for tt in range(2):
                    tileG, tnG = nt()
                    for hc in range(NHC):
                        w = []
                        if hc == 0:
                            w = twait(tnG) + [("DW", state["wgate_ready"])]
                        p.t(lambda en, tileG=tileG, oc=oc, tt=tt, hc=hc:
                            en.matmul(tileG[:, :],
                                      W2GB[:, hc, oc * 128:(oc + 1) * 128],
                                      XLN[:, hc, tt * 512:(tt + 1) * 512],
                                      start=(hc == 0), stop=(hc == NHC - 1)),
                            waits=w)
                    tGat = p.cnt["T"]
                    hs = HID[:, oc, tt * 512:(tt + 1) * 512]
                    gpar = (oc * 2 + tt) % 2
                    wv = [("T", tGat)]
                    if gsc_free[gpar]:
                        wv.append(("V", gsc_free[gpar]))
                    p.s(lambda en, tileG=tileG, oc=oc, gpar=gpar: en.activation(
                        GSC[:, gpar, :], tileG[:, :], AF.Sigmoid,
                        bias=CF[:, CF_BGATE + oc:CF_BGATE + oc + 1]), waits=wv)
                    sG = p.cnt["S"]
                    guard[tnG] = ("S", sG)
                    p.v(lambda en, hs=hs, gpar=gpar: en.tensor_tensor(
                        out=hs, in0=hs, in1=GSC[:, gpar, :], op=ALU.mult),
                        waits=[("S", sG)])
                    gsc_free[gpar] = p.cnt["V"]
                    u_mark[oc][tt] = p.cnt["V"]
            vU = p.cnt["V"]
            if tb == 0:
                marks["t_a_xln0"] = p.cnt["T"]
                lr_t1, lr_t1_out = make_lr(1, waitX_t1, False,
                                           marks["t_a_xln0"], sq_vec=True)
                marks["lr_t1_out"] = lr_t1_out
                lr_t1["A0"]()
            else:
                marks["t_xln_free"] = p.cnt["T"]
            for tt in range(2):
                for hc in range(NHC):
                    w = [("V", u_mark[hc][tt])]
                    if hc == 0:
                        w += twait("pC") + [("DW", dwConst)]
                    p.t(lambda en, tt=tt, hc=hc: en.matmul(
                        pC[0:16, :], BMT[:, hc, :],
                        HID[:, hc, tt * 512:(tt + 1) * 512],
                        start=(hc == 0), stop=(hc == NHC - 1)), waits=w)
                tUb = p.cnt["T"]
                p.s(lambda en, tb=tb, tt=tt: en.copy(
                    UHS[0:16, tb * TB + tt * 512:tb * TB + (tt + 1) * 512],
                    pC[0:16, :]), waits=[("T", tUb)])
                guard["pC"] = ("S", p.cnt["S"])
            ub_mark_s[tb] = p.cnt["S"]

            if tb == 0:
                lr_t1["A"]()
                p.v(lambda e: e.tensor_copy(pow_slot(0), AT_),
                    waits=[("DW", dwConst)])
                vAT = p.cnt["V"]
                ppool = [(pC, "pC"), (pD, "pD")]
                par = [0]

                def npp():
                    t_ = ppool[par[0] % 2]
                    par[0] += 1
                    return t_

                spow = [0]

                def chain(lhs_fn, base, first_waits):
                    for k in range(1, 8):
                        tile, tn = npp()
                        w = (list(first_waits) if k == 1 else
                             [("V", spow[0])]) + twait(tn)
                        p.t(lambda e, tile=tile, k=k, base=base: e.matmul(
                            tile[0:16, 0:16], lhs_fn(),
                            pow_slot(base + k - 1), start=True, stop=True),
                            waits=w)
                        tPk = p.cnt["T"]
                        p.v(lambda e, tile=tile, k=k, base=base: e.tensor_copy(
                            pow_slot(base + k), tile[0:16, 0:16]),
                            waits=[("T", tPk)])
                        spow[0] = p.cnt["V"]
                        guard[tn] = ("V", spow[0])

                chain(lambda: A_, 0, [("V", vAT), ("DW", dwConst)])
                lr_t1["Bh"](0)
                lr_t1["Bh"](1)
                p.t(lambda e: e.transpose(pT[0:16, 0:16], pow_slot(7), ID16),
                    waits=[("V", spow[0])] + twait("pT"))
                tP = p.cnt["T"]
                p.v(lambda e: e.tensor_copy(pow_slot(24), pT[0:16, 0:16]),
                    waits=[("T", tP)])
                spow[0] = p.cnt["V"]
                guard["pT"] = ("V", spow[0])
                p.vw(lambda e: e.tensor_copy(pow_slot(8), pow_slot(7)))
                vC = p.cnt["V"]
                lr_t1["Bh"](2)
                lr_t1["Bh"](3)
                chain(lambda: pow_slot(24), 8, [("V", vC)])
                lr_t1["Bh"](4)
                lr_t1["Bh"](5)
                p.t(lambda e: e.transpose(pT[0:16, 16:32], pow_slot(15), ID16),
                    waits=[("V", spow[0])] + twait("pT"))
                tP = p.cnt["T"]
                p.v(lambda e: e.tensor_copy(pow_slot(25), pT[0:16, 16:32]),
                    waits=[("T", tP)])
                spow[0] = p.cnt["V"]
                guard["pT"] = ("V", spow[0])
                p.vw(lambda e: e.tensor_copy(pow_slot(16), pow_slot(15)))
                vC = p.cnt["V"]
                chain(lambda: pow_slot(25), 16, [("V", vC)])
                state["sPowAll"] = spow[0]
                lr_t1["Bh"](6)
                lr_t1["Bh"](7)
                lr_t1["Bt"]()

                def build_ker(K, src0, tile, tn):
                    first = True
                    for j in range(8):
                        for i in range(j + 1):
                            d = j - i
                            rhs = ID16 if d == 0 else pow_slot(src0 + d - 1)
                            w = ([("V", state["sPowAll"])] + twait(tn)
                                 if first else [])
                            first = False
                            p.t(lambda e, j=j, i=i, rhs=rhs, tile=tile:
                                e.matmul(
                                    tile[:, 16 * j:16 * (j + 1)], STK0l(i),
                                    rhs, start=(i == 0), stop=(i == j)),
                                waits=w)
                    tKer = p.cnt["T"]
                    p.s(lambda e, K=K, tile=tile: e.copy(K[:, :],
                                                         tile[:, 0:128]),
                        waits=[("T", tKer)])
                    sK = ("S", p.cnt["S"])
                    guard[tn] = sK
                    return sK

                kmarks.clear()
                kmarks.append(build_ker(KER1, 0, pD, "pD"))
                kmarks.append(build_ker(KER2, 8, pC, "pC"))
                kmarks.append(build_ker(KER3, 16, pD, "pD"))
                ARNB = ARN[:, :].bitcast(BF16)
                for idx, src0 in enumerate((0, 8, 16)):
                    for k in range(8):
                        p.vw(lambda e, idx=idx, src0=src0, k=k: e.tensor_copy(
                            ARNB[0:16, idx * 128 + 16 * k:
                                 idx * 128 + 16 * (k + 1)],
                            pow_slot(src0 + k)),
                            waits=[("V", state["sPowAll"])])
                state["vPowT"] = p.cnt["V"]
                for l in range(8):
                    w = ([("S", ub_mark_s[0]), ("DW", dwConst)]
                         + twait("pE0") if l == 0 else [])
                    p.t(lambda e, l=l: e.matmul(
                        pE[:, 0:128], STK0l(l), UHS[0:16, l::8][:, 0:128],
                        start=(l == 0), stop=(l == 7)), waits=w)
                tB1a = p.cnt["T"]
                p.s(lambda e: e.copy(B1[:, 0:128], pE[:, 0:128]),
                    waits=[("T", tB1a)])
                sB1a = p.cnt["S"]
                p.t(lambda e: e.matmul(pE[:, 512:640], KER1[:, :],
                                       B1[:, 0:128], start=True, stop=False,
                                       skip_group_check=True),
                    waits=[kmarks[0], ("S", sB1a)] + twait("pE1"))
                tS1a = p.cnt["T"]
                p.s(lambda e: e.copy(S1SB[:, 0:128], pE[:, 512:640]),
                    waits=[("T", tS1a)])
                sS1a = p.cnt["S"]
                p.t(lambda e: e.matmul(pC[0:16, 0:128], UNSj(7),
                                       S1SB[:, 0:128], start=True, stop=True),
                    waits=[("S", sS1a)] + twait("pC"))
                tE1a = p.cnt["T"]
                p.s(lambda e: e.copy(UHS[0:16, 0:128],
                                     pC[0:16, 0:128]), waits=[("T", tE1a)])
                sE1a = p.cnt["S"]
                guard["pC"] = ("S", sE1a)
                for l in range(8):
                    w = ([("S", sE1a)] + twait("pD") if l == 0 else [])
                    p.t(lambda e, l=l: e.matmul(
                        pD[:, 32:48], STK0l(l),
                        UHS[0:16, 0:128][:, l::8][:, 0:16],
                        start=(l == 0), stop=(l == 7)), waits=w)
                tB2a = p.cnt["T"]
                p.s(lambda e: e.copy(B2[:, 0:16], pD[:, 32:48]),
                    waits=[("T", tB2a)])
                sB2a = p.cnt["S"]
                guard["pD"] = ("S", sB2a)
                p.t(lambda e: e.matmul(pE[:, 768:784], KER2[:, :],
                                       B2[:, 0:16], start=False, stop=False,
                                       skip_group_check=True),
                    waits=[kmarks[1], ("S", sB2a)])
                state["tS2a"] = p.cnt["T"]
                state["sB1a"] = sB1a
                vXlnT_cur = lr_t1_out["vXlnT"]

        ARNB = ARN[:, :].bitcast(BF16)

        def POWT(i):
            return ARNB[0:16, i * 128:(i + 1) * 128]

        for l in range(8):
            w = ([("S", ub_mark_s[1])] if l == 0 else [])
            p.t(lambda e, l=l: e.matmul(
                pE[:, 128:256], STK0l(l), UHS[0:16, l::8][:, 128:256],
                start=(l == 0), stop=(l == 7)), waits=w)
        tB1 = p.cnt["T"]
        p.s(lambda e: e.copy(B1[:, 128:256], pE[:, 128:256]),
            waits=[("T", tB1)])
        sB1 = p.cnt["S"]
        guard["pE0"] = ("S", sB1)
        p.t(lambda e: e.matmul(pE[:, 640:768], KER1[:, :], B1[:, 128:256],
                               start=False, stop=False, skip_group_check=True),
            waits=[("S", sB1)])
        tS1 = p.cnt["T"]
        p.s(lambda e: e.copy(S1SB[:, 128:256], pE[:, 640:768]),
            waits=[("T", tS1)])
        sS1 = p.cnt["S"]
        p.t(lambda e: e.matmul(pC[0:16, 128:256], UNSj(7), S1SB[:, 128:256],
                               start=True, stop=True),
            waits=[("S", sS1)] + twait("pC"))
        tE1 = p.cnt["T"]
        p.s(lambda e: e.copy(B1[0:16, 128:256], pC[0:16, 128:256]),
            waits=[("T", tE1)])
        sE1 = p.cnt["S"]
        guard["pC"] = ("S", sE1)
        for l in range(8):
            w = ([("S", sE1)] + twait("pD") if l == 0 else [])
            p.t(lambda e, l=l: e.matmul(
                pD[:, 0:16], STK0l(l), B1[0:16, 128:256][:, l::8][:, 0:16],
                start=(l == 0), stop=(l == 7)), waits=w)
        tB2 = p.cnt["T"]
        p.s(lambda e: e.copy(B2[:, 16:32], pD[:, 0:16]), waits=[("T", tB2)])
        sB2 = p.cnt["S"]
        guard["pD"] = ("S", sB2)
        p.t(lambda e: e.matmul(pE[:, 784:800], KER2[:, :], B2[:, 16:32],
                               start=False, stop=False, skip_group_check=True),
            waits=[("S", sB2)])
        tS2 = p.cnt["T"]
        p.s(lambda e: e.copy(S2SB[:, :], pE[:, 768:800]), waits=[("T", tS2)])
        sS2 = p.cnt["S"]
        p.t(lambda e: e.matmul(pC[0:16, 256:288], UNSj(7), S2SB[:, :],
                               start=True, stop=True),
            waits=[("S", sS2)] + twait("pC"))
        tE2 = p.cnt["T"]
        p.s(lambda e: e.copy(B1[0:16, 0:32], pC[0:16, 256:288]),
            waits=[("T", tE2)])
        sE2 = p.cnt["S"]
        guard["pC"] = ("S", sE2)
        for l in range(8):
            w = ([("S", sE2)] + twait("pD") if l == 0 else [])
            p.t(lambda e, l=l: e.matmul(
                pD[:, 64:68], STK0l(l), B1[0:16, l::8][:, 0:4],
                start=(l == 0), stop=(l == 7)), waits=w)
        tB3 = p.cnt["T"]
        p.s(lambda e: e.copy(B3[:, 0:4], pD[:, 64:68]), waits=[("T", tB3)])
        sB3 = p.cnt["S"]
        guard["pD"] = ("S", sB3)
        p.t(lambda e: e.matmul(pB[:, 0:4], KER3[:, :], B3[:, 0:4],
                               start=True, stop=False, skip_group_check=True),
            waits=[kmarks[2], ("S", sB3)] + twait("pB"))
        tS3 = p.cnt["T"]
        p.s(lambda e: e.copy(S3SB[:, 0:4], pB[:, 0:4]), waits=[("T", tS3)])
        sS3 = p.cnt["S"]
        p.t(lambda e: e.matmul(pC[0:16, 288:292], UNSj(7), S3SB[:, 0:4],
                               start=True, stop=True),
            waits=[("S", sS3)] + twait("pC"))
        tE3 = p.cnt["T"]
        p.s(lambda e: e.copy(E3[0:16, 0:4], pC[0:16, 288:292]),
            waits=[("T", tE3)])
        sE3 = p.cnt["S"]
        guard["pC"] = ("S", sE3)
        p.v(lambda e: e.tensor_copy(GC[0:16, 0:1], E3[0:16, 0:1]),
            waits=[("S", sE3)])
        vG = p.cnt["V"]
        for c in range(1, 4):
            p.t(lambda e, c=c: e.matmul(
                pC[0:16, 296 + c:297 + c], pow_slot(23), GC[0:16, c - 1:c],
                start=True, stop=True),
                waits=[("V", vG), ("V", state["sPowAll"])])
            tGc = p.cnt["T"]
            p.v(lambda e, c=c: e.tensor_tensor(
                out=GC[0:16, c:c + 1], in0=pC[0:16, 296 + c:297 + c],
                in1=E3[0:16, c:c + 1], op=ALU.add), waits=[("T", tGc)])
            vG = p.cnt["V"]
        guard["pC"] = ("V", vG)
        p.vw(lambda e: e.tensor_copy(CIN3[0:16, 1:4], GC[0:16, 0:3]),
             waits=[("V", vG), ("V", vInitCin)])
        vCin3 = p.cnt["V"]
        p.t(lambda e: e.matmul(pB[:, 0:4], POWT(2), CIN3[0:16, 0:4],
                               start=False, stop=True, skip_group_check=True),
            waits=[("V", vCin3), ("V", state["vPowT"])])
        tC3 = p.cnt["T"]
        p.s(lambda e: e.copy(S3SB[:, 0:4], pB[:, 0:4]), waits=[("T", tC3)])
        sS3t = p.cnt["S"]
        guard["pB"] = ("S", sS3t)
        for l3 in range(8):
            w = ([("S", sS3t)] + twait("pC") if l3 == 0 else [])
            p.t(lambda e, l3=l3: e.matmul(
                pC[0:16, 304 + 4 * l3:304 + 4 * (l3 + 1)], UNSj(l3),
                S3SB[:, 0:4], start=True, stop=True), waits=w)
        tCin2 = p.cnt["T"]
        for l3 in range(8):
            if l3 < 7:
                p.s(lambda e, l3=l3: e.copy(
                    CIN2[0:16, (l3 + 1)::8][:, 0:4],
                    pC[0:16, 304 + 4 * l3:304 + 4 * (l3 + 1)]),
                    waits=[("T", tCin2)] if l3 == 0 else [])
            else:
                p.s(lambda e: e.copy(CIN2[0:16, 8::8][:, 0:3],
                                     pC[0:16, 332:335]))
        sCin2 = p.cnt["S"]
        guard["pC"] = ("S", sCin2)
        p.t(lambda e: e.matmul(pE[:, 768:800], POWT(1), CIN2[0:16, :],
                               start=False, stop=True, skip_group_check=True),
            waits=[("S", sCin2)])
        tC2 = p.cnt["T"]
        p.s(lambda e: e.copy(S2SB[:, :], pE[:, 768:800]), waits=[("T", tC2)])
        sS2t = p.cnt["S"]
        for l2 in range(8):
            w = ([("S", sS2t)] + twait("pC") if l2 == 0 else [])
            p.t(lambda e, l2=l2: e.matmul(
                pC[0:16, 32 * l2:32 * (l2 + 1)], UNSj(l2), S2SB[:, :],
                start=True, stop=True), waits=w)
        tCin1 = p.cnt["T"]
        for l2 in range(8):
            if l2 < 7:
                p.s(lambda e, l2=l2: e.copy(
                    CIN1[0:16, (l2 + 1)::8][:, 0:32],
                    pC[0:16, 32 * l2:32 * (l2 + 1)]),
                    waits=[("T", tCin1)] if l2 == 0 else [])
            else:
                p.s(lambda e: e.copy(CIN1[0:16, 8::8][:, 0:31],
                                     pC[0:16, 224:255]))
        sCin1 = p.cnt["S"]
        guard["pC"] = ("S", sCin1)
        p.t(lambda e: e.matmul(pE[:, 512:768], POWT(0), CIN1[0:16, :],
                               start=False, stop=True, skip_group_check=True),
            waits=[("S", sCin1)])
        tC1 = p.cnt["T"]
        p.s(lambda e: e.copy(S1SB[:, :], pE[:, 512:768]), waits=[("T", tC1)])
        sHst = p.cnt["S"]
        guard["pE1"] = ("S", sHst)
        for j in range(8):
            tile, tn = (pC, "pC") if j % 2 == 0 else (pD, "pD")
            p.t(lambda e, j=j, tile=tile: e.matmul(
                tile[0:16, 256:512], UNSj(j), S1SB[:, :],
                start=True, stop=True), waits=[("S", sHst)] + twait(tn))
            tU = p.cnt["T"]
            if j % 2 == 0:
                p.s(lambda e, j=j, tile=tile: e.copy(
                    UHS[0:16, j::8][:, 0:256], tile[0:16, 256:512]),
                    waits=[("T", tU)])
                guard[tn] = ("S", p.cnt["S"])
                sHs_s = p.cnt["S"]
            else:
                p.v(lambda e, j=j, tile=tile: e.tensor_copy(
                    UHS[0:16, j::8][:, 0:256], tile[0:16, 256:512]),
                    waits=[("T", tU)])
                guard[tn] = ("V", p.cnt["V"])
                sHs_v = p.cnt["V"]
        sHs = ("S", sHs_s)
        sHs2 = ("V", sHs_v)

        pool2 = [(pA, "pA"), (pB, "pB"), (pD, "pD")]
        pj = [0]

        def nt2():
            t_ = pool2[pj[0] % 3]
            pj[0] += 1
            return t_

        lr_next = None
        lr_next_out = None
        for gt in range(4):
            ypar = gt % 2
            if lr_next_spec is not None and gt == 2:
                lr_next["A"]()
            for oc in range(8):
                tile, tn = nt2()
                w = ([sHs, sHs2, ("DW", dwConst)] + twait(tn) +
                     twait("Y%d" % ypar))
                p.t(lambda e, oc=oc, gt=gt, tile=tile: e.matmul(
                    tile[:, :], CMT[0:16, oc * 128:(oc + 1) * 128],
                    UHS[0:16, gt * 512:(gt + 1) * 512], start=True, stop=True),
                    waits=w)
                tY = p.cnt["T"]
                if oc % 2 == 0:
                    p.s(lambda e, oc=oc, ypar=ypar, tile=tile: e.copy(
                        HID[:, 8 + oc, ypar * 512:(ypar + 1) * 512],
                        tile[:, :]), waits=[("T", tY)])
                    guard[tn] = ("S", p.cnt["S"])
                    sY_s = p.cnt["S"]
                else:
                    p.v(lambda e, oc=oc, ypar=ypar, tile=tile: e.tensor_copy(
                        HID[:, 8 + oc, ypar * 512:(ypar + 1) * 512],
                        tile[:, :]), waits=[("T", tY)])
                    guard[tn] = ("V", p.cnt["V"])
                    sY_v = p.cnt["V"]
            sY = ("S", sY_s)
            sY2 = ("V", sY_v)
            for oc in range(8):
                tile, tn = nt2()
                for hc in range(NHC):
                    w = []
                    if hc == 0:
                        w = twait(tn) + [sY, sY2,
                                         ("DW", state["wout_ready"])]
                    p.t(lambda e, oc=oc, hc=hc, ypar=ypar, tile=tile: e.matmul(
                        tile[:, :], W2GB[:, 8 + hc, oc * 128:(oc + 1) * 128],
                        HID[:, 8 + hc, ypar * 512:(ypar + 1) * 512],
                        start=(hc == 0), stop=(hc == NHC - 1)), waits=w)
                tO = p.cnt["T"]
                p.v(lambda e, oc=oc, gt=gt, tile=tile: e.scalar_tensor_tensor(
                    out=X[:, oc, gt * 512:(gt + 1) * 512], in0=tile[:, :],
                    scalar=CF[:, CF_BOUT + oc:CF_BOUT + oc + 1],
                    in1=X[:, oc, gt * 512:(gt + 1) * 512],
                    op0=ALU.add, op1=ALU.add), waits=[("T", tO)])
                guard[tn] = ("V", p.cnt["V"])
                if lr_next is not None and gt == 2:
                    lr_next["Bh"](oc)
                    if oc == 7:
                        lr_next["Bt"]()

            guard["Y%d" % ypar] = ("T", p.cnt["T"])
            if gt == 1 and lr_next_spec is not None:
                vX01 = p.cnt["V"]
                tb2, do_rout2 = lr_next_spec
                lr_next, lr_next_out = make_lr(
                    tb2, [("V", vX01)], do_rout2, marks["t_xln_free"])
                lr_next["A0"]()
        if lr_next is not None:
            lr_next["C"]()
            marks["lr_d"] = lr_next["D"]
        marks["vX"] = p.cnt["V"]
        marks["tEnd"] = p.cnt["T"]
        return marks, lr_next_out

    lr0, lr0_out = make_lr(0, [("DS", dsX0)], True, 0)
    lr0["A0"]()
    lr0["A"]()
    for i_ in range(8):
        lr0["Bh"](i_)
    lr0["Bt"]()

    m_f1t0, lr1_out = ffn_body(0, lr0_out,
                               (1, [("DS", dsX1)], True),
                               pending_head=lr0["C"], pending_d2=lr0["D"])
    load_into_w2gb(d_ws2, 0, m_f1t0["tEnd"])
    load_w1(0, 0, m_f1t0["tEnd"])
    load_w1(1, 1, m_f1t0["tEnd"])

    m_f1t1, lra0_out = ffn_body(1, lr1_out,
                                (0, [("V", m_f1t0["vX"])], False),
                                pending_d=m_f1t0["lr_d"])
    p.g(lambda eng: eng.dma_start(
        out=W1B[:, :, :, :].rearrange("p a b c -> p (a b c)"), in_=d_win),
        waits=[("T", m_f1t1["t_xln_free"])])
    state["win_ready"] = p.cnt["DW"]
    state["wgate_ready"] = load_into_w2gb(d_wgate, 0, m_f1t1["tEnd"])
    state["wout_ready"] = load_into_w2gb(d_wout, 1, m_f1t1["tEnd"])

    m_attn, lrf2_out = attn_body(lra0_out,
                                 [("V", m_f1t1["vX"])], (0, True),
                                 pending_d=m_f1t1["lr_d"])
    load_into_w2gb(d_ws2, 0, m_attn["t_xln_free"])
    load_w1(0, 0, m_attn["t_xln_free"])
    load_w1(1, 1, m_attn["t_xln_free"])

    m_f2t0, lrf2t1_out = ffn_body(0, lrf2_out,
                                  (1, [("V", m_attn["vX"])], True),
                                  pending_d=m_attn["lr_d"])
    load_into_w2gb(d_ws2, 0, m_f2t0["tEnd"])
    load_w1(0, 0, m_f2t0["tEnd"])
    load_w1(1, 1, m_f2t0["tEnd"])

    ov = d_out.rearrange("(c p) t -> p c t", p=128)
    p.y(lambda e: e.dma_start(out=ov[:, :, 0:TB], in_=X[:, :, 0:TB]),
        waits=[("V", m_f2t0["vX"])])

    m_f2t1, _ = ffn_body(1, lrf2t1_out, None,
                         pending_d=m_f2t0["lr_d"])

    p.y(lambda e: e.dma_start(out=ov[:, :, TB:SEQ], in_=X[:, :, TB:SEQ]),
        waits=[("V", m_f2t1["vX"])])
    dsOut = p.cnt["DS"]
    p.ywait([("DS", dsOut)])

    with nc.Block() as block:
        p.emit(block, sems)
    ctx.close()
    return nc


def _host_prepare(inp):
    f32 = np.float32
    x = np.ascontiguousarray(inp["L0_attn_x"], dtype=f32)
    w1 = np.ascontiguousarray(inp["L0_ffn_W1"], dtype=f32)
    w2 = np.ascontiguousarray(inp["L0_ffn_W2"], dtype=f32)
    ws1 = np.ascontiguousarray(inp["L0_ffn_Ws1"], dtype=f32)
    ws2 = np.ascontiguousarray(inp["L0_ffn_Ws2"], dtype=f32)
    wg = np.ascontiguousarray(inp["L0_ffn_Wg"], dtype=f32)
    bias_corr = np.ascontiguousarray(inp["L0_ffn_bias_corr"], dtype=f32)
    win = np.ascontiguousarray(inp["L0_attn_W_in"], dtype=f32)
    b_in = np.ascontiguousarray(inp["L0_attn_b_in"], dtype=f32)
    a = np.ascontiguousarray(inp["L0_attn_A"], dtype=f32)
    bm = np.ascontiguousarray(inp["L0_attn_Bm"], dtype=f32)
    cm = np.ascontiguousarray(inp["L0_attn_Cm"], dtype=f32)
    wgate = np.ascontiguousarray(inp["L0_attn_W_gate"], dtype=f32)
    b_gate = np.ascontiguousarray(inp["L0_attn_b_gate"], dtype=f32)
    wout = np.ascontiguousarray(inp["L0_attn_W_out"], dtype=f32)
    b_out = np.ascontiguousarray(inp["L0_attn_b_out"], dtype=f32)

    def fm_chunks(wT):
        inf, outf = wT.shape
        return np.ascontiguousarray(
            wT.reshape(inf // 128, 128, outf).transpose(1, 0, 2).reshape(128, -1))

    w1t = np.stack([fm_chunks(w1[e].T) for e in range(E)])
    w2g = np.zeros((NG, 128, 16 * DIM), f32)
    for g in range(NG):
        for ei in range(EG):
            w2g[g][:, ei * 4 * DIM:(ei + 1) * 4 * DIM] = fm_chunks(
                w2[g * EG + ei].T)
    ws1t = fm_chunks(ws1.T)
    ws2t = fm_chunks(ws2.T)
    wint = fm_chunks(win[DIM:2 * DIM].T)
    wgatet = fm_chunks(wgate.T)
    woutt = fm_chunks(wout.T)
    wgt = fm_chunks(wg.T)
    bmt = fm_chunks(bm.T)
    cmt = np.ascontiguousarray(cm.T)

    cb = np.zeros((128, CBW), f32)
    cb[:, OFF_ONES:OFF_ONES + 128] = 1.0
    cb[:, OFF_ID:OFF_ID + 128] = np.eye(128)
    for j in range(8):
        blk = np.zeros((128, 16), f32)
        for m in range(16):
            blk[16 * j + m, m] = 1.0
        cb[:, OFF_UNS + 16 * j:OFF_UNS + 16 * (j + 1)] = blk
    for e in range(E):
        blk = np.zeros((16, 128), f32)
        blk[e, :] = 1.0
        cb[0:16, OFF_SELB + 128 * e:OFF_SELB + 128 * (e + 1)] = blk
    for l in range(8):
        blk = np.zeros((16, 128), f32)
        for n in range(16):
            blk[n, 16 * l + n] = 1.0
        cb[0:16, OFF_STK0 + 128 * l:OFF_STK0 + 128 * (l + 1)] = blk
    cb[0:16, OFF_A:OFF_A + 16] = a
    cb[0:16, OFF_AT:OFF_AT + 16] = a.T

    cf = np.zeros((128, CFW), f32)
    cf[:, CF_BCB:CF_BCB + 16] = np.broadcast_to(bias_corr, (128, 16))
    cf[:, CF_BIN:CF_BIN + 8] = b_in[DIM:2 * DIM].reshape(8, 128).T
    cf[:, CF_BGATE:CF_BGATE + 8] = b_gate.reshape(8, 128).T
    cf[:, CF_BOUT:CF_BOUT + 8] = b_out.reshape(8, 128).T
    cf[:, CF_EPS] = DIM * LNEPS
    cf[:, CF_LNC] = np.log(np.sqrt(float(DIM)))

    common = {
        "w1t": w1t, "w2g": w2g, "ws1t": ws1t, "ws2t": ws2t,
        "wint": wint, "wgatet": wgatet, "woutt": woutt,
        "wgt": wgt, "bmt": bmt, "cmt": cmt, "cb": cb, "cf": cf,
    }
    in_maps = []
    for b in range(NCORES):
        m = dict(common)
        m["xt"] = np.ascontiguousarray(x[b].T)
        in_maps.append(m)
    return in_maps


def kernel(**inputs):
    in_maps = _host_prepare(inputs)
    nc = build_program()
    res = run_bass_kernel_spmd(nc, in_maps, core_ids=list(range(NCORES)))
    out = np.empty((NCORES, SEQ, DIM), np.float32)
    for b in range(NCORES):
        out[b] = np.asarray(res.results[b]["yt"], np.float32).T
    return out
